# version 02
# baseline (speedup 1.0000x reference)
import math
import numpy as np
import concourse.bass as bass
import concourse.mybir as mybir
from concourse.bass_utils import run_bass_kernel_spmd

F32 = mybir.dt.float32
BF16 = mybir.dt.bfloat16
U8 = mybir.dt.uint8
AF = mybir.ActivationFunctionType
ALU = mybir.AluOpType
AX = mybir.AxisListType

D = 2048
DT = 16
B = 4
L = 2048
HALF = 1024
NST = 16
RNK = 64
HID = 5632
HT = 44
HEADS = 32
HDIM = 64
STATE = 256
CONVD = 2560
PROJ = 4640
NZT = 16
NCT = 20
KCONV = 4
CHUNK = 64
EXPERTS = 16
FDIM = 768
FT = 6
EPS = 1e-5
UNW = 32256

ENGINES = ("sync", "tensor", "vector", "scalar", "gpsimd")
N_DSEM = {"sync": 12, "gpsimd": 8}
DMA_SEMS = {e: tuple(f"d_{e}{i}" for i in range(n)) for e, n in N_DSEM.items()}
SEMS = ("s_sync", "s_pe", "s_dve", "s_act", "s_gps") \
    + DMA_SEMS["sync"] + DMA_SEMS["gpsimd"]
OWN_SEM = {"sync": "s_sync", "tensor": "s_pe", "vector": "s_dve",
           "scalar": "s_act", "gpsimd": "s_gps"}


class Op:
    __slots__ = ("fn", "deps", "inc")
    def __init__(self, fn, deps, inc):
        self.fn = fn
        self.deps = deps
        self.inc = inc


class Prog:
    def __init__(self):
        self.ops = {e: [] for e in ENGINES}
        self.cnt = {s: 0 for s in SEMS}
        self.drot = {e: 0 for e in N_DSEM}

    def emit(self, eng, fn, deps=(), inc=False, dma=False):
        deps = tuple(d for d in deps if d is not None)
        if dma:
            sems = DMA_SEMS[eng]
            sem = sems[self.drot[eng] % len(sems)]
            self.drot[eng] += 1
            if self.cnt[sem] > 0:
                deps = deps + ((sem, self.cnt[sem]),)
            self.cnt[sem] += 16
            tok = (sem, self.cnt[sem])
            self.ops[eng].append(Op(fn, deps, (sem, 16)))
            return tok
        if inc:
            sem = OWN_SEM[eng]
            self.cnt[sem] += 1
            tok = (sem, self.cnt[sem])
            self.ops[eng].append(Op(fn, deps, (sem, 1)))
            return tok
        self.ops[eng].append(Op(fn, deps, None))
        return None

    def barrier(self, engs=ENGINES):
        toks = {}
        for e in engs:
            toks[e] = self.emit(e, lambda x: x.nop(), inc=True)
        for e in engs:
            self.emit(e, lambda x: x.nop(),
                      deps=[toks[o] for o in engs if o != e])


def _replay_one(prog, ename, eng, sems):
    own = OWN_SEM[ename]
    used = set()
    for e2 in ENGINES:
        for op in prog.ops[e2]:
            for d in op.deps:
                used.add(d)
    waited = {}
    cur = 0
    drained = 0
    for op in prog.ops[ename]:
        for (sn, v) in op.deps:
            if sn == own:
                if ename != "tensor" and v > drained:
                    eng.drain()
                    drained = cur
                continue
            if waited.get(sn, 0) >= v:
                continue
            eng.wait_ge(sems[sn], v)
            waited[sn] = v
        instr = op.fn(eng)
        if op.inc is not None:
            sn, k = op.inc
            instr.then_inc(sems[sn], k)
            if sn == own:
                cur += 1
                if (own, cur) in used:
                    eng.wait_ge(sems[own], cur)



def build_program(upto="all"):
    nc = bass.Bass("TRN2", target_bir_lowering=False, debug=False)

    hbm = {}
    def H(name, shape, dt=F32, kind="ExternalInput"):
        hbm[name] = nc.dram_tensor(name, shape, dt, kind=kind).ap()
        return hbm[name]

    H("xpre", [D, HALF])
    H("xown", [D, HALF])
    H("wx_t", [D, 96])
    H("wdt_t", [RNK, D])
    H("bdt", [128, DT])
    H("a0", [128, DT * NST])
    H("d0", [128, DT])
    H("wqg_t", [D, HID], U8)
    H("wqu_t", [D, HID], U8)
    H("wqd_t", [HID, D], U8)
    H("sg", [128, HT])
    H("su", [128, HT])
    H("sd", [128, DT])
    H("nw", [128, DT])
    H("win", [D, PROJ])
    H("convw", [128, NCT * KCONV])
    H("convb", [128, NCT])
    H("dtb", [128, 1])
    H("a1", [128, 1])
    H("dskipbc", [128, HEADS])
    H("gnw", [128, DT])
    H("ones32", [32, 128])
    H("i32", [32, 32])
    H("wout", [D, D])
    H("wo", [D, D])
    H("router", [D, EXPERTS])
    H("gw", [EXPERTS, D, FDIM])
    H("uw", [EXPERTS, D, FDIM])
    H("dw", [EXPERTS, FDIM, D])
    H("ident", [128, 128])
    H("onesr", [1, 128])
    H("negr", [1, 128])
    H("onesc", [128, 1])
    H("maskt", [64, 64])
    H("sspill", [128, 4096], kind="ExternalOutput")
    H("out", [HALF, D], kind="ExternalOutput")
    H("dbg", [D, HALF], kind="ExternalOutput")

    import contextlib
    stack = contextlib.ExitStack()

    with stack:
        x_fm = stack.enter_context(nc.sbuf_tensor("x_fm", [128, DT * HALF], F32))
        xfb = x_fm[:, :].bitcast(BF16)
        un = stack.enter_context(nc.sbuf_tensor("un", [128, UNW], F32))
        unb = un[:, :].bitcast(BF16)
        unu = un[:, :].bitcast(U8)
        xproj = stack.enter_context(nc.sbuf_tensor("xproj", [128, 512], F32))
        hcar = stack.enter_context(nc.sbuf_tensor("hcar", [128, DT * NST], F32))
        a0p = stack.enter_context(nc.sbuf_tensor("a0p", [128, DT * NST], F32))
        bcb = stack.enter_context(nc.sbuf_tensor("bcb", [128, 1792], F32))
        lnscr = stack.enter_context(nc.sbuf_tensor("lnscr", [128, 512], F32))
        identt = stack.enter_context(nc.sbuf_tensor("identt", [128, 128], F32))
        masktt = stack.enter_context(nc.sbuf_tensor("masktt", [128, 64], F32))
        halo = stack.enter_context(nc.sbuf_tensor("halo", [128, NCT * (KCONV - 1)], F32))
        params = stack.enter_context(nc.sbuf_tensor("params", [128, 192], F32))
        convwp = stack.enter_context(nc.sbuf_tensor("convwp", [128, NCT * KCONV], F32))
        convbp = stack.enter_context(nc.sbuf_tensor("convbp", [128, NCT], F32))
        dskipp = stack.enter_context(nc.sbuf_tensor("dskipp", [128, HEADS], F32))
        ones32t = stack.enter_context(nc.sbuf_tensor("ones32t", [32, 128], F32))
        i32t = stack.enter_context(nc.sbuf_tensor("i32t", [32, 32], F32))
        routerp = stack.enter_context(nc.sbuf_tensor("routerp", [128, DT * EXPERTS], F32))

        ps = [stack.enter_context(nc.psum_tensor(f"ps{i}", [128, 512], F32))
              for i in range(8)]

        sems = {}
        for s in SEMS:
            sems[s] = stack.enter_context(nc.semaphore(s))

        p = Prog()
        psum_guard = [None] * 8

        bdt_c = lambda dt: params[:, dt:dt + 1]
        dp1_c = lambda dt: params[:, 16 + dt:17 + dt]
        sg_c = lambda f: params[:, 32 + f:33 + f]
        su_c = lambda f: params[:, 76 + f:77 + f]
        sd_c = lambda dt: params[:, 120 + dt:121 + dt]
        nw_c = lambda dt: params[:, 136 + dt:137 + dt]
        dtb_c = params[:, 152:153]
        a1n_c = params[:, 153:154]
        ones_c = params[:, 154:155]
        eps_c = params[:, 155:156]
        onesr_ap = bcb[0:1, 0:128]
        negr_ap = bcb[0:1, 1664:1792]
        st_row = lambda i, w: bcb[0:1, 128 + i * 512:128 + i * 512 + w]

        def ld(dst, src, eng="sync", deps=()):
            return p.emit(eng, lambda e, d=dst, s=src: e.dma_start(out=d, in_=s),
                          deps=deps, dma=True)

        t_bdt = ld(params[:, 0:16], hbm["bdt"][:, :])
        t_d0 = ld(params[:, 16:32], hbm["d0"][:, :])
        ld(params[:, 32:76], hbm["sg"][:, :])
        ld(params[:, 76:120], hbm["su"][:, :])
        ld(params[:, 120:136], hbm["sd"][:, :])
        ld(params[:, 136:152], hbm["nw"][:, :])
        t_dtb = ld(params[:, 152:153], hbm["dtb"][:, :])
        t_a1 = ld(params[:, 153:154], hbm["a1"][:, :])
        t_oc = ld(params[:, 154:155], hbm["onesc"][:, :])
        t_a0 = ld(a0p[:, :], hbm["a0"][:, :])
        ld(convwp[:, :], hbm["convw"][:, :])
        ld(convbp[:, :], hbm["convb"][:, :])
        ld(dskipp[:, :], hbm["dskipbc"][:, :])
        ld(ones32t[:, :], hbm["ones32"][:, :])
        ld(i32t[:, :], hbm["i32"][:, :])
        ld(params[:, 160:176], hbm["gnw"][:, :])
        ld(identt[:, :], hbm["ident"][:, :])
        ld(masktt[0:64, :], hbm["maskt"][:, :])
        t_or = ld(bcb[0:1, 0:128], hbm["onesr"][:, :])
        t_nr = ld(bcb[0:1, 1664:1792], hbm["negr"][:, :])
        ld(routerp[:, :].rearrange("p (t e) -> p t e", t=DT),
           hbm["router"][:, :].rearrange("(t p) e -> p t e", p=128))

        t_eps = p.emit("vector", lambda e: e.memset(eps_c, EPS), inc=True)
        t_dp1 = p.emit("vector", lambda e: e.tensor_scalar_add(
            params[:, 16:32], params[:, 16:32], 1.0), deps=[t_d0], inc=True)
        t_bdt = p.emit("vector", lambda e: e.tensor_scalar_mul(
            params[:, 0:16], params[:, 0:16], -1.0), deps=[t_bdt], inc=True)
        t_dtb = p.emit("vector", lambda e: e.tensor_scalar_mul(
            dtb_c, dtb_c, -1.0), deps=[t_dtb], inc=True)
        t_a1n = p.emit("scalar", lambda e: e.activation(
            out=a1n_c, in_=a1n_c, func=AF.Exp), deps=[t_a1], inc=True)
        t_a0e = p.emit("scalar", lambda e: e.activation(
            out=a0p[:, :], in_=a0p[:, :], func=AF.Exp), deps=[t_a0], inc=True)
        t_hz = p.emit("vector", lambda e: e.memset(hcar[:, :], 0.0), inc=True)
        t_hlz = p.emit("vector", lambda e: e.memset(halo[:, :], 0.0), inc=True)

        p.barrier()

        def ln_fm(w, src_ap, src_toks, dsthi, dstlo, dst32, rms=False,
                  wcol=None):
            scr = lnscr[:, 0:w]
            st0 = st_row(0, w)
            st1 = st_row(1, w)
            st2 = st_row(2, w)
            t_sum = None
            t_sqmm = None
            for dt in range(DT):
                t_sq = p.emit("vector", lambda e, dt=dt: e.tensor_mul(
                    scr, src_ap(dt), src_ap(dt)),
                    deps=[src_toks[dt]] + ([t_sqmm] if dt else []), inc=True)
                if not rms:
                    t_sum = p.emit("tensor", lambda e, dt=dt: e.matmul(
                        ps[6][0:1, 0:w], ones_c, src_ap(dt),
                        start=(dt == 0), stop=(dt == DT - 1)),
                        deps=[src_toks[dt], t_oc] + ([psum_guard[6]] if dt == 0 else []),
                        inc=(dt == DT - 1))
                t_sqmm = p.emit("tensor", lambda e, dt=dt: e.matmul(
                    ps[7][0:1, 0:w], ones_c, scr,
                    start=(dt == 0), stop=(dt == DT - 1)),
                    deps=[t_sq] + ([psum_guard[7]] if dt == 0 else []),
                    inc=True)
            if not rms:
                t_mean = p.emit("vector", lambda e: e.tensor_scalar_mul(
                    st0, ps[6][0:1, 0:w], 1.0 / D), deps=[t_sum], inc=True)
                psum_guard[6] = t_mean
                t_msq = p.emit("vector", lambda e: e.tensor_mul(
                    st2, st0, st0), deps=[t_mean], inc=True)
                t_var = p.emit("vector", lambda e: e.scalar_tensor_tensor(
                    st1, ps[7][0:1, 0:w], 1.0 / D, st2,
                    ALU.mult, ALU.subtract), deps=[t_msq, t_sqmm], inc=True)
            else:
                t_var = p.emit("vector", lambda e: e.tensor_scalar_mul(
                    st1, ps[7][0:1, 0:w], 1.0 / D), deps=[t_sqmm], inc=True)
            psum_guard[7] = t_var
            t_rstd = p.emit("scalar", lambda e: e.activation(
                out=st1, in_=st1, func=AF.Sqrt, bias=eps_c[0:1, :]),
                deps=[t_var, t_eps], inc=True)
            t_rstd = p.emit("vector", lambda e: e.reciprocal(st1, st1),
                            deps=[t_rstd], inc=True)
            if not rms:
                t_bm = p.emit("tensor", lambda e: e.matmul(
                    ps[5][:, 0:w], onesr_ap, st0, start=True, stop=True),
                    deps=[t_mean, t_or, psum_guard[5]], inc=True)
            t_br = p.emit("tensor", lambda e: e.matmul(
                ps[4][:, 0:w], onesr_ap, st1, start=True, stop=True),
                deps=[t_rstd, t_or, psum_guard[4]], inc=True)
            toks = [None] * DT
            prev = None
            for dt in range(DT):
                if not rms:
                    tt = p.emit("vector", lambda e, dt=dt: e.tensor_sub(
                        scr, src_ap(dt), ps[5][:, 0:w]),
                        deps=[t_bm, src_toks[dt], prev], inc=True)
                    tu = p.emit("vector", lambda e: e.tensor_mul(
                        scr, scr, ps[4][:, 0:w]), deps=[tt, t_br], inc=True)
                else:
                    tu = p.emit("vector", lambda e, dt=dt: e.scalar_tensor_tensor(
                        scr, src_ap(dt), wcol(dt), ps[4][:, 0:w],
                        ALU.mult, ALU.mult),
                        deps=[src_toks[dt], t_br, prev], inc=True)
                if dst32 is not None:
                    tu = p.emit("vector", lambda e, dt=dt: e.tensor_copy(
                        dst32(dt), scr), deps=[tu], inc=True)
                if dsthi is not None:
                    th = p.emit("vector", lambda e, dt=dt: e.tensor_copy(
                        dsthi(dt), scr), deps=[tu], inc=True)
                    tu = p.emit("vector", lambda e, dt=dt: e.tensor_tensor(
                        dstlo(dt), scr, dsthi(dt), ALU.subtract),
                        deps=[th], inc=True)
                toks[dt] = tu
                prev = tu
            psum_guard[4] = toks[DT - 1]
            if not rms:
                psum_guard[5] = toks[DT - 1]
            return toks

        def wdt_ap(dt):
            return un[0:RNK, 28672 + dt * 128: 28672 + (dt + 1) * 128]

        def wx_ap(k):
            return un[:, 30720 + k * 96: 30720 + (k + 1) * 96]

        def load_l0a_weights(deps=()):
            t1 = p.emit("sync", lambda e: e.dma_start(
                out=un[:, 30720:32256].rearrange("p (t c) -> p t c", t=DT),
                in_=hbm["wx_t"][:, :].rearrange("(t p) c -> p t c", p=128)),
                deps=deps, dma=True)
            t2 = p.emit("sync", lambda e: e.dma_start(
                out=un[0:RNK, 28672:30720], in_=hbm["wdt_t"][:, :]),
                deps=deps, dma=True)
            return [t1, t2]

        hcar_dep = {}

        def l0a_half(c0, x_toks, w_toks):
            def x_sb(dt):
                return x_fm[:, dt * HALF + c0: dt * HALF + c0 + 512]
            t_xp = None
            for k in range(DT):
                t_xp = p.emit("tensor", lambda e, k=k: e.matmul(
                    ps[7][0:96, :], wx_ap(k), x_sb(k),
                    start=(k == 0), stop=(k == DT - 1)),
                    deps=list(x_toks) + list(w_toks) + ([psum_guard[7]] if k == 0 else []),
                    inc=(k == DT - 1))
            t_xpe = p.emit("vector", lambda e: e.tensor_copy(
                xproj[0:96, :], ps[7][0:96, :]), deps=[t_xp], inc=True)
            psum_guard[7] = t_xpe

            t2_toks = [None] * DT
            dtx_toks = [None] * DT
            for dt in range(DT):
                bi = 6 if dt % 2 == 0 else 5
                mm = p.emit("tensor", lambda e, dt=dt, bi=bi: e.matmul(
                    ps[bi][:, :], wdt_ap(dt), xproj[0:RNK, :],
                    start=True, stop=True),
                    deps=[t_xpe, psum_guard[bi]] + list(w_toks), inc=True)
                sp = p.emit("scalar", lambda e, dt=dt, bi=bi: e.activation(
                    out=un[:, dt * 512: dt * 512 + 512], in_=ps[bi][:, :],
                    func=AF.Sigmoid, scale=-1.0, bias=bdt_c(dt)),
                    deps=[mm, t_bdt], inc=True)
                psum_guard[bi] = sp
                sp = p.emit("scalar", lambda e, dt=dt: e.activation(
                    out=un[:, dt * 512: dt * 512 + 512],
                    in_=un[:, dt * 512: dt * 512 + 512], func=AF.Ln),
                    deps=[sp], inc=True)
                t2_toks[dt] = sp
                dtx_toks[dt] = p.emit("vector", lambda e, dt=dt: e.tensor_mul(
                    un[:, 8192 + dt * 512: 8192 + dt * 512 + 512],
                    un[:, dt * 512: dt * 512 + 512], x_sb(dt)),
                    deps=[sp] + list(x_toks), inc=True)

            y_toks = [None] * DT
            for dt in range(DT):
                y_toks[dt] = p.emit("vector", lambda e, dt=dt: e.tensor_scalar(
                    x_sb(dt), x_sb(dt), dp1_c(dt), None, ALU.mult),
                    deps=[t_dp1, dtx_toks[dt]], inc=True)

            scan_tok = [None, None]
            mul_tok = [None, None]
            add_tok = [None, None]
            bcb_guard = [None, None]
            for n in range(NST):
                npar = n % 2
                bbank, cbank = npar, 2 + npar
                boff = 20480 + npar * 1024
                tb = p.emit("sync", lambda e, n=n, bo=boff: e.dma_start(
                    out=un[0:1, bo:bo + 512], in_=xproj[64 + n:65 + n, :]),
                    deps=[t_xpe, bcb_guard[npar]], dma=True)
                tc = p.emit("sync", lambda e, n=n, bo=boff: e.dma_start(
                    out=un[0:1, bo + 512:bo + 1024], in_=xproj[80 + n:81 + n, :]),
                    deps=[t_xpe, bcb_guard[npar]], dma=True)
                bb = p.emit("tensor", lambda e, bo=boff, bbank=bbank: e.matmul(
                    ps[bbank][:, :], negr_ap, un[0:1, bo:bo + 512],
                    start=True, stop=True),
                    deps=[tb, t_nr, psum_guard[bbank]], inc=True)
                cb = p.emit("tensor", lambda e, bo=boff, cbank=cbank: e.matmul(
                    ps[cbank][:, :], onesr_ap, un[0:1, bo + 512:bo + 1024],
                    start=True, stop=True),
                    deps=[tc, t_or, psum_guard[cbank]], inc=True)
                bcb_guard[npar] = cb
                for dt in range(DT):
                    col = dt * NST + n
                    it = n * DT + dt
                    par = it % 2
                    o_da = 16384 + par * 2048
                    o_db = o_da + 512
                    o_h = o_da + 1024
                    o_p = o_da + 1536
                    da = p.emit("scalar", lambda e, dt=dt, col=col, o_da=o_da: e.activation(
                        out=un[:, o_da:o_da + 512],
                        in_=un[:, dt * 512: dt * 512 + 512],
                        func=AF.Exp, scale=a0p[:, col:col + 1]),
                        deps=[t2_toks[dt], t_a0e, scan_tok[par]], inc=True)
                    db = p.emit("vector", lambda e, dt=dt, o_db=o_db, bbank=bbank: e.scalar_tensor_tensor(
                        un[:, o_db:o_db + 512],
                        un[:, 8192 + dt * 512: 8192 + dt * 512 + 512],
                        1.0, ps[bbank][:, :], ALU.mult, ALU.mult),
                        deps=[dtx_toks[dt], bb, scan_tok[par]], inc=True)
                    psum_guard[bbank] = db
                    h = p.emit("vector", lambda e, col=col, o_da=o_da, o_db=o_db, o_h=o_h:
                        e.tensor_tensor_scan(
                            un[:, o_h:o_h + 512], un[:, o_da:o_da + 512],
                            un[:, o_db:o_db + 512], hcar[:, col:col + 1],
                            ALU.mult, ALU.add),
                        deps=[da, db, hcar_dep[col], mul_tok[par]], inc=True)
                    scan_tok[par] = h
                    hc = p.emit("vector", lambda e, col=col, o_h=o_h: e.tensor_copy(
                        hcar[:, col:col + 1], un[:, o_h + 511:o_h + 512]),
                        deps=[h], inc=True)
                    hcar_dep[col] = hc
                    m1 = p.emit("vector", lambda e, o_h=o_h, o_p=o_p, cbank=cbank: e.tensor_mul(
                        un[:, o_p:o_p + 512], un[:, o_h:o_h + 512], ps[cbank][:, :]),
                        deps=[h, cb, add_tok[par]], inc=True)
                    psum_guard[cbank] = m1
                    mul_tok[par] = m1
                    at_ = p.emit("gpsimd", lambda e, dt=dt, o_p=o_p: e.tensor_add(
                        x_sb(dt), x_sb(dt), un[:, o_p:o_p + 512]),
                        deps=[m1, y_toks[dt]], inc=True)
                    y_toks[dt] = at_
                    add_tok[par] = at_
            return y_toks

        def x1hi(dt, c0, w=512):
            return xfb[:, 2 * (dt * HALF + c0): 2 * (dt * HALF + c0) + w]

        def x1lo(dt, c0, w=512):
            return xfb[:, 2 * (dt * HALF + c0) + 512: 2 * (dt * HALF + c0) + 512 + w]

        def l0ffn_chunk(c0, x_toks):
            whb = [unb[:, 45056 + i * 2048: 45056 + (i + 1) * 2048] for i in range(4)]
            wu8 = [unu[:, 106496 + i * 2048: 106496 + (i + 1) * 2048] for i in range(2)]
            slot_tok = [None, None, None, None]
            u8_tok = [None, None]
            gwork = un[:, 27648:28160]
            hwork = un[:, 28160:28672]
            hhi = lambda f: unb[:, f * 512:(f + 1) * 512]
            hlo = lambda f: unb[:, 22528 + f * 512:22528 + (f + 1) * 512]
            h_toks = [None] * HT
            conv_engs = ("scalar", "vector", "gpsimd")

            def conv_op(ceng, dst, srcu8):
                if ceng == "scalar":
                    return lambda e: e.activation(out=dst, in_=srcu8,
                                                  func=AF.Copy, bias=-128.0)
                return lambda e: e.tensor_scalar_add(dst, srcu8, -128.0)

            tg = None
            for f in range(HT):
                for gi, wsrc in enumerate((hbm["wqg_t"], hbm["wqu_t"])):
                    slot = (2 * f + gi) % 4
                    u8s = gi
                    tl = p.emit("sync", lambda e, f=f, wsrc=wsrc, u8s=u8s: e.dma_start(
                        out=wu8[u8s].rearrange("p (k c) -> p k c", k=DT),
                        in_=wsrc[:, f * 128:(f + 1) * 128].rearrange(
                            "(k p) c -> p k c", p=128)),
                        deps=[u8_tok[u8s]], dma=True)
                    ceng = conv_engs[(2 * f + gi) % 3]
                    tcv = p.emit(ceng, conv_op(ceng, whb[slot], wu8[u8s]),
                                 deps=[tl, slot_tok[slot]], inc=True)
                    u8_tok[u8s] = tcv
                    bank = 6 + gi
                    mm = None
                    for k in range(DT):
                        for hl, xap in ((0, x1hi), (1, x1lo)):
                            mm = p.emit("tensor", lambda e, k=k, hl=hl, xap=xap,
                                        slot=slot, bank=bank: e.matmul(
                                ps[bank][:, :], whb[slot][:, k * 128:(k + 1) * 128],
                                xap(k, c0), start=(k == 0 and hl == 0),
                                stop=(k == DT - 1 and hl == 1)),
                                deps=[tcv, psum_guard[bank]]
                                + (list(x_toks) if (k == 0 and hl == 0) else []),
                                inc=(k == DT - 1 and hl == 1))
                    slot_tok[slot] = mm
                    prev_h = h_toks[f - 1] if f else None
                    if gi == 0:
                        tg = p.emit("scalar", lambda e, f=f: e.activation(
                            out=gwork, in_=ps[6][:, :], func=AF.Silu, scale=sg_c(f)),
                            deps=[mm, prev_h], inc=True)
                        psum_guard[6] = tg
                    else:
                        tu = p.emit("vector", lambda e, f=f: e.tensor_scalar(
                            hwork, ps[7][:, :], su_c(f), None, ALU.mult),
                            deps=[mm, prev_h], inc=True)
                        psum_guard[7] = tu
                        tm = p.emit("vector", lambda e: e.tensor_mul(
                            gwork, gwork, hwork), deps=[tu, tg], inc=True)
                        th = p.emit("vector", lambda e, f=f: e.tensor_copy(
                            hhi(f), gwork), deps=[tm], inc=True)
                        h_toks[f] = p.emit("vector", lambda e, f=f: e.tensor_tensor(
                            hlo(f), gwork, hhi(f), ALU.subtract), deps=[th], inc=True)

            dwb = [unb[:, 45056 + i * 5632: 45056 + (i + 1) * 5632] for i in range(2)]
            du8 = [unu[:, 116736 + i * 5632: 116736 + (i + 1) * 5632] for i in range(2)]
            dwork = [un[:, 28160:28672], un[:, 28672:29184]]
            dslot_tok = [slot_tok[0] or slot_tok[2], slot_tok[1] or slot_tok[3]]
            dslot_tok = [slot_tok[2], slot_tok[3]]
            du8_tok = [u8_tok[0], u8_tok[1]]
            dn_toks = [None] * DT
            for dt in range(DT):
                slot = dt % 2
                tl = p.emit("sync", lambda e, dt=dt, slot=slot: e.dma_start(
                    out=du8[slot].rearrange("p (k c) -> p k c", k=HT),
                    in_=hbm["wqd_t"][:, dt * 128:(dt + 1) * 128].rearrange(
                        "(k p) c -> p k c", p=128)),
                    deps=[du8_tok[slot]], dma=True)
                ceng = conv_engs[dt % 3]
                gd = list(slot_tok) + [h_toks[HT - 1]] \
                    if dt < 2 else [dslot_tok[slot]]
                tcv = p.emit(ceng, conv_op(ceng, dwb[slot], du8[slot]),
                             deps=[tl] + gd, inc=True)
                du8_tok[slot] = tcv
                bank = 6 + slot
                mm = None
                for k in range(HT):
                    for hl, hap in ((0, hhi), (1, hlo)):
                        mm = p.emit("tensor", lambda e, k=k, hl=hl, hap=hap,
                                    slot=slot, bank=bank: e.matmul(
                            ps[bank][:, :], dwb[slot][:, k * 128:(k + 1) * 128],
                            hap(k), start=(k == 0 and hl == 0),
                            stop=(k == HT - 1 and hl == 1)),
                            deps=[tcv, psum_guard[bank], h_toks[k]],
                            inc=(k == HT - 1 and hl == 1))
                dslot_tok[slot] = mm
                tv = p.emit("vector", lambda e, dt=dt, bank=bank, slot=slot: e.scalar_tensor_tensor(
                    dwork[slot], ps[bank][:, :], sd_c(dt), x1hi(dt, c0),
                    ALU.mult, ALU.add),
                    deps=[mm] + ([dn_toks[dt - 2]] if dt >= 2 else [h_toks[HT - 1]]),
                    inc=True)
                psum_guard[bank] = tv
                dn_toks[dt] = p.emit("vector", lambda e, dt=dt, slot=slot: e.tensor_add(
                    x_fm[:, dt * HALF + c0: dt * HALF + c0 + 512],
                    dwork[slot], x1lo(dt, c0)), deps=[tv], inc=True)
            return dn_toks

        A_CONV, A_Z, A_DT, A_ACS = 0, 5120, 9216, 9472
        A_XBCT, A_XH, A_G, A_CBT = 9728, 9992, 12040, 14088
        A_XT, A_Y, A_ZT, A_MISC = 14152, 16200, 18248, 20296
        A_Y3, A_S, A_WIN = 20992, 25088, 29184
        M_AT = A_MISC
        M_EXPA = A_MISC + 32
        M_DECT = A_MISC + 64
        M_DECBC = A_MISC + 96
        M_ALB = A_MISC + 128
        M_DTT = A_MISC + 160
        M_BTM = A_MISC + 192
        M_ONES = A_MISC + 448
        M_RSTD = A_MISC + 512
        gnw_c = lambda dt: params[:, 160 + dt:161 + dt]

        def ap(off, w, pn=128):
            return un[0:pn, off:off + w]

        def l1a_quarter(phase, q, xq_toks, s_toks, nchunks=4):
            own = (phase == 1)
            qc0 = q * 256
            xsl = lambda dt: x_fm[:, dt * HALF + qc0: dt * HALF + qc0 + 256]
            xn_off = lambda k: (A_XT + k * 256) if k < 8 else (A_Y + (k - 8) * 256)
            xn_ap = lambda k: un[:, xn_off(k):xn_off(k) + 256]
            xn_toks = ln_fm(256, xsl, xq_toks, None, None, xn_ap,
                            rms=True, wcol=nw_c)
            t_ones = p.emit("vector", lambda e: e.memset(
                ap(M_ONES, 64, 32), 1.0), inc=True)

            conv_toks = [None] * NCT
            z_toks = [None] * NZT
            wslot = un[:, A_WIN:A_WIN + 2048]
            prev_mm = [None]
            t_stage = None
            dt_tok = [None, None]
            mrange = list(range(NZT, 37)) + (list(range(NZT)) if own else [])
            for mi, m in enumerate(mrange):
                bank = 6 + (mi % 2)
                mw = 32 if m == 36 else 128
                t_w = p.emit("sync", lambda e, m=m, mw=mw: e.dma_start(
                    out=wslot.rearrange("p (k c) -> p k c", k=DT)[:, :, 0:mw],
                    in_=hbm["win"][:, m * 128:m * 128 + mw].rearrange(
                        "(k p) c -> p k c", p=128)),
                    deps=[prev_mm[0]], dma=True)
                mm = None
                for k in range(DT):
                    mm = p.emit("tensor", lambda e, k=k, mw=mw, bank=bank: e.matmul(
                        ps[bank][0:mw, 0:256], wslot[:, k * 128:k * 128 + mw],
                        xn_ap(k), start=(k == 0), stop=(k == DT - 1)),
                        deps=[t_w, psum_guard[bank], xn_toks[k]],
                        inc=(k == DT - 1))
                prev_mm[0] = mm
                if m < NZT:
                    z_toks[m] = p.emit("vector", lambda e, m=m, bank=bank: e.tensor_copy(
                        ap(A_Z + m * 256, 256), ps[bank][:, 0:256]),
                        deps=[mm], inc=True)
                    psum_guard[bank] = z_toks[m]
                elif m < 36:
                    ct = m - NZT
                    tse = p.emit("vector", lambda e, bank=bank: e.tensor_copy(
                        un[:, A_XBCT + 3:A_XBCT + 259], ps[bank][:, 0:256]),
                        deps=[mm, t_stage], inc=True)
                    psum_guard[bank] = tse
                    th = p.emit("vector", lambda e, ct=ct: e.tensor_copy(
                        un[:, A_XBCT:A_XBCT + 3], halo[:, ct * 3:ct * 3 + 3]),
                        deps=[tse], inc=True)
                    cv = p.emit("vector", lambda e, ct=ct: e.tensor_scalar(
                        ap(A_CONV + ct * 256, 256), un[:, A_XBCT:A_XBCT + 256],
                        convwp[:, ct * KCONV:ct * KCONV + 1], convbp[:, ct:ct + 1],
                        ALU.mult, ALU.add), deps=[th], inc=True)
                    for kk in range(1, KCONV):
                        cv = p.emit("vector", lambda e, ct=ct, kk=kk: e.scalar_tensor_tensor(
                            ap(A_CONV + ct * 256, 256),
                            un[:, A_XBCT + kk:A_XBCT + kk + 256],
                            convwp[:, ct * KCONV + kk:ct * KCONV + kk + 1],
                            ap(A_CONV + ct * 256, 256), ALU.mult, ALU.add),
                            deps=[cv], inc=True)
                    hn = p.emit("gpsimd", lambda e, ct=ct: e.tensor_copy(
                        halo[:, ct * 3:ct * 3 + 3],
                        un[:, A_XBCT + 256:A_XBCT + 259]), deps=[cv], inc=True)
                    t_stage = hn
                    conv_toks[ct] = p.emit("scalar", lambda e, ct=ct: e.activation(
                        out=ap(A_CONV + ct * 256, 256),
                        in_=ap(A_CONV + ct * 256, 256),
                        func=AF.Silu), deps=[cv], inc=True)
                else:
                    t1 = p.emit("scalar", lambda e, bank=bank: e.activation(
                        out=ap(A_DT, 256, 32), in_=ps[bank][0:32, 0:256],
                        func=AF.Sigmoid, scale=-1.0, bias=dtb_c[0:32, :]),
                        deps=[mm, t_dtb], inc=True)
                    psum_guard[bank] = t1
                    t1 = p.emit("scalar", lambda e: e.activation(
                        out=ap(A_DT, 256, 32), in_=ap(A_DT, 256, 32),
                        func=AF.Ln), deps=[t1], inc=True)
                    t2a = p.emit("vector", lambda e: e.tensor_scalar(
                        ap(A_ACS, 256, 32), ap(A_DT, 256, 32),
                        a1n_c[0:32, :], None, ALU.mult),
                        deps=[t1, t_a1n], inc=True)
                    t2b = p.emit("vector", lambda e: e.tensor_scalar_mul(
                        ap(A_DT, 256, 32), ap(A_DT, 256, 32), -1.0),
                        deps=[t2a], inc=True)
                    cs = None
                    for c in range(4):
                        cs = p.emit("vector", lambda e, c=c: e.tensor_tensor_scan(
                            ap(A_ACS + c * 64, 64, 32), ap(M_ONES, 64, 32),
                            ap(A_ACS + c * 64, 64, 32), 0.0, ALU.mult, ALU.add),
                            deps=[t2a, t_ones, cs], inc=True)
                    dt_tok = [t2b, cs]

            s_dep = list(s_toks)
            cg = {"xt": [prev_mm[0]], "y": [prev_mm[0]], "g": [],
                  "xh": [], "zt": [], "decbc": [], "y3last": None}
            y3_toks = [None] * DT
            for c in range(nchunks):
                if c:
                    p.barrier()
                acs_c = ap(A_ACS + c * 64, 64, 32)
                ta = p.emit("tensor", lambda e, acs_c=acs_c: e.transpose(
                    ps[6][0:64, 0:32], acs_c, i32t[:, :]),
                    deps=[dt_tok[1], psum_guard[6]], inc=True)
                tb = p.emit("tensor", lambda e, c=c: e.transpose(
                    ps[6][0:64, 64:96], ap(A_DT + c * 64, 64, 32), i32t[:, :]),
                    deps=[dt_tok[0]], inc=True)
                te1 = p.emit("vector", lambda e: e.tensor_copy(
                    ap(M_AT, 32, 64), ps[6][0:64, 0:32]),
                    deps=[ta] + cg["g"], inc=True)
                te2 = p.emit("vector", lambda e: e.tensor_copy(
                    ap(M_DTT, 32, 64), ps[6][0:64, 64:96]), deps=[tb], inc=True)
                psum_guard[6] = te2
                if own:
                    tex = p.emit("scalar", lambda e: e.activation(
                        out=ap(M_EXPA, 32, 64), in_=ap(M_AT, 32, 64),
                        func=AF.Exp), deps=[te1], inc=True)
                tx2 = p.emit("vector", lambda e, acs_c=acs_c: e.tensor_scalar(
                    ap(M_ALB, 32, 32), i32t[:, :], acs_c[:, 63:64],
                    None, ALU.mult), deps=[dt_tok[1], te2], inc=True)
                tdm = p.emit("tensor", lambda e: e.matmul(
                    ps[6][:, 0:32], ones32t[:, :], ap(M_ALB, 32, 32),
                    start=True, stop=True), deps=[tx2, psum_guard[6]], inc=True)
                td1 = p.emit("vector", lambda e: e.tensor_sub(
                    ap(M_DECT, 32, 64), ps[6][0:64, 0:32], ap(M_AT, 32, 64)),
                    deps=[tdm, te1], inc=True)
                td2 = p.emit("scalar", lambda e: e.activation(
                    out=ap(M_DECT, 32, 64), in_=ap(M_DECT, 32, 64),
                    func=AF.Exp), deps=[td1], inc=True)
                tdb = p.emit("scalar", lambda e: e.activation(
                    out=ap(M_DECBC, 32), in_=ps[6][:, 0:32], func=AF.Exp),
                    deps=[tdm] + cg["decbc"], inc=True)
                psum_guard[6] = tdb
                xt_toks = []
                for dt in range(DT):
                    bi = dt // 4
                    tt = p.emit("tensor", lambda e, dt=dt, c=c, bi=bi: e.transpose(
                        ps[bi][0:64, (dt % 4) * 128:(dt % 4) * 128 + 128],
                        un[:, A_CONV + dt * 256 + c * 64:
                           A_CONV + dt * 256 + c * 64 + 64],
                        identt[:, :]),
                        deps=[conv_toks[dt]]
                        + ([psum_guard[bi]] if dt % 4 == 0 else []),
                        inc=(dt % 4 == 3))
                    if dt % 4 == 3:
                        ev = p.emit("vector", lambda e, bi=bi: e.tensor_copy(
                            ap(A_XT + bi * 512, 512, 64), ps[bi][0:64, 0:512]),
                            deps=[tt] + cg["xt"], inc=True)
                        psum_guard[bi] = ev
                        xt_toks.append(ev)
                tbt = None
                for i in range(2):
                    tbt = p.emit("tensor", lambda e, i=i, c=c: e.transpose(
                        ps[4][0:64, i * 128:(i + 1) * 128],
                        un[:, A_CONV + (16 + i) * 256 + c * 64:
                           A_CONV + (16 + i) * 256 + c * 64 + 64],
                        identt[:, :]),
                        deps=[conv_toks[16 + i]]
                        + ([psum_guard[4]] if i == 0 else []),
                        inc=(i == 1))
                tbe = p.emit("vector", lambda e: e.tensor_copy(
                    ap(M_BTM, 256, 64), ps[4][0:64, 0:256]),
                    deps=[tbt] + s_dep, inc=True)
                psum_guard[4] = tbe
                ty0 = None
                if own:
                    ty0 = p.emit("vector", lambda e: e.tensor_tensor(
                        ap(A_Y, 2048, 64).rearrange("p (h w) -> p h w", h=HEADS),
                        ap(A_XT, 2048, 64).rearrange("p (h w) -> p h w", h=HEADS),
                        dskipp[0:64, :, None].broadcast_to([64, HEADS, HDIM]),
                        ALU.mult), deps=xt_toks + cg["y"], inc=True)
                    tcb = None
                    for i in range(2):
                        tcb = p.emit("tensor", lambda e, i=i, c=c: e.matmul(
                            ps[4][0:64, 256:320],
                            un[:, A_CONV + (16 + i) * 256 + c * 64:
                               A_CONV + (16 + i) * 256 + c * 64 + 64],
                            un[:, A_CONV + (18 + i) * 256 + c * 64:
                               A_CONV + (18 + i) * 256 + c * 64 + 64],
                            start=(i == 0), stop=(i == 1)),
                            deps=[conv_toks[16 + i], conv_toks[18 + i], tbe],
                            inc=(i == 1))
                    tcbe = p.emit("vector", lambda e: e.tensor_mul(
                        ap(A_CBT, 64, 64), ps[4][0:64, 256:320],
                        masktt[0:64, 0:64]), deps=[tcb], inc=True)
                    psum_guard[4] = tcbe
                    txh = p.emit("vector", lambda e, acs_c=acs_c: e.tensor_tensor(
                        ap(A_XH, 2048, 32).rearrange("p (h l) -> p h l", h=HEADS),
                        i32t[:, :, None].broadcast_to([32, 32, 64]),
                        acs_c[:, None, :].broadcast_to([32, 32, 64]),
                        ALU.mult), deps=[dt_tok[1]] + cg["xh"], inc=True)
                    g_toks = []
                    for j in range(4):
                        tgm = p.emit("tensor", lambda e, j=j: e.matmul(
                            ps[j][0:64, 0:512], ones32t[:, 0:64],
                            ap(A_XH + j * 512, 512, 32),
                            start=True, stop=True),
                            deps=[txh, psum_guard[j]], inc=True)
                        tgs = p.emit("vector", lambda e, j=j: e.tensor_tensor(
                            ap(A_G + j * 512, 512, 64).rearrange(
                                "p (h l) -> p h l", h=8),
                            ps[j][0:64, 0:512].rearrange("p (h l) -> p h l", h=8),
                            ap(M_AT, 32, 64)[:, j * 8:(j + 1) * 8, None]
                            .broadcast_to([64, 8, 64]),
                            ALU.subtract), deps=[tgm, te1] + cg["g"], inc=True)
                        psum_guard[j] = tgs
                        g_toks.append(tgs)
                    tgc = p.emit("vector", lambda e: e.tensor_scalar_min(
                        ap(A_G, 2048, 64), ap(A_G, 2048, 64), 0.0),
                        deps=g_toks, inc=True)
                    tge = p.emit("scalar", lambda e: e.activation(
                        out=ap(A_G, 2048, 64), in_=ap(A_G, 2048, 64),
                        func=AF.Exp), deps=[tgc], inc=True)
                    tgk = p.emit("vector", lambda e: e.tensor_tensor(
                        ap(A_G, 2048, 64).rearrange("p (h l) -> p h l", h=HEADS),
                        ap(A_G, 2048, 64).rearrange("p (h l) -> p h l", h=HEADS),
                        masktt[0:64, None, 0:64].broadcast_to([64, HEADS, 64]),
                        ALU.mult), deps=[tge], inc=True)
                    tgk = p.emit("vector", lambda e: e.tensor_tensor(
                        ap(A_G, 2048, 64).rearrange("p (h l) -> p h l", h=HEADS),
                        ap(A_G, 2048, 64).rearrange("p (h l) -> p h l", h=HEADS),
                        ap(A_CBT, 64, 64)[:, None, :].broadcast_to([64, HEADS, 64]),
                        ALU.mult), deps=[tgk, tcbe], inc=True)
                txt = p.emit("vector", lambda e: e.tensor_tensor(
                    ap(A_XT, 2048, 64).rearrange("p (h w) -> p h w", h=HEADS),
                    ap(A_XT, 2048, 64).rearrange("p (h w) -> p h w", h=HEADS),
                    ap(M_DTT, 32, 64)[:, :, None].broadcast_to([64, HEADS, HDIM]),
                    ALU.mult),
                    deps=xt_toks + [te2] + ([ty0] if own else []), inc=True)
                yd_toks = [None] * 4
                yo_toks = [None] * 4
                y_toks_c = []
                if own:
                    for h in range(HEADS):
                        bi = h // 8
                        off = (h % 8) * 64
                        td = p.emit("tensor", lambda e, h=h, bi=bi, off=off: e.matmul(
                            ps[bi][0:64, off:off + 64],
                            ap(A_G + h * 64, 64, 64),
                            ap(A_XT + h * 64, 64, 64),
                            start=True, stop=True),
                            deps=[tgk, txt]
                            + ([psum_guard[bi]] if h % 8 == 0 else []),
                            inc=(h % 8 == 7))
                        if h % 8 == 7:
                            yd_toks[bi] = td
                        to = None
                        for i in range(2):
                            to = p.emit("tensor", lambda e, h=h, bi=bi, off=off,
                                        i=i, c=c: e.matmul(
                                ps[4 + bi][0:64, off:off + 64],
                                un[:, A_CONV + (18 + i) * 256 + c * 64:
                                   A_CONV + (18 + i) * 256 + c * 64 + 64],
                                un[:, A_S + i * 2048 + h * 64:
                                   A_S + i * 2048 + h * 64 + 64],
                                start=(i == 0), stop=(i == 1)),
                                deps=[conv_toks[18 + i]] + s_dep
                                + ([psum_guard[4 + bi]] if h % 8 == 0 and i == 0 else []),
                                inc=(h % 8 == 7 and i == 1))
                        if h % 8 == 7:
                            yo_toks[bi] = to
                    prev_t1r = None
                    for bi in range(4):
                        t1 = p.emit("vector", lambda e, bi=bi: e.tensor_tensor(
                            ap(A_XH, 512, 64).rearrange("p (h l) -> p h l", h=8),
                            ps[4 + bi][0:64, 0:512].rearrange("p (h l) -> p h l", h=8),
                            ap(M_EXPA, 32, 64)[:, bi * 8:(bi + 1) * 8, None]
                            .broadcast_to([64, 8, 64]), ALU.mult),
                            deps=[yo_toks[bi], tex, prev_t1r, txh], inc=True)
                        psum_guard[4 + bi] = t1
                        t2_ = p.emit("gpsimd", lambda e, bi=bi: e.tensor_add(
                            ap(A_Y + bi * 512, 512, 64),
                            ap(A_Y + bi * 512, 512, 64),
                            ap(A_XH, 512, 64)), deps=[t1, ty0], inc=True)
                        prev_t1r = t2_
                        t3 = p.emit("vector", lambda e, bi=bi: e.tensor_add(
                            ap(A_Y + bi * 512, 512, 64),
                            ap(A_Y + bi * 512, 512, 64),
                            ps[bi][0:64, 0:512]), deps=[t2_, yd_toks[bi]], inc=True)
                        psum_guard[bi] = t3
                        y_toks_c.append(t3)
                    cg["xh"] = [prev_t1r, y_toks_c[-1]]
                txtd = p.emit("vector", lambda e: e.tensor_tensor(
                    ap(A_XT, 2048, 64).rearrange("p (h w) -> p h w", h=HEADS),
                    ap(A_XT, 2048, 64).rearrange("p (h w) -> p h w", h=HEADS),
                    ap(M_DECT, 32, 64)[:, :, None].broadcast_to([64, HEADS, HDIM]),
                    ALU.mult),
                    deps=[txt, td2] + (yd_toks if own else []), inc=True)
                new_s = []
                tsm_last = None
                for nt2 in range(2):
                    for j in range(4):
                        bi = 6 + ((nt2 * 4 + j) % 2)
                        tsm = p.emit("tensor", lambda e, nt2=nt2, j=j, bi=bi: e.matmul(
                            ps[bi][:, 0:512], ap(M_BTM + nt2 * 128, 128, 64),
                            ap(A_XT + j * 512, 512, 64),
                            start=True, stop=True),
                            deps=[txtd, tbe, psum_guard[bi]], inc=True)
                        tsm_last = tsm
                        sslice = un[:, A_S + nt2 * 2048 + j * 512:
                                    A_S + nt2 * 2048 + j * 512 + 512]
                        tu1 = p.emit("vector", lambda e, sslice=sslice, j=j: e.tensor_tensor(
                            sslice.rearrange("p (h w) -> p h w", h=8),
                            sslice.rearrange("p (h w) -> p h w", h=8),
                            ap(M_DECBC, 32)[:, j * 8:(j + 1) * 8, None]
                            .broadcast_to([128, 8, HDIM]), ALU.mult),
                            deps=[tsm, tdb] + s_dep
                            + (yo_toks if own else []), inc=True)
                        tu2 = p.emit("vector", lambda e, sslice=sslice, bi=bi: e.tensor_add(
                            sslice, sslice, ps[bi][:, 0:512]),
                            deps=[tu1], inc=True)
                        psum_guard[bi] = tu2
                        new_s.append(tu2)
                s_dep = new_s
                cg["xt"] = [tsm_last]
                cg["decbc"] = new_s[-1:]
                if upto == "l1a_sdump" and q == 0 and phase == 0:
                    for j2 in range(4):
                        tdmp = p.emit("gpsimd", lambda e, j2=j2, c=c: e.dma_start(
                            out=hbm["dbg"][:, :].rearrange(
                                "(t p) l -> p t l", p=128)[:, c * 4 + j2, :],
                            in_=un[:, A_S + j2 * 1024:A_S + (j2 + 1) * 1024]),
                            deps=new_s, dma=True)
                        s_dep = s_dep + [tdmp]
                if upto in ("l1a_sdump", "l1a_pre1c1") and q == 0 and phase == 0 \
                        and c == 0:
                    d1 = p.emit("gpsimd", lambda e: e.dma_start(
                        out=hbm["out"][0:64, :], in_=ap(A_XT, 2048, 64)),
                        deps=[txtd, tsm_last], dma=True)
                    d2 = p.emit("gpsimd", lambda e: e.dma_start(
                        out=hbm["out"][64:128, 0:256], in_=ap(M_BTM, 256, 64)),
                        deps=[tbe, tsm_last], dma=True)
                    d3 = p.emit("gpsimd", lambda e: e.dma_start(
                        out=hbm["out"][64:128, 256:288], in_=ap(M_DECT, 32, 64)),
                        deps=[td2, txtd], dma=True)
                    d4 = p.emit("gpsimd", lambda e: e.dma_start(
                        out=hbm["out"][128:256, 0:32], in_=ap(M_DECBC, 32)),
                        deps=[tdb] + new_s, dma=True)
                    d5 = p.emit("gpsimd", lambda e: e.dma_start(
                        out=hbm["out"][256:288, 0:64], in_=ap(A_ACS, 64, 32)),
                        deps=[td2], dma=True)
                    d6 = p.emit("gpsimd", lambda e: e.dma_start(
                        out=hbm["out"][288:352, 0:32], in_=ap(M_AT, 32, 64)),
                        deps=[td2], dma=True)
                    d7 = p.emit("gpsimd", lambda e: e.dma_start(
                        out=hbm["out"][352:384, 0:32], in_=ap(M_ALB, 32, 32)),
                        deps=[tdm], dma=True)
                    p.emit("gpsimd", lambda e: e.nop(),
                           deps=[d1, d2, d3, d4, d5, d6, d7])
                    s_dep = s_dep + [d1, d2, d3, d4, d5, d6, d7]
                if own:
                    zt_toks = []
                    for dt in range(DT):
                        gidx = dt // 4
                        bi = 4 + gidx % 2
                        tt = p.emit("tensor", lambda e, dt=dt, c=c, bi=bi: e.transpose(
                            ps[bi][0:64, (dt % 4) * 128:(dt % 4) * 128 + 128],
                            un[:, A_Z + dt * 256 + c * 64:
                               A_Z + dt * 256 + c * 64 + 64],
                            identt[:, :]),
                            deps=[z_toks[dt]]
                            + ([psum_guard[bi]] if dt % 4 == 0 else []),
                            inc=(dt % 4 == 3))
                        if dt % 4 == 3:
                            ev = p.emit("scalar", lambda e, gidx=gidx, bi=bi: e.activation(
                                out=ap(A_ZT + gidx * 512, 512, 64),
                                in_=ps[bi][0:64, 0:512], func=AF.Silu),
                                deps=[tt] + cg["zt"], inc=True)
                            psum_guard[bi] = ev
                            zt_toks.append(ev)
                    tyz = p.emit("vector", lambda e: e.tensor_mul(
                        ap(A_Y, 2048, 64), ap(A_Y, 2048, 64),
                        ap(A_ZT, 2048, 64)),
                        deps=zt_toks + y_toks_c, inc=True)
                    cg["zt"] = [tyz]
                    tss = p.emit("scalar", lambda e: e.activation(
                        out=ap(A_G, 2048, 64), in_=ap(A_Y, 2048, 64),
                        func=AF.Square, accum_out=ap(M_RSTD, 1, 64)),
                        deps=[tyz], inc=True)
                    cg["g"] = [tss]
                    trs = p.emit("scalar", lambda e: e.activation(
                        out=ap(M_RSTD, 1, 64), in_=ap(M_RSTD, 1, 64),
                        func=AF.Sqrt, scale=1.0 / D, bias=eps_c[0:64, :]),
                        deps=[tss, t_eps], inc=True)
                    trs = p.emit("vector", lambda e: e.reciprocal(
                        ap(M_RSTD, 1, 64), ap(M_RSTD, 1, 64)),
                        deps=[trs], inc=True)
                    tyn = p.emit("vector", lambda e: e.tensor_scalar(
                        ap(A_Y, 2048, 64), ap(A_Y, 2048, 64),
                        ap(M_RSTD, 1, 64), None, ALU.mult),
                        deps=[trs], inc=True)
                    lastev = None
                    for dt in range(DT):
                        bi = dt // 4
                        tt = p.emit("tensor", lambda e, dt=dt, bi=bi: e.transpose(
                            ps[bi][:, (dt % 4) * 64:(dt % 4) * 64 + 64],
                            ap(A_Y + dt * 128, 128, 64), identt[0:64, 0:64]),
                            deps=[tyn]
                            + ([psum_guard[bi]] if dt % 4 == 0 else []),
                            inc=(dt % 4 == 3))
                        if dt % 4 == 3:
                            for d2 in range(dt - 3, dt + 1):
                                lastev = p.emit("vector", lambda e, d2=d2, bi=bi, c=c:
                                    e.tensor_scalar(
                                        un[:, A_Y3 + d2 * 256 + c * 64:
                                           A_Y3 + d2 * 256 + c * 64 + 64],
                                        ps[bi][:, (d2 % 4) * 64:(d2 % 4) * 64 + 64],
                                        gnw_c(d2), None, ALU.mult),
                                    deps=[tt], inc=True)
                                y3_toks[d2] = lastev
                            psum_guard[bi] = lastev
                    cg["y"] = [lastev]

            if not own:
                return s_dep
            o1_toks = [None] * DT
            for m in range(DT):
                bank = 6 + (m % 2)
                t_w2 = p.emit("sync", lambda e, m=m: e.dma_start(
                    out=wslot.rearrange("p (k c) -> p k c", k=DT),
                    in_=hbm["wout"][:, m * 128:(m + 1) * 128].rearrange(
                        "(k p) c -> p k c", p=128)),
                    deps=[prev_mm[0]], dma=True)
                mm = None
                for k in range(DT):
                    mm = p.emit("tensor", lambda e, k=k, bank=bank: e.matmul(
                        ps[bank][:, 0:256], wslot[:, k * 128:(k + 1) * 128],
                        un[:, A_Y3 + k * 256:A_Y3 + k * 256 + 256],
                        start=(k == 0), stop=(k == DT - 1)),
                        deps=[t_w2, psum_guard[bank], y3_toks[k]],
                        inc=(k == DT - 1))
                prev_mm[0] = mm
                o1_toks[m] = p.emit("vector", lambda e, m=m, bank=bank: e.tensor_copy(
                    ap(A_CONV + m * 256, 256), ps[bank][:, 0:256]),
                    deps=[mm], inc=True)
                psum_guard[bank] = o1_toks[m]
            x3_toks = [None] * DT
            for m in range(DT):
                bank = 6 + (m % 2)
                t_w2 = p.emit("sync", lambda e, m=m: e.dma_start(
                    out=wslot.rearrange("p (k c) -> p k c", k=DT),
                    in_=hbm["wo"][:, m * 128:(m + 1) * 128].rearrange(
                        "(k p) c -> p k c", p=128)),
                    deps=[prev_mm[0]], dma=True)
                mm = None
                for k in range(DT):
                    mm = p.emit("tensor", lambda e, k=k, bank=bank: e.matmul(
                        ps[bank][:, 0:256], wslot[:, k * 128:(k + 1) * 128],
                        ap(A_CONV + k * 256, 256),
                        start=(k == 0), stop=(k == DT - 1)),
                        deps=[t_w2, psum_guard[bank], o1_toks[k]],
                        inc=(k == DT - 1))
                prev_mm[0] = mm
                x3_toks[m] = p.emit("vector", lambda e, m=m, bank=bank: e.tensor_add(
                    xsl(m), xsl(m), ps[bank][:, 0:256]), deps=[mm], inc=True)
                psum_guard[bank] = x3_toks[m]
            return ln_fm(256, xsl, x3_toks, None, None, xsl)

        MO_ACC, MO_H, MO_WD = 0, 8192, 11264
        MO_WG, MO_WU, MO_SC, MO_CB = 23552, 25600, 27648, 27776
        MO_COL, MO_GT, MO_XS = 27904, 28160, 28672

        def moe_phase():
            prev_rt = [None]
            out_dma = []
            mm = None
            for ts in range(8):
                for k in range(DT):
                    mm = p.emit("tensor", lambda e, ts=ts, k=k: e.matmul(
                        ps[6][0:128, ts * 16:ts * 16 + 16],
                        x_fm[:, k * HALF + ts * 128: k * HALF + ts * 128 + 128],
                        routerp[:, k * 16:(k + 1) * 16],
                        start=(k == 0), stop=(k == DT - 1)),
                        deps=[psum_guard[6]] if ts == 0 and k == 0 else [],
                        inc=(k == DT - 1))
            tsc = p.emit("vector", lambda e: e.tensor_copy(
                un[:, MO_SC:MO_SC + 128], ps[6][:, 0:128]), deps=[mm], inc=True)
            psum_guard[6] = tsc
            for ts in range(8):
                sc = un[:, MO_SC + ts * 16:MO_SC + ts * 16 + 16]
                cb = un[:, MO_CB + ts * 16:MO_CB + ts * 16 + 16]
                m1 = un[:, MO_COL:MO_COL + 1]
                nm1 = un[:, MO_COL + 1:MO_COL + 2]
                m2 = un[:, MO_COL + 2:MO_COL + 3]
                den = un[:, MO_COL + 3:MO_COL + 4]
                t1 = p.emit("vector", lambda e, sc=sc: e.reduce_max(
                    m1, sc, AX.X), deps=[tsc, prev_rt[0]], inc=True)
                t2 = p.emit("vector", lambda e: e.tensor_scalar_mul(
                    nm1, m1, -1.0), deps=[t1], inc=True)
                t3 = p.emit("scalar", lambda e, sc=sc, cb=cb: e.activation(
                    out=cb, in_=sc, func=AF.Exp, bias=nm1), deps=[t2], inc=True)
                eq1 = un[:, MO_COL + 4:MO_COL + 20]
                msk = un[:, MO_COL + 20:MO_COL + 36]
                t4 = p.emit("vector", lambda e, sc=sc: e.tensor_scalar(
                    eq1, sc, m1, None, ALU.is_equal), deps=[t1], inc=True)
                t5 = p.emit("vector", lambda e, sc=sc: e.scalar_tensor_tensor(
                    msk, eq1, -1e30, sc, ALU.mult, ALU.add), deps=[t4], inc=True)
                t6 = p.emit("vector", lambda e: e.reduce_max(
                    m2, msk, AX.X), deps=[t5], inc=True)
                t7 = p.emit("vector", lambda e: e.tensor_scalar(
                    msk, msk, m2, None, ALU.is_equal), deps=[t6], inc=True)
                t8 = p.emit("vector", lambda e: e.tensor_add(
                    eq1, eq1, msk), deps=[t7, t4], inc=True)
                t9 = p.emit("vector", lambda e, cb=cb: e.tensor_mul(
                    cb, cb, eq1), deps=[t8, t3], inc=True)
                ta_ = p.emit("vector", lambda e, cb=cb: e.reduce_sum(
                    den, cb, AX.X), deps=[t9], inc=True)
                tb_ = p.emit("vector", lambda e: e.reciprocal(
                    den, den), deps=[ta_], inc=True)
                tc_ = p.emit("vector", lambda e, cb=cb: e.tensor_scalar(
                    cb, cb, den, None, ALU.mult), deps=[tb_], inc=True)
                prev_rt[0] = tc_
            p.barrier()

            for half in range(2):
                t0 = half * 512
                tz = p.emit("vector", lambda e: e.memset(
                    un[:, MO_ACC:MO_ACC + 8192], 0.0), inc=True)
                acc_toks = [tz] * 16
                wg_tok = [None]
                wd_toks = [None] * 6
                for ex in range(EXPERTS):
                    h_toks = [None] * FT
                    for ft in range(FT):
                        twg = p.emit("sync", lambda e, ex=ex, ft=ft: e.dma_start(
                            out=un[:, MO_WG:MO_WG + 2048].rearrange(
                                "p (k c) -> p k c", k=DT),
                            in_=hbm["gw"][ex, :, ft * 128:(ft + 1) * 128].rearrange(
                                "(k p) c -> p k c", p=128)),
                            deps=[wg_tok[0]], dma=True)
                        twu = p.emit("sync", lambda e, ex=ex, ft=ft: e.dma_start(
                            out=un[:, MO_WU:MO_WU + 2048].rearrange(
                                "p (k c) -> p k c", k=DT),
                            in_=hbm["uw"][ex, :, ft * 128:(ft + 1) * 128].rearrange(
                                "(k p) c -> p k c", p=128)),
                            deps=[wg_tok[0]], dma=True)
                        twd = p.emit("sync", lambda e, ex=ex, ft=ft: e.dma_start(
                            out=un[:, MO_WD + ft * 2048:MO_WD + (ft + 1) * 2048],
                            in_=hbm["dw"][ex, ft * 128:(ft + 1) * 128, :]),
                            deps=[wd_toks[ft]], dma=True)
                        wd_toks[ft] = twd
                        mmg = None
                        mmu = None
                        for k in range(DT):
                            mmg = p.emit("tensor", lambda e, k=k, t0=t0: e.matmul(
                                ps[6][:, :],
                                un[:, MO_WG + k * 128:MO_WG + (k + 1) * 128],
                                x_fm[:, k * HALF + t0: k * HALF + t0 + 512],
                                start=(k == 0), stop=(k == DT - 1)),
                                deps=[twg, psum_guard[6]], inc=(k == DT - 1))
                        for k in range(DT):
                            mmu = p.emit("tensor", lambda e, k=k, t0=t0: e.matmul(
                                ps[7][:, :],
                                un[:, MO_WU + k * 128:MO_WU + (k + 1) * 128],
                                x_fm[:, k * HALF + t0: k * HALF + t0 + 512],
                                start=(k == 0), stop=(k == DT - 1)),
                                deps=[twu, psum_guard[7]], inc=(k == DT - 1))
                        wg_tok[0] = mmu
                        tg_ = p.emit("scalar", lambda e: e.activation(
                            out=un[:, MO_GT:MO_GT + 512], in_=ps[6][:, :],
                            func=AF.Silu), deps=[mmg, h_toks[ft - 1] if ft else None],
                            inc=True)
                        psum_guard[6] = tg_
                        th_ = p.emit("vector", lambda e, ft=ft: e.tensor_mul(
                            un[:, MO_H + ft * 512:MO_H + (ft + 1) * 512],
                            un[:, MO_GT:MO_GT + 512], ps[7][:, :]),
                            deps=[tg_, mmu], inc=True)
                        psum_guard[7] = th_
                        h_toks[ft] = th_
                    for ts in range(4):
                        gts = half * 4 + ts
                        ccol = un[:, MO_CB + gts * 16 + ex:MO_CB + gts * 16 + ex + 1]
                        for dc in range(4):
                            bank = (ts * 4 + dc) % 4
                            mm2 = None
                            for ft in range(FT):
                                mm2 = p.emit("tensor", lambda e, ft=ft, ts=ts,
                                             dc=dc, bank=bank: e.matmul(
                                    ps[bank][:, :],
                                    un[:, MO_H + ft * 512 + ts * 128:
                                       MO_H + ft * 512 + ts * 128 + 128],
                                    un[:, MO_WD + ft * 2048 + dc * 512:
                                       MO_WD + ft * 2048 + dc * 512 + 512],
                                    start=(ft == 0), stop=(ft == FT - 1)),
                                    deps=[h_toks[ft], wd_toks[ft],
                                          psum_guard[bank]], inc=(ft == FT - 1))
                            ai = ts * 4 + dc
                            tacc = p.emit("vector", lambda e, ts=ts, dc=dc,
                                          bank=bank, ccol=ccol: e.scalar_tensor_tensor(
                                un[:, MO_ACC + ts * 2048 + dc * 512:
                                   MO_ACC + ts * 2048 + dc * 512 + 512],
                                ps[bank][:, :], ccol,
                                un[:, MO_ACC + ts * 2048 + dc * 512:
                                   MO_ACC + ts * 2048 + dc * 512 + 512],
                                ALU.mult, ALU.add),
                                deps=[mm2, acc_toks[ai], prev_rt[0]], inc=True)
                            psum_guard[bank] = tacc
                            acc_toks[ai] = tacc
                    wd_toks = [acc_toks[15]] * FT
                for ts in range(4):
                    gts = half * 4 + ts
                    tts = None
                    for dt in range(DT):
                        bank = 4 + (dt // 4) % 2
                        tts = p.emit("tensor", lambda e, dt=dt, ts=ts, t0=t0,
                                     bank=bank: e.transpose(
                            ps[bank][:, (dt % 4) * 128:(dt % 4) * 128 + 128],
                            x_fm[:, dt * HALF + t0 + ts * 128:
                                 dt * HALF + t0 + ts * 128 + 128],
                            identt[:, :]),
                            deps=[psum_guard[bank]] if dt % 4 == 0 else [],
                            inc=(dt % 4 == 3))
                        if dt % 4 == 3:
                            tsum = p.emit("vector", lambda e, dt=dt, ts=ts,
                                          bank=bank: e.tensor_add(
                                un[:, MO_XS + (dt - 3) * 128:MO_XS + (dt + 1) * 128],
                                ps[bank][:, 0:512],
                                un[:, MO_ACC + ts * 2048 + (dt - 3) * 128:
                                   MO_ACC + ts * 2048 + (dt + 1) * 128]),
                                deps=[tts, acc_toks[ts * 4 + (dt // 4)],
                                      prev_rt[0]], inc=True)
                            psum_guard[bank] = tsum
                    mcol = un[:, MO_COL + 36:MO_COL + 37]
                    sscol = un[:, MO_COL + 37:MO_COL + 38]
                    t1 = p.emit("vector", lambda e: e.reduce_sum(
                        mcol, un[:, MO_XS:MO_XS + 2048], AX.X),
                        deps=[tsum, prev_rt[0]], inc=True)
                    t2 = p.emit("vector", lambda e: e.tensor_scalar_mul(
                        mcol, mcol, 1.0 / D), deps=[t1], inc=True)
                    tsq = p.emit("scalar", lambda e: e.activation(
                        out=un[:, MO_WD:MO_WD + 2048],
                        in_=un[:, MO_XS:MO_XS + 2048],
                        func=AF.Square, accum_out=sscol),
                        deps=[t2, acc_toks[15]], inc=True)
                    t4 = p.emit("vector", lambda e: e.tensor_scalar_mul(
                        sscol, sscol, 1.0 / D), deps=[tsq], inc=True)
                    t5 = p.emit("vector", lambda e: e.tensor_mul(
                        un[:, MO_COL + 38:MO_COL + 39], mcol, mcol),
                        deps=[t2, t4], inc=True)
                    t6 = p.emit("vector", lambda e: e.tensor_sub(
                        sscol, sscol, un[:, MO_COL + 38:MO_COL + 39]),
                        deps=[t5], inc=True)
                    t7 = p.emit("scalar", lambda e: e.activation(
                        out=sscol, in_=sscol, func=AF.Sqrt, bias=eps_c),
                        deps=[t6, t_eps], inc=True)
                    t8 = p.emit("vector", lambda e: e.reciprocal(
                        sscol, sscol), deps=[t7], inc=True)
                    t9 = p.emit("vector", lambda e: e.tensor_scalar(
                        un[:, MO_XS:MO_XS + 2048], un[:, MO_XS:MO_XS + 2048],
                        mcol, sscol, ALU.subtract, ALU.mult),
                        deps=[t8], inc=True)
                    td_ = p.emit("gpsimd", lambda e, gts=gts: e.dma_start(
                        out=hbm["out"][gts * 128:(gts + 1) * 128, :],
                        in_=un[:, MO_XS:MO_XS + 2048]), deps=[t9], dma=True)
                    out_dma.append(td_)
                    prev_rt[0] = td_
                p.barrier()
            return out_dma

        for col in range(DT * NST):
            hcar_dep[col] = t_hz

        def load_x(src, c0):
            return p.emit("sync", lambda e, c0=c0, src=src: e.dma_start(
                out=x_fm[:, :].rearrange("p (t c) -> p t c", t=DT)[:, :, c0:c0 + 512],
                in_=hbm[src][:, :].rearrange("(t p) l -> p t l", p=128)[:, :, c0:c0 + 512]),
                dma=True)

        out_toks = []

        def dbg_dump_fm(ap_fn, toks):
            for dt in range(DT):
                out_toks.append(p.emit("gpsimd", lambda e, dt=dt: e.dma_start(
                    out=hbm["dbg"][:, :].rearrange("(t p) l -> p t l", p=128)[:, dt, :],
                    in_=ap_fn(dt)), deps=[toks[dt]], dma=True))

        def dbg_dump_hilo(toks_h):
            for half in range(2):
                c0 = half * 512
                for dt in range(DT):
                    tt = p.emit("vector", lambda e, dt=dt, c0=c0: e.tensor_add(
                        lnscr[:, 0:512], x1hi(dt, c0), x1lo(dt, c0)),
                        deps=[toks_h[half][dt]] + out_toks[-1:], inc=True)
                    out_toks.append(p.emit("gpsimd", lambda e, dt=dt, c0=c0: e.dma_start(
                        out=hbm["dbg"][:, :].rearrange("(t p) l -> p t l", p=128)[:, dt, c0:c0 + 512],
                        in_=lnscr[:, 0:512]), deps=[tt], dma=True))

        stop = False
        for phase, src in enumerate(("xpre", "xown")):
            want = "" if phase == 1 else "_pre"
            w_l0a = load_l0a_weights()
            x_toks_h = []
            for half in range(2):
                c0 = half * 512
                xt = load_x(src, c0)
                y_toks = l0a_half(c0, [xt], w_l0a)
                toks = ln_fm(512,
                             (lambda dt, c0=c0: x_fm[:, dt * HALF + c0: dt * HALF + c0 + 512]),
                             y_toks,
                             dsthi=(lambda dt, c0=c0: x1hi(dt, c0)),
                             dstlo=(lambda dt, c0=c0: x1lo(dt, c0)),
                             dst32=None)
                x_toks_h.append(toks)
                p.barrier()
            if upto == "l0a" + want:
                dbg_dump_hilo(x_toks_h)
                stop = True
                break

            for half in range(2):
                c0 = half * 512
                toks = l0ffn_chunk(c0, x_toks_h[half])
                x_toks_h[half] = ln_fm(
                    512,
                    (lambda dt, c0=c0: x_fm[:, dt * HALF + c0: dt * HALF + c0 + 512]),
                    toks, dsthi=None, dstlo=None,
                    dst32=(lambda dt, c0=c0: x_fm[:, dt * HALF + c0: dt * HALF + c0 + 512]))
                p.barrier()
            if upto == "l0ffn" + want:
                dbg_dump_fm(lambda dt: x_fm[:, dt * HALF: dt * HALF + HALF],
                            [x_toks_h[1][dt] for dt in range(DT)])
                stop = True
                break
            p.barrier()

            if phase == 0:
                t_s0 = p.emit("vector", lambda e: e.memset(
                    un[:, A_S:A_S + 4096], 0.0), inc=True)
                s_dep = [t_s0]
            else:
                t_sr = p.emit("sync", lambda e: e.dma_start(
                    out=un[:, A_S:A_S + 4096], in_=hbm["sspill"][:, :]),
                    dma=True)
                s_dep = [t_sr]
            nq = 1 if ((upto.startswith("l1a_pre1") or upto == "l1a_sdump")
                       and phase == 0) else 4
            nch = 4
            if upto.startswith("l1a_pre1c"):
                nch = int(upto[len("l1a_pre1c"):])
            for q in range(nq):
                res = l1a_quarter(phase, q, x_toks_h[q // 2], s_dep, nch)
                p.barrier()
                if phase == 0:
                    s_dep = res
                else:
                    x_toks_h[q // 2] = res
            if phase == 0:
                t_sp = p.emit("gpsimd", lambda e: e.dma_start(
                    out=hbm["sspill"][:, :], in_=un[:, A_S:A_S + 4096]),
                    deps=s_dep, dma=True)
                p.emit("gpsimd", lambda e: e.nop(), deps=[t_sp])
                p.barrier()
                if upto == "l1a_sdump":
                    stop = True
                    break
                if upto.startswith("l1a_pre"):
                    for j in range(4):
                        out_toks.append(p.emit("gpsimd", lambda e, j=j: e.dma_start(
                            out=hbm["dbg"][:, :].rearrange(
                                "(t p) l -> p t l", p=128)[:, j, :],
                            in_=un[:, A_S + j * 1024:A_S + (j + 1) * 1024]),
                            deps=s_dep, dma=True))
                    stop = True
                    break
            else:
                if upto == "l1a":
                    dbg_dump_fm(lambda dt: x_fm[:, dt * HALF: dt * HALF + HALF],
                                [x_toks_h[1][dt] for dt in range(DT)])
                    stop = True
                    break
                out_toks.extend(moe_phase())
            p.barrier()

        if out_toks:
            p.emit("gpsimd", lambda e: e.nop(), deps=list(out_toks))

        with nc.Block() as block:
            @block.sync
            def _(eng):
                _replay_one(p, "sync", eng, sems)
            @block.tensor
            def _(eng):
                _replay_one(p, "tensor", eng, sems)
            @block.vector
            def _(eng):
                _replay_one(p, "vector", eng, sems)
            @block.scalar
            def _(eng):
                _replay_one(p, "scalar", eng, sems)
            @block.gpsimd
            def _(eng):
                _replay_one(p, "gpsimd", eng, sems)

    return nc



def _u8(wq):
    a = np.ascontiguousarray(np.asarray(wq, np.int32))
    return a.view(np.uint8).reshape(a.shape + (4,))[..., 0]


def _prep_inputs(inp):
    f32 = np.float32
    x = np.ascontiguousarray(np.asarray(inp["L0_attn_x"], f32))
    def tile_col(v):
        return np.ascontiguousarray(v.reshape(DT, 128).T)
    w = {}
    w["wx_t"] = np.ascontiguousarray(np.asarray(inp["L0_attn_W_x"], f32).T)
    w["wdt_t"] = np.ascontiguousarray(np.asarray(inp["L0_attn_W_dt"], f32).T)
    w["bdt"] = tile_col(np.asarray(inp["L0_attn_b_dt"], f32))
    a0 = np.asarray(inp["L0_attn_A_log"], f32)
    w["a0"] = np.ascontiguousarray(a0.reshape(DT, 128, NST).transpose(1, 0, 2).reshape(128, DT * NST))
    w["d0"] = tile_col(np.asarray(inp["L0_attn_D"], f32))
    w["wqg_t"] = np.ascontiguousarray(_u8(inp["L0_ffn_wq_gate"]).T)
    w["wqu_t"] = np.ascontiguousarray(_u8(inp["L0_ffn_wq_up"]).T)
    w["wqd_t"] = np.ascontiguousarray(_u8(inp["L0_ffn_wq_down"]).T)
    w["sg"] = np.ascontiguousarray(np.asarray(inp["L0_ffn_s_gate"], f32).reshape(HT, 128).T)
    w["su"] = np.ascontiguousarray(np.asarray(inp["L0_ffn_s_up"], f32).reshape(HT, 128).T)
    w["sd"] = tile_col(np.asarray(inp["L0_ffn_s_down"], f32))
    w["nw"] = tile_col(np.asarray(inp["L1_attn_norm_w"], f32))
    w["win"] = np.ascontiguousarray(np.asarray(inp["L1_attn_w_in"], f32))
    cw = np.asarray(inp["L1_attn_conv_w"], f32)
    w["convw"] = np.ascontiguousarray(cw.reshape(NCT, 128, KCONV).transpose(1, 0, 2).reshape(128, NCT * KCONV))
    w["convb"] = np.ascontiguousarray(np.asarray(inp["L1_attn_conv_b"], f32).reshape(NCT, 128).T)
    pad = np.zeros((128, 1), f32); pad[:HEADS, 0] = np.asarray(inp["L1_attn_dt_bias"], f32)
    w["dtb"] = pad
    pad = np.zeros((128, 1), f32); pad[:HEADS, 0] = np.asarray(inp["L1_attn_A_log"], f32)
    w["a1"] = pad
    w["dskipbc"] = np.ascontiguousarray(np.broadcast_to(
        np.asarray(inp["L1_attn_D_skip"], f32)[None, :], (128, HEADS)))
    w["gnw"] = tile_col(np.asarray(inp["L1_attn_gnorm_w"], f32))
    w["wout"] = np.ascontiguousarray(np.asarray(inp["L1_attn_w_out"], f32))
    w["wo"] = np.ascontiguousarray(np.asarray(inp["L1_attn_wo_w"], f32))
    w["router"] = np.ascontiguousarray(np.asarray(inp["L1_ffn_router_w"], f32))
    w["gw"] = np.ascontiguousarray(np.asarray(inp["L1_ffn_gate_w"], f32))
    w["uw"] = np.ascontiguousarray(np.asarray(inp["L1_ffn_up_w"], f32))
    w["dw"] = np.ascontiguousarray(np.asarray(inp["L1_ffn_down_w"], f32))
    w["ident"] = np.eye(128, dtype=f32)
    w["onesr"] = np.ones((1, 128), f32)
    w["negr"] = -np.ones((1, 128), f32)
    w["onesc"] = np.ones((128, 1), f32)
    w["maskt"] = np.ascontiguousarray(np.triu(np.ones((64, 64), f32)))
    w["ones32"] = np.ones((32, 128), f32)
    w["i32"] = np.eye(32, dtype=f32)

    zeros_half = np.zeros((D, HALF), f32)
    in_maps = []
    for c in range(8):
        b = c // 2
        odd = c % 2
        m = dict(w)
        if odd:
            m["xpre"] = np.ascontiguousarray(x[b, 0:HALF, :].T)
            m["xown"] = np.ascontiguousarray(x[b, HALF:L, :].T)
        else:
            m["xpre"] = zeros_half
            m["xown"] = np.ascontiguousarray(x[b, 0:HALF, :].T)
        in_maps.append(m)
    return in_maps


def kernel(**inputs):
    in_maps = _prep_inputs(inputs)
    nc = build_program()
    res = run_bass_kernel_spmd(nc, in_maps, core_ids=list(range(8)))
    out = np.empty((B, L, D), np.float32)
    for c in range(8):
        b, odd = c // 2, c % 2
        out[b, odd * HALF:(odd + 1) * HALF, :] = res.results[c]["out"]
    return out


# version 04
# speedup vs baseline: 1.1474x; 1.1474x over previous
import math
import numpy as np
import concourse.bass as bass
import concourse.mybir as mybir
from concourse.bass_utils import run_bass_kernel_spmd

F32 = mybir.dt.float32
BF16 = mybir.dt.bfloat16
U8 = mybir.dt.uint8
AF = mybir.ActivationFunctionType
ALU = mybir.AluOpType
AX = mybir.AxisListType

D = 2048
DT = 16
B = 4
L = 2048
HALF = 1024
NST = 16
RNK = 64
HID = 5632
HT = 44
HEADS = 32
HDIM = 64
STATE = 256
CONVD = 2560
PROJ = 4640
NZT = 16
NCT = 20
KCONV = 4
CHUNK = 64
EXPERTS = 16
FDIM = 768
FT = 6
EPS = 1e-5
UNW = 32000

ENGINES = ("sync", "tensor", "vector", "scalar", "gpsimd")
N_DSEM = {"sync": 12, "gpsimd": 8}
DMA_SEMS = {e: tuple(f"d_{e}{i}" for i in range(n)) for e, n in N_DSEM.items()}
SEMS = ("s_sync", "s_pe", "s_dve", "s_act", "s_gps") \
    + DMA_SEMS["sync"] + DMA_SEMS["gpsimd"]
OWN_SEM = {"sync": "s_sync", "tensor": "s_pe", "vector": "s_dve",
           "scalar": "s_act", "gpsimd": "s_gps"}


class Op:
    __slots__ = ("fn", "deps", "inc")
    def __init__(self, fn, deps, inc):
        self.fn = fn
        self.deps = deps
        self.inc = inc


class Prog:
    def __init__(self):
        self.ops = {e: [] for e in ENGINES}
        self.cnt = {s: 0 for s in SEMS}
        self.drot = {e: 0 for e in N_DSEM}

    def emit(self, eng, fn, deps=(), inc=False, dma=False):
        deps = tuple(d for d in deps if d is not None)
        if dma:
            sems = DMA_SEMS[eng]
            sem = sems[self.drot[eng] % len(sems)]
            self.drot[eng] += 1
            if self.cnt[sem] > 0:
                deps = deps + ((sem, self.cnt[sem]),)
            self.cnt[sem] += 16
            tok = (sem, self.cnt[sem])
            self.ops[eng].append(Op(fn, deps, (sem, 16)))
            return tok
        if inc:
            sem = OWN_SEM[eng]
            self.cnt[sem] += 1
            tok = (sem, self.cnt[sem])
            self.ops[eng].append(Op(fn, deps, (sem, 1)))
            return tok
        self.ops[eng].append(Op(fn, deps, None))
        return None

    def barrier(self, engs=ENGINES):
        toks = {}
        for e in engs:
            toks[e] = self.emit(e, lambda x: x.nop(), inc=True)
        for e in engs:
            self.emit(e, lambda x: x.nop(),
                      deps=[toks[o] for o in engs if o != e])


def _replay_one(prog, ename, eng, sems):
    own = OWN_SEM[ename]
    used = set()
    for e2 in ENGINES:
        for op in prog.ops[e2]:
            for d in op.deps:
                used.add(d)
    waited = {}
    cur = 0
    drained = 0
    for op in prog.ops[ename]:
        for (sn, v) in op.deps:
            if sn == own:
                if ename != "tensor" and v > drained:
                    eng.drain()
                    drained = cur
                continue
            if waited.get(sn, 0) >= v:
                continue
            eng.wait_ge(sems[sn], v)
            waited[sn] = v
        instr = op.fn(eng)
        if op.inc is not None:
            sn, k = op.inc
            instr.then_inc(sems[sn], k)
            if sn == own:
                cur += 1
                if (own, cur) in used:
                    eng.wait_ge(sems[own], cur)



def build_program(upto="all"):
    nc = bass.Bass("TRN2", target_bir_lowering=False, debug=False)

    hbm = {}
    def H(name, shape, dt=F32, kind="ExternalInput"):
        hbm[name] = nc.dram_tensor(name, shape, dt, kind=kind).ap()
        return hbm[name]

    H("xpre", [D, HALF])
    H("xown", [D, HALF])
    H("wx_t", [D, 96])
    H("wdt_t", [RNK, D])
    H("bdt", [128, DT])
    H("a0", [128, DT * NST])
    H("d0", [128, DT])
    H("wqg_t", [D, HID], U8)
    H("wqu_t", [D, HID], U8)
    H("wqd_t", [HID, D], U8)
    H("sg", [128, HT])
    H("su", [128, HT])
    H("sd", [128, DT])
    H("nw", [128, DT])
    H("win", [D, PROJ])
    H("convw", [128, NCT * KCONV])
    H("convb", [128, NCT])
    H("dtb", [128, 1])
    H("a1", [128, 1])
    H("dskipbc", [128, HEADS])
    H("gnw", [128, DT])
    H("ones32", [32, 128])
    H("i32", [32, 32])
    H("iotac_h", [128, 2])
    H("iotabc_h", [128, 256])
    H("wout", [D, D])
    H("wo", [D, D])
    H("router", [D, EXPERTS])
    H("gw", [EXPERTS, D, FDIM])
    H("uw", [EXPERTS, D, FDIM])
    H("dw", [EXPERTS, FDIM, D])
    H("ident", [128, 128])
    H("onesr", [1, 128])
    H("negr", [1, 128])
    H("onesc", [128, 1])
    H("maskt", [64, 64])
    H("sspill", [128, 4096], kind="ExternalOutput")
    H("out", [HALF, D], kind="ExternalOutput")
    H("dbg", [D, HALF], kind="ExternalOutput")

    import contextlib
    stack = contextlib.ExitStack()

    with stack:
        x_fm = stack.enter_context(nc.sbuf_tensor("x_fm", [128, DT * HALF], F32))
        xfb = x_fm[:, :].bitcast(BF16)
        un = stack.enter_context(nc.sbuf_tensor("un", [128, UNW], F32))
        unb = un[:, :].bitcast(BF16)
        unu = un[:, :].bitcast(U8)
        xproj = stack.enter_context(nc.sbuf_tensor("xproj", [128, 512], F32))
        hcar = stack.enter_context(nc.sbuf_tensor("hcar", [128, DT * NST], F32))
        a0p = stack.enter_context(nc.sbuf_tensor("a0p", [128, DT * NST], F32))
        bcb = stack.enter_context(nc.sbuf_tensor("bcb", [128, 1792], F32))
        lnscr = stack.enter_context(nc.sbuf_tensor("lnscr", [128, 512], F32))
        identt = stack.enter_context(nc.sbuf_tensor("identt", [128, 128], F32))
        masktt = stack.enter_context(nc.sbuf_tensor("masktt", [128, 64], F32))
        halo = stack.enter_context(nc.sbuf_tensor("halo", [128, NCT * (KCONV - 1)], F32))
        params = stack.enter_context(nc.sbuf_tensor("params", [128, 192], F32))
        convwp = stack.enter_context(nc.sbuf_tensor("convwp", [128, NCT * KCONV], F32))
        convbp = stack.enter_context(nc.sbuf_tensor("convbp", [128, NCT], F32))
        dskipp = stack.enter_context(nc.sbuf_tensor("dskipp", [128, HEADS], F32))
        ones32t = stack.enter_context(nc.sbuf_tensor("ones32t", [32, 128], F32))
        i32t = stack.enter_context(nc.sbuf_tensor("i32t", [32, 32], F32))
        iotac = stack.enter_context(nc.sbuf_tensor("iotac", [128, 2], F32))
        iotabc = stack.enter_context(nc.sbuf_tensor("iotabc", [128, 256], F32))
        routerp = stack.enter_context(nc.sbuf_tensor("routerp", [128, DT * EXPERTS], F32))

        ps = [stack.enter_context(nc.psum_tensor(f"ps{i}", [128, 512], F32))
              for i in range(8)]

        sems = {}
        for s in SEMS:
            sems[s] = stack.enter_context(nc.semaphore(s))

        p = Prog()
        psum_guard = [None] * 8

        bdt_c = lambda dt: params[:, dt:dt + 1]
        dp1_c = lambda dt: params[:, 16 + dt:17 + dt]
        sg_c = lambda f: params[:, 32 + f:33 + f]
        su_c = lambda f: params[:, 76 + f:77 + f]
        sd_c = lambda dt: params[:, 120 + dt:121 + dt]
        nw_c = lambda dt: params[:, 136 + dt:137 + dt]
        dtb_c = params[:, 152:153]
        a1n_c = params[:, 153:154]
        ones_c = params[:, 154:155]
        eps_c = params[:, 155:156]
        onesr_ap = bcb[0:1, 0:128]
        negr_ap = bcb[0:1, 1664:1792]
        st_row = lambda i, w: bcb[0:1, 128 + i * 512:128 + i * 512 + w]

        def ld(dst, src, eng="sync", deps=()):
            return p.emit(eng, lambda e, d=dst, s=src: e.dma_start(out=d, in_=s),
                          deps=deps, dma=True)

        t_bdt = ld(params[:, 0:16], hbm["bdt"][:, :])
        t_d0 = ld(params[:, 16:32], hbm["d0"][:, :])
        ld(params[:, 32:76], hbm["sg"][:, :])
        ld(params[:, 76:120], hbm["su"][:, :])
        ld(params[:, 120:136], hbm["sd"][:, :])
        ld(params[:, 136:152], hbm["nw"][:, :])
        t_dtb = ld(params[:, 152:153], hbm["dtb"][:, :])
        t_a1 = ld(params[:, 153:154], hbm["a1"][:, :])
        t_oc = ld(params[:, 154:155], hbm["onesc"][:, :])
        t_a0 = ld(a0p[:, :], hbm["a0"][:, :])
        ld(convwp[:, :], hbm["convw"][:, :])
        ld(convbp[:, :], hbm["convb"][:, :])
        ld(dskipp[:, :], hbm["dskipbc"][:, :])
        ld(ones32t[:, :], hbm["ones32"][:, :])
        ld(i32t[:, :], hbm["i32"][:, :])
        ld(iotac[:, :], hbm["iotac_h"][:, :])
        ld(iotabc[:, :], hbm["iotabc_h"][:, :])
        ld(params[:, 160:176], hbm["gnw"][:, :])
        ld(identt[:, :], hbm["ident"][:, :])
        ld(masktt[0:64, :], hbm["maskt"][:, :])
        t_or = ld(bcb[0:1, 0:128], hbm["onesr"][:, :])
        t_nr = ld(bcb[0:1, 1664:1792], hbm["negr"][:, :])
        ld(routerp[:, :].rearrange("p (t e) -> p t e", t=DT),
           hbm["router"][:, :].rearrange("(t p) e -> p t e", p=128))

        t_eps = p.emit("vector", lambda e: e.memset(eps_c, EPS), inc=True)
        t_dp1 = p.emit("vector", lambda e: e.tensor_scalar_add(
            params[:, 16:32], params[:, 16:32], 1.0), deps=[t_d0], inc=True)
        t_bdt = p.emit("vector", lambda e: e.tensor_scalar_mul(
            params[:, 0:16], params[:, 0:16], -1.0), deps=[t_bdt], inc=True)
        t_dtb = p.emit("vector", lambda e: e.tensor_scalar_mul(
            dtb_c, dtb_c, -1.0), deps=[t_dtb], inc=True)
        t_a1n = p.emit("scalar", lambda e: e.activation(
            out=a1n_c, in_=a1n_c, func=AF.Exp), deps=[t_a1], inc=True)
        t_a0e = p.emit("scalar", lambda e: e.activation(
            out=a0p[:, :], in_=a0p[:, :], func=AF.Exp), deps=[t_a0], inc=True)
        t_hz = p.emit("vector", lambda e: e.memset(hcar[:, :], 0.0), inc=True)
        t_hlz = p.emit("vector", lambda e: e.memset(halo[:, :], 0.0), inc=True)

        p.barrier()

        def ln_fm(w, src_ap, src_toks, dsthi, dstlo, dst32, rms=False,
                  wcol=None):
            scr = lnscr[:, 0:w]
            st0 = st_row(0, w)
            st1 = st_row(1, w)
            st2 = st_row(2, w)
            t_sum = None
            t_sqmm = None
            for dt in range(DT):
                t_sq = p.emit("vector", lambda e, dt=dt: e.tensor_mul(
                    scr, src_ap(dt), src_ap(dt)),
                    deps=[src_toks[dt]] + ([t_sqmm] if dt else []), inc=True)
                if not rms:
                    t_sum = p.emit("tensor", lambda e, dt=dt: e.matmul(
                        ps[6][0:1, 0:w], ones_c, src_ap(dt),
                        start=(dt == 0), stop=(dt == DT - 1)),
                        deps=[src_toks[dt], t_oc] + ([psum_guard[6]] if dt == 0 else []),
                        inc=(dt == DT - 1))
                t_sqmm = p.emit("tensor", lambda e, dt=dt: e.matmul(
                    ps[7][0:1, 0:w], ones_c, scr,
                    start=(dt == 0), stop=(dt == DT - 1)),
                    deps=[t_sq] + ([psum_guard[7]] if dt == 0 else []),
                    inc=True)
            if not rms:
                t_mean = p.emit("vector", lambda e: e.tensor_scalar_mul(
                    st0, ps[6][0:1, 0:w], 1.0 / D), deps=[t_sum], inc=True)
                psum_guard[6] = t_mean
                t_msq = p.emit("vector", lambda e: e.tensor_mul(
                    st2, st0, st0), deps=[t_mean], inc=True)
                t_var = p.emit("vector", lambda e: e.scalar_tensor_tensor(
                    st1, ps[7][0:1, 0:w], 1.0 / D, st2,
                    ALU.mult, ALU.subtract), deps=[t_msq, t_sqmm], inc=True)
            else:
                t_var = p.emit("vector", lambda e: e.tensor_scalar_mul(
                    st1, ps[7][0:1, 0:w], 1.0 / D), deps=[t_sqmm], inc=True)
            psum_guard[7] = t_var
            t_rstd = p.emit("scalar", lambda e: e.activation(
                out=st1, in_=st1, func=AF.Sqrt, bias=eps_c[0:1, :]),
                deps=[t_var, t_eps], inc=True)
            t_rstd = p.emit("vector", lambda e: e.reciprocal(st1, st1),
                            deps=[t_rstd], inc=True)
            if not rms:
                t_bm = p.emit("tensor", lambda e: e.matmul(
                    ps[5][:, 0:w], onesr_ap, st0, start=True, stop=True),
                    deps=[t_mean, t_or, psum_guard[5]], inc=True)
            t_br = p.emit("tensor", lambda e: e.matmul(
                ps[4][:, 0:w], onesr_ap, st1, start=True, stop=True),
                deps=[t_rstd, t_or, psum_guard[4]], inc=True)
            toks = [None] * DT
            prev = None
            for dt in range(DT):
                if not rms:
                    tt = p.emit("vector", lambda e, dt=dt: e.tensor_sub(
                        scr, src_ap(dt), ps[5][:, 0:w]),
                        deps=[t_bm, src_toks[dt], prev], inc=True)
                    tu = p.emit("vector", lambda e: e.tensor_mul(
                        scr, scr, ps[4][:, 0:w]), deps=[tt, t_br], inc=True)
                else:
                    tu = p.emit("vector", lambda e, dt=dt: e.scalar_tensor_tensor(
                        scr, src_ap(dt), wcol(dt), ps[4][:, 0:w],
                        ALU.mult, ALU.mult),
                        deps=[src_toks[dt], t_br, prev], inc=True)
                if dst32 is not None:
                    tu = p.emit("vector", lambda e, dt=dt: e.tensor_copy(
                        dst32(dt), scr), deps=[tu], inc=True)
                if dsthi is not None:
                    th = p.emit("vector", lambda e, dt=dt: e.tensor_copy(
                        dsthi(dt), scr), deps=[tu], inc=True)
                    tu = p.emit("vector", lambda e, dt=dt: e.tensor_tensor(
                        dstlo(dt), scr, dsthi(dt), ALU.subtract),
                        deps=[th], inc=True)
                toks[dt] = tu
                prev = tu
            psum_guard[4] = toks[DT - 1]
            if not rms:
                psum_guard[5] = toks[DT - 1]
            return toks

        def wdt_ap(dt):
            return un[0:RNK, 27904 + dt * 128: 27904 + (dt + 1) * 128]

        def wx_ap(k):
            return un[:, 29952 + k * 96: 29952 + (k + 1) * 96]

        def load_l0a_weights(deps=()):
            t1 = p.emit("sync", lambda e: e.dma_start(
                out=un[:, 29952:31488].rearrange("p (t c) -> p t c", t=DT),
                in_=hbm["wx_t"][:, :].rearrange("(t p) c -> p t c", p=128)),
                deps=deps, dma=True)
            t2 = p.emit("sync", lambda e: e.dma_start(
                out=un[0:RNK, 27904:29952], in_=hbm["wdt_t"][:, :]),
                deps=deps, dma=True)
            return [t1, t2]

        hcar_dep = {}

        def l0a_half(c0, x_toks, w_toks):
            def x_sb(dt):
                return x_fm[:, dt * HALF + c0: dt * HALF + c0 + 512]
            t_xp = None
            for k in range(DT):
                t_xp = p.emit("tensor", lambda e, k=k: e.matmul(
                    ps[7][0:96, :], wx_ap(k), x_sb(k),
                    start=(k == 0), stop=(k == DT - 1)),
                    deps=list(x_toks) + list(w_toks) + ([psum_guard[7]] if k == 0 else []),
                    inc=(k == DT - 1))
            t_xpe = p.emit("vector", lambda e: e.tensor_copy(
                xproj[0:96, :], ps[7][0:96, :]), deps=[t_xp], inc=True)
            psum_guard[7] = t_xpe

            t2_toks = [None] * DT
            dtx_toks = [None] * DT
            for dt in range(DT):
                bi = 6 if dt % 2 == 0 else 5
                mm = p.emit("tensor", lambda e, dt=dt, bi=bi: e.matmul(
                    ps[bi][:, :], wdt_ap(dt), xproj[0:RNK, :],
                    start=True, stop=True),
                    deps=[t_xpe, psum_guard[bi]] + list(w_toks), inc=True)
                sp = p.emit("scalar", lambda e, dt=dt, bi=bi: e.activation(
                    out=un[:, dt * 512: dt * 512 + 512], in_=ps[bi][:, :],
                    func=AF.Sigmoid, scale=-1.0, bias=bdt_c(dt)),
                    deps=[mm, t_bdt], inc=True)
                psum_guard[bi] = sp
                sp = p.emit("scalar", lambda e, dt=dt: e.activation(
                    out=un[:, dt * 512: dt * 512 + 512],
                    in_=un[:, dt * 512: dt * 512 + 512], func=AF.Ln),
                    deps=[sp], inc=True)
                t2_toks[dt] = sp
                dtx_toks[dt] = p.emit("vector", lambda e, dt=dt: e.tensor_mul(
                    un[:, 8192 + dt * 512: 8192 + dt * 512 + 512],
                    un[:, dt * 512: dt * 512 + 512], x_sb(dt)),
                    deps=[sp] + list(x_toks), inc=True)

            y_toks = [None] * DT
            for dt in range(DT):
                y_toks[dt] = p.emit("vector", lambda e, dt=dt: e.tensor_scalar(
                    x_sb(dt), x_sb(dt), dp1_c(dt), None, ALU.mult),
                    deps=[t_dp1, dtx_toks[dt]], inc=True)

            scan_tok = [None, None]
            mul_tok = [None, None]
            add_tok = [None, None]
            bcb_guard = [None, None]
            for n in range(NST):
                npar = n % 2
                bbank, cbank = npar, 2 + npar
                boff = 20480 + npar * 1024
                tb = p.emit("sync", lambda e, n=n, bo=boff: e.dma_start(
                    out=un[0:1, bo:bo + 512], in_=xproj[64 + n:65 + n, :]),
                    deps=[t_xpe, bcb_guard[npar]], dma=True)
                tc = p.emit("sync", lambda e, n=n, bo=boff: e.dma_start(
                    out=un[0:1, bo + 512:bo + 1024], in_=xproj[80 + n:81 + n, :]),
                    deps=[t_xpe, bcb_guard[npar]], dma=True)
                bb = p.emit("tensor", lambda e, bo=boff, bbank=bbank: e.matmul(
                    ps[bbank][:, :], negr_ap, un[0:1, bo:bo + 512],
                    start=True, stop=True),
                    deps=[tb, t_nr, psum_guard[bbank]], inc=True)
                cb = p.emit("tensor", lambda e, bo=boff, cbank=cbank: e.matmul(
                    ps[cbank][:, :], onesr_ap, un[0:1, bo + 512:bo + 1024],
                    start=True, stop=True),
                    deps=[tc, t_or, psum_guard[cbank]], inc=True)
                bcb_guard[npar] = cb
                for dt in range(DT):
                    col = dt * NST + n
                    it = n * DT + dt
                    par = it % 2
                    o_da = 16384 + par * 2048
                    o_db = o_da + 512
                    o_h = o_da + 1024
                    o_p = o_da + 1536
                    da = p.emit("scalar", lambda e, dt=dt, col=col, o_da=o_da: e.activation(
                        out=un[:, o_da:o_da + 512],
                        in_=un[:, dt * 512: dt * 512 + 512],
                        func=AF.Exp, scale=a0p[:, col:col + 1]),
                        deps=[t2_toks[dt], t_a0e, scan_tok[par]], inc=True)
                    db = p.emit("vector", lambda e, dt=dt, o_db=o_db, bbank=bbank: e.scalar_tensor_tensor(
                        un[:, o_db:o_db + 512],
                        un[:, 8192 + dt * 512: 8192 + dt * 512 + 512],
                        1.0, ps[bbank][:, :], ALU.mult, ALU.mult),
                        deps=[dtx_toks[dt], bb, scan_tok[par]], inc=True)
                    psum_guard[bbank] = db
                    h = p.emit("vector", lambda e, col=col, o_da=o_da, o_db=o_db, o_h=o_h:
                        e.tensor_tensor_scan(
                            un[:, o_h:o_h + 512], un[:, o_da:o_da + 512],
                            un[:, o_db:o_db + 512], hcar[:, col:col + 1],
                            ALU.mult, ALU.add),
                        deps=[da, db, hcar_dep[col], mul_tok[par]], inc=True)
                    scan_tok[par] = h
                    hc = p.emit("vector", lambda e, col=col, o_h=o_h: e.tensor_copy(
                        hcar[:, col:col + 1], un[:, o_h + 511:o_h + 512]),
                        deps=[h], inc=True)
                    hcar_dep[col] = hc
                    m1 = p.emit("vector", lambda e, o_h=o_h, o_p=o_p, cbank=cbank: e.tensor_mul(
                        un[:, o_p:o_p + 512], un[:, o_h:o_h + 512], ps[cbank][:, :]),
                        deps=[h, cb, add_tok[par]], inc=True)
                    psum_guard[cbank] = m1
                    mul_tok[par] = m1
                    at_ = p.emit("gpsimd", lambda e, dt=dt, o_p=o_p: e.tensor_add(
                        x_sb(dt), x_sb(dt), un[:, o_p:o_p + 512]),
                        deps=[m1, y_toks[dt]], inc=True)
                    y_toks[dt] = at_
                    add_tok[par] = at_
            return y_toks

        def x1hi(dt, c0, w=512):
            return xfb[:, 2 * (dt * HALF + c0): 2 * (dt * HALF + c0) + w]

        def x1lo(dt, c0, w=512):
            return xfb[:, 2 * (dt * HALF + c0) + 512: 2 * (dt * HALF + c0) + 512 + w]

        def l0ffn_chunk(c0, x_toks):
            whb = [unb[:, 45056 + i * 2048: 45056 + (i + 1) * 2048] for i in range(4)]
            wu8 = [unu[:, 106496 + i * 2048: 106496 + (i + 1) * 2048] for i in range(2)]
            slot_tok = [None, None, None, None]
            u8_tok = [None, None]
            gwork = un[:, 27648:28160]
            hwork = un[:, 28160:28672]
            hhi = lambda f: unb[:, f * 512:(f + 1) * 512]
            hlo = lambda f: unb[:, 22528 + f * 512:22528 + (f + 1) * 512]
            h_toks = [None] * HT
            conv_engs = ("scalar", "vector", "gpsimd")

            def conv_op(ceng, dst, srcu8):
                if ceng == "scalar":
                    return lambda e: e.activation(out=dst, in_=srcu8,
                                                  func=AF.Copy, bias=-128.0)
                return lambda e: e.tensor_scalar_add(dst, srcu8, -128.0)

            tg = None
            for f in range(HT):
                for gi, wsrc in enumerate((hbm["wqg_t"], hbm["wqu_t"])):
                    slot = (2 * f + gi) % 4
                    u8s = gi
                    tl = p.emit("sync", lambda e, f=f, wsrc=wsrc, u8s=u8s: e.dma_start(
                        out=wu8[u8s].rearrange("p (k c) -> p k c", k=DT),
                        in_=wsrc[:, f * 128:(f + 1) * 128].rearrange(
                            "(k p) c -> p k c", p=128)),
                        deps=[u8_tok[u8s]], dma=True)
                    ceng = conv_engs[(2 * f + gi) % 3]
                    tcv = p.emit(ceng, conv_op(ceng, whb[slot], wu8[u8s]),
                                 deps=[tl, slot_tok[slot]], inc=True)
                    u8_tok[u8s] = tcv
                    bank = 6 + gi
                    mm = None
                    for k in range(DT):
                        for hl, xap in ((0, x1hi), (1, x1lo)):
                            mm = p.emit("tensor", lambda e, k=k, hl=hl, xap=xap,
                                        slot=slot, bank=bank: e.matmul(
                                ps[bank][:, :], whb[slot][:, k * 128:(k + 1) * 128],
                                xap(k, c0), start=(k == 0 and hl == 0),
                                stop=(k == DT - 1 and hl == 1)),
                                deps=[tcv, psum_guard[bank]]
                                + (list(x_toks) if (k == 0 and hl == 0) else []),
                                inc=(k == DT - 1 and hl == 1))
                    slot_tok[slot] = mm
                    prev_h = h_toks[f - 1] if f else None
                    if gi == 0:
                        tg = p.emit("scalar", lambda e, f=f: e.activation(
                            out=gwork, in_=ps[6][:, :], func=AF.Silu, scale=sg_c(f)),
                            deps=[mm, prev_h], inc=True)
                        psum_guard[6] = tg
                    else:
                        tu = p.emit("vector", lambda e, f=f: e.tensor_scalar(
                            hwork, ps[7][:, :], su_c(f), None, ALU.mult),
                            deps=[mm, prev_h], inc=True)
                        psum_guard[7] = tu
                        tm = p.emit("vector", lambda e: e.tensor_mul(
                            gwork, gwork, hwork), deps=[tu, tg], inc=True)
                        th = p.emit("vector", lambda e, f=f: e.tensor_copy(
                            hhi(f), gwork), deps=[tm], inc=True)
                        h_toks[f] = p.emit("vector", lambda e, f=f: e.tensor_tensor(
                            hlo(f), gwork, hhi(f), ALU.subtract), deps=[th], inc=True)

            dwb = [unb[:, 45056 + i * 5632: 45056 + (i + 1) * 5632] for i in range(2)]
            du8 = [unu[:, 116736 + i * 5632: 116736 + (i + 1) * 5632] for i in range(2)]
            dwork = [un[:, 28160:28672], un[:, 28672:29184]]
            dslot_tok = [slot_tok[0] or slot_tok[2], slot_tok[1] or slot_tok[3]]
            dslot_tok = [slot_tok[2], slot_tok[3]]
            du8_tok = [u8_tok[0], u8_tok[1]]
            dn_toks = [None] * DT
            for dt in range(DT):
                slot = dt % 2
                tl = p.emit("sync", lambda e, dt=dt, slot=slot: e.dma_start(
                    out=du8[slot].rearrange("p (k c) -> p k c", k=HT),
                    in_=hbm["wqd_t"][:, dt * 128:(dt + 1) * 128].rearrange(
                        "(k p) c -> p k c", p=128)),
                    deps=[du8_tok[slot]], dma=True)
                ceng = conv_engs[dt % 3]
                gd = list(slot_tok) + [h_toks[HT - 1]] \
                    if dt < 2 else [dslot_tok[slot]]
                tcv = p.emit(ceng, conv_op(ceng, dwb[slot], du8[slot]),
                             deps=[tl] + gd, inc=True)
                du8_tok[slot] = tcv
                bank = 6 + slot
                mm = None
                for k in range(HT):
                    for hl, hap in ((0, hhi), (1, hlo)):
                        mm = p.emit("tensor", lambda e, k=k, hl=hl, hap=hap,
                                    slot=slot, bank=bank: e.matmul(
                            ps[bank][:, :], dwb[slot][:, k * 128:(k + 1) * 128],
                            hap(k), start=(k == 0 and hl == 0),
                            stop=(k == HT - 1 and hl == 1)),
                            deps=[tcv, psum_guard[bank], h_toks[k]],
                            inc=(k == HT - 1 and hl == 1))
                dslot_tok[slot] = mm
                tv = p.emit("vector", lambda e, dt=dt, bank=bank, slot=slot: e.scalar_tensor_tensor(
                    dwork[slot], ps[bank][:, :], sd_c(dt), x1hi(dt, c0),
                    ALU.mult, ALU.add),
                    deps=[mm] + ([dn_toks[dt - 2]] if dt >= 2 else [h_toks[HT - 1]]),
                    inc=True)
                psum_guard[bank] = tv
                dn_toks[dt] = p.emit("vector", lambda e, dt=dt, slot=slot: e.tensor_add(
                    x_fm[:, dt * HALF + c0: dt * HALF + c0 + 512],
                    dwork[slot], x1lo(dt, c0)), deps=[tv], inc=True)
            return dn_toks

        A_CONV, A_Z, A_DT, A_ACS = 0, 5120, 9216, 9472
        A_XBCT, A_XH, A_G, A_CBT = 9728, 9992, 12040, 14088
        A_XT, A_Y, A_ZT, A_MISC = 14152, 16200, 18248, 20296
        A_Y3, A_S, A_WIN = 20992, 25088, 29184
        M_AT = A_MISC
        M_EXPA = A_MISC + 32
        M_DECT = A_MISC + 64
        M_DECBC = A_MISC + 96
        M_ALB = A_MISC + 128
        M_DTT = A_MISC + 160
        M_BTM = A_MISC + 192
        M_ONES = A_MISC + 448
        M_RSTD = A_MISC + 512
        gnw_c = lambda dt: params[:, 160 + dt:161 + dt]

        def ap(off, w, pn=128):
            return un[0:pn, off:off + w]

        def l1a_quarter(phase, q, xq_toks, s_toks, nchunks=4):
            own = (phase == 1)
            qc0 = q * 256
            xsl = lambda dt: x_fm[:, dt * HALF + qc0: dt * HALF + qc0 + 256]
            xn_off = lambda k: (A_XT + k * 256) if k < 8 else (A_Y + (k - 8) * 256)
            xn_ap = lambda k: un[:, xn_off(k):xn_off(k) + 256]
            xn_toks = ln_fm(256, xsl, xq_toks, None, None, xn_ap,
                            rms=True, wcol=nw_c)
            t_ones = p.emit("vector", lambda e: e.memset(
                ap(M_ONES, 64, 32), 1.0), inc=True)

            conv_toks = [None] * NCT
            z_toks = [None] * NZT
            wslot = un[:, A_WIN:A_WIN + 2048]
            prev_mm = [None]
            t_stage = None
            dt_tok = [None, None]
            mrange = list(range(NZT, 37)) + (list(range(NZT)) if own else [])
            for mi, m in enumerate(mrange):
                bank = 6 + (mi % 2)
                mw = 32 if m == 36 else 128
                t_w = p.emit("sync", lambda e, m=m, mw=mw: e.dma_start(
                    out=wslot.rearrange("p (k c) -> p k c", k=DT)[:, :, 0:mw],
                    in_=hbm["win"][:, m * 128:m * 128 + mw].rearrange(
                        "(k p) c -> p k c", p=128)),
                    deps=[prev_mm[0]], dma=True)
                mm = None
                for k in range(DT):
                    mm = p.emit("tensor", lambda e, k=k, mw=mw, bank=bank: e.matmul(
                        ps[bank][0:mw, 0:256], wslot[:, k * 128:k * 128 + mw],
                        xn_ap(k), start=(k == 0), stop=(k == DT - 1)),
                        deps=[t_w, psum_guard[bank], xn_toks[k]],
                        inc=(k == DT - 1))
                prev_mm[0] = mm
                if m < NZT:
                    z_toks[m] = p.emit("vector", lambda e, m=m, bank=bank: e.tensor_copy(
                        ap(A_Z + m * 256, 256), ps[bank][:, 0:256]),
                        deps=[mm], inc=True)
                    psum_guard[bank] = z_toks[m]
                elif m < 36:
                    ct = m - NZT
                    tse = p.emit("vector", lambda e, bank=bank: e.tensor_copy(
                        un[:, A_XBCT + 3:A_XBCT + 259], ps[bank][:, 0:256]),
                        deps=[mm, t_stage], inc=True)
                    psum_guard[bank] = tse
                    th = p.emit("vector", lambda e, ct=ct: e.tensor_copy(
                        un[:, A_XBCT:A_XBCT + 3], halo[:, ct * 3:ct * 3 + 3]),
                        deps=[tse], inc=True)
                    cv = p.emit("vector", lambda e, ct=ct: e.tensor_scalar(
                        ap(A_CONV + ct * 256, 256), un[:, A_XBCT:A_XBCT + 256],
                        convwp[:, ct * KCONV:ct * KCONV + 1], convbp[:, ct:ct + 1],
                        ALU.mult, ALU.add), deps=[th], inc=True)
                    for kk in range(1, KCONV):
                        cv = p.emit("vector", lambda e, ct=ct, kk=kk: e.scalar_tensor_tensor(
                            ap(A_CONV + ct * 256, 256),
                            un[:, A_XBCT + kk:A_XBCT + kk + 256],
                            convwp[:, ct * KCONV + kk:ct * KCONV + kk + 1],
                            ap(A_CONV + ct * 256, 256), ALU.mult, ALU.add),
                            deps=[cv], inc=True)
                    hn = p.emit("gpsimd", lambda e, ct=ct: e.tensor_copy(
                        halo[:, ct * 3:ct * 3 + 3],
                        un[:, A_XBCT + 256:A_XBCT + 259]), deps=[cv], inc=True)
                    t_stage = hn
                    conv_toks[ct] = p.emit("scalar", lambda e, ct=ct: e.activation(
                        out=ap(A_CONV + ct * 256, 256),
                        in_=ap(A_CONV + ct * 256, 256),
                        func=AF.Silu), deps=[cv], inc=True)
                else:
                    t1 = p.emit("scalar", lambda e, bank=bank: e.activation(
                        out=ap(A_DT, 256, 32), in_=ps[bank][0:32, 0:256],
                        func=AF.Sigmoid, scale=-1.0, bias=dtb_c[0:32, :]),
                        deps=[mm, t_dtb], inc=True)
                    psum_guard[bank] = t1
                    t1 = p.emit("scalar", lambda e: e.activation(
                        out=ap(A_DT, 256, 32), in_=ap(A_DT, 256, 32),
                        func=AF.Ln), deps=[t1], inc=True)
                    t2a = p.emit("vector", lambda e: e.tensor_scalar(
                        ap(A_ACS, 256, 32), ap(A_DT, 256, 32),
                        a1n_c[0:32, :], None, ALU.mult),
                        deps=[t1, t_a1n], inc=True)
                    t2b = p.emit("vector", lambda e: e.tensor_scalar_mul(
                        ap(A_DT, 256, 32), ap(A_DT, 256, 32), -1.0),
                        deps=[t2a], inc=True)
                    cs = None
                    for c in range(4):
                        cs = p.emit("vector", lambda e, c=c: e.tensor_tensor_scan(
                            ap(A_ACS + c * 64, 64, 32), ap(M_ONES, 64, 32),
                            ap(A_ACS + c * 64, 64, 32), 0.0, ALU.mult, ALU.add),
                            deps=[t2a, t_ones, cs], inc=True)
                    dt_tok = [t2b, cs]

            s_dep = list(s_toks)
            cg = {"xt": [prev_mm[0]], "y": [prev_mm[0]], "g": [],
                  "xh": [], "zt": [], "decbc": [], "y3last": None}
            y3_toks = [None] * DT
            for c in range(nchunks):
                if c:
                    p.barrier()
                acs_c = ap(A_ACS + c * 64, 64, 32)
                ta = p.emit("tensor", lambda e, acs_c=acs_c: e.transpose(
                    ps[6][0:64, 0:32], acs_c, i32t[:, :]),
                    deps=[dt_tok[1], psum_guard[6]], inc=True)
                tb = p.emit("tensor", lambda e, c=c: e.transpose(
                    ps[6][0:64, 64:96], ap(A_DT + c * 64, 64, 32), i32t[:, :]),
                    deps=[dt_tok[0]], inc=True)
                te1 = p.emit("vector", lambda e: e.tensor_copy(
                    ap(M_AT, 32, 64), ps[6][0:64, 0:32]),
                    deps=[ta] + cg["g"], inc=True)
                te2 = p.emit("vector", lambda e: e.tensor_copy(
                    ap(M_DTT, 32, 64), ps[6][0:64, 64:96]), deps=[tb], inc=True)
                psum_guard[6] = te2
                if own:
                    tex = p.emit("scalar", lambda e: e.activation(
                        out=ap(M_EXPA, 32, 64), in_=ap(M_AT, 32, 64),
                        func=AF.Exp), deps=[te1], inc=True)
                tx2 = p.emit("vector", lambda e, acs_c=acs_c: e.tensor_scalar(
                    ap(M_ALB, 32, 32), i32t[:, :], acs_c[:, 63:64],
                    None, ALU.mult), deps=[dt_tok[1], te2], inc=True)
                tdm = p.emit("tensor", lambda e: e.matmul(
                    ps[6][:, 0:32], ones32t[:, :], ap(M_ALB, 32, 32),
                    start=True, stop=True), deps=[tx2, psum_guard[6]], inc=True)
                td1 = p.emit("vector", lambda e: e.tensor_sub(
                    ap(M_DECT, 32, 64), ps[6][0:64, 0:32], ap(M_AT, 32, 64)),
                    deps=[tdm, te1], inc=True)
                td2 = p.emit("scalar", lambda e: e.activation(
                    out=ap(M_DECT, 32, 64), in_=ap(M_DECT, 32, 64),
                    func=AF.Exp), deps=[td1], inc=True)
                tdb = p.emit("scalar", lambda e: e.activation(
                    out=ap(M_DECBC, 32), in_=ps[6][:, 0:32], func=AF.Exp),
                    deps=[tdm] + cg["decbc"], inc=True)
                psum_guard[6] = tdb
                xt_toks = []
                for dt in range(DT):
                    bi = dt // 4
                    tt = p.emit("tensor", lambda e, dt=dt, c=c, bi=bi: e.transpose(
                        ps[bi][0:64, (dt % 4) * 128:(dt % 4) * 128 + 128],
                        un[:, A_CONV + dt * 256 + c * 64:
                           A_CONV + dt * 256 + c * 64 + 64],
                        identt[:, :]),
                        deps=[conv_toks[dt]]
                        + ([psum_guard[bi]] if dt % 4 == 0 else []),
                        inc=(dt % 4 == 3))
                    if dt % 4 == 3:
                        ev = p.emit("vector", lambda e, bi=bi: e.tensor_copy(
                            ap(A_XT + bi * 512, 512, 64), ps[bi][0:64, 0:512]),
                            deps=[tt] + cg["xt"], inc=True)
                        psum_guard[bi] = ev
                        xt_toks.append(ev)
                tbt = None
                for i in range(2):
                    tbt = p.emit("tensor", lambda e, i=i, c=c: e.transpose(
                        ps[4][0:64, i * 128:(i + 1) * 128],
                        un[:, A_CONV + (16 + i) * 256 + c * 64:
                           A_CONV + (16 + i) * 256 + c * 64 + 64],
                        identt[:, :]),
                        deps=[conv_toks[16 + i]]
                        + ([psum_guard[4]] if i == 0 else []),
                        inc=(i == 1))
                tbe = p.emit("vector", lambda e: e.tensor_copy(
                    ap(M_BTM, 256, 64), ps[4][0:64, 0:256]),
                    deps=[tbt] + s_dep, inc=True)
                psum_guard[4] = tbe
                ty0 = None
                if own:
                    ty0 = p.emit("vector", lambda e: e.tensor_tensor(
                        ap(A_Y, 2048, 64).rearrange("p (h w) -> p h w", h=HEADS),
                        ap(A_XT, 2048, 64).rearrange("p (h w) -> p h w", h=HEADS),
                        dskipp[0:64, :, None].broadcast_to([64, HEADS, HDIM]),
                        ALU.mult), deps=xt_toks + cg["y"], inc=True)
                    tcb = None
                    for i in range(2):
                        tcb = p.emit("tensor", lambda e, i=i, c=c: e.matmul(
                            ps[4][0:64, 256:320],
                            un[:, A_CONV + (16 + i) * 256 + c * 64:
                               A_CONV + (16 + i) * 256 + c * 64 + 64],
                            un[:, A_CONV + (18 + i) * 256 + c * 64:
                               A_CONV + (18 + i) * 256 + c * 64 + 64],
                            start=(i == 0), stop=(i == 1)),
                            deps=[conv_toks[16 + i], conv_toks[18 + i], tbe],
                            inc=(i == 1))
                    tcbe = p.emit("vector", lambda e: e.tensor_mul(
                        ap(A_CBT, 64, 64), ps[4][0:64, 256:320],
                        masktt[0:64, 0:64]), deps=[tcb], inc=True)
                    psum_guard[4] = tcbe
                    txh = p.emit("vector", lambda e, acs_c=acs_c: e.tensor_tensor(
                        ap(A_XH, 2048, 32).rearrange("p (h l) -> p h l", h=HEADS),
                        i32t[:, :, None].broadcast_to([32, 32, 64]),
                        acs_c[:, None, :].broadcast_to([32, 32, 64]),
                        ALU.mult), deps=[dt_tok[1]] + cg["xh"], inc=True)
                    g_toks = []
                    for j in range(4):
                        tgm = p.emit("tensor", lambda e, j=j: e.matmul(
                            ps[j][0:64, 0:512], ones32t[:, 0:64],
                            ap(A_XH + j * 512, 512, 32),
                            start=True, stop=True),
                            deps=[txh, psum_guard[j]], inc=True)
                        tgs = p.emit("vector", lambda e, j=j: e.tensor_tensor(
                            ap(A_G + j * 512, 512, 64).rearrange(
                                "p (h l) -> p h l", h=8),
                            ps[j][0:64, 0:512].rearrange("p (h l) -> p h l", h=8),
                            ap(M_AT, 32, 64)[:, j * 8:(j + 1) * 8, None]
                            .broadcast_to([64, 8, 64]),
                            ALU.subtract), deps=[tgm, te1] + cg["g"], inc=True)
                        psum_guard[j] = tgs
                        g_toks.append(tgs)
                    tgc = p.emit("vector", lambda e: e.tensor_scalar_min(
                        ap(A_G, 2048, 64), ap(A_G, 2048, 64), 0.0),
                        deps=g_toks, inc=True)
                    tge = p.emit("scalar", lambda e: e.activation(
                        out=ap(A_G, 2048, 64), in_=ap(A_G, 2048, 64),
                        func=AF.Exp), deps=[tgc], inc=True)
                    tgk = p.emit("vector", lambda e: e.tensor_tensor(
                        ap(A_G, 2048, 64).rearrange("p (h l) -> p h l", h=HEADS),
                        ap(A_G, 2048, 64).rearrange("p (h l) -> p h l", h=HEADS),
                        masktt[0:64, None, 0:64].broadcast_to([64, HEADS, 64]),
                        ALU.mult), deps=[tge], inc=True)
                    tgk = p.emit("vector", lambda e: e.tensor_tensor(
                        ap(A_G, 2048, 64).rearrange("p (h l) -> p h l", h=HEADS),
                        ap(A_G, 2048, 64).rearrange("p (h l) -> p h l", h=HEADS),
                        ap(A_CBT, 64, 64)[:, None, :].broadcast_to([64, HEADS, 64]),
                        ALU.mult), deps=[tgk, tcbe], inc=True)
                txt = p.emit("vector", lambda e: e.tensor_tensor(
                    ap(A_XT, 2048, 64).rearrange("p (h w) -> p h w", h=HEADS),
                    ap(A_XT, 2048, 64).rearrange("p (h w) -> p h w", h=HEADS),
                    ap(M_DTT, 32, 64)[:, :, None].broadcast_to([64, HEADS, HDIM]),
                    ALU.mult),
                    deps=xt_toks + [te2] + ([ty0] if own else []), inc=True)
                yd_toks = [None] * 4
                yo_toks = [None] * 4
                y_toks_c = []
                if own:
                    for h in range(HEADS):
                        bi = h // 8
                        off = (h % 8) * 64
                        td = p.emit("tensor", lambda e, h=h, bi=bi, off=off: e.matmul(
                            ps[bi][0:64, off:off + 64],
                            ap(A_G + h * 64, 64, 64),
                            ap(A_XT + h * 64, 64, 64),
                            start=True, stop=True),
                            deps=[tgk, txt]
                            + ([psum_guard[bi]] if h % 8 == 0 else []),
                            inc=(h % 8 == 7))
                        if h % 8 == 7:
                            yd_toks[bi] = td
                        to = None
                        for i in range(2):
                            to = p.emit("tensor", lambda e, h=h, bi=bi, off=off,
                                        i=i, c=c: e.matmul(
                                ps[4 + bi][0:64, off:off + 64],
                                un[:, A_CONV + (18 + i) * 256 + c * 64:
                                   A_CONV + (18 + i) * 256 + c * 64 + 64],
                                un[:, A_S + i * 2048 + h * 64:
                                   A_S + i * 2048 + h * 64 + 64],
                                start=(i == 0), stop=(i == 1)),
                                deps=[conv_toks[18 + i]] + s_dep
                                + ([psum_guard[4 + bi]] if h % 8 == 0 and i == 0 else []),
                                inc=(h % 8 == 7 and i == 1))
                        if h % 8 == 7:
                            yo_toks[bi] = to
                    prev_t1r = None
                    for bi in range(4):
                        t1 = p.emit("vector", lambda e, bi=bi: e.tensor_tensor(
                            ap(A_XH, 512, 64).rearrange("p (h l) -> p h l", h=8),
                            ps[4 + bi][0:64, 0:512].rearrange("p (h l) -> p h l", h=8),
                            ap(M_EXPA, 32, 64)[:, bi * 8:(bi + 1) * 8, None]
                            .broadcast_to([64, 8, 64]), ALU.mult),
                            deps=[yo_toks[bi], tex, prev_t1r, txh], inc=True)
                        psum_guard[4 + bi] = t1
                        t2_ = p.emit("gpsimd", lambda e, bi=bi: e.tensor_add(
                            ap(A_Y + bi * 512, 512, 64),
                            ap(A_Y + bi * 512, 512, 64),
                            ap(A_XH, 512, 64)), deps=[t1, ty0], inc=True)
                        prev_t1r = t2_
                        t3 = p.emit("vector", lambda e, bi=bi: e.tensor_add(
                            ap(A_Y + bi * 512, 512, 64),
                            ap(A_Y + bi * 512, 512, 64),
                            ps[bi][0:64, 0:512]), deps=[t2_, yd_toks[bi]], inc=True)
                        psum_guard[bi] = t3
                        y_toks_c.append(t3)
                    cg["xh"] = [prev_t1r, y_toks_c[-1]]
                txtd = p.emit("vector", lambda e: e.tensor_tensor(
                    ap(A_XT, 2048, 64).rearrange("p (h w) -> p h w", h=HEADS),
                    ap(A_XT, 2048, 64).rearrange("p (h w) -> p h w", h=HEADS),
                    ap(M_DECT, 32, 64)[:, :, None].broadcast_to([64, HEADS, HDIM]),
                    ALU.mult),
                    deps=[txt, td2] + (yd_toks if own else []), inc=True)
                new_s = []
                tsm_last = None
                for nt2 in range(2):
                    for j in range(4):
                        bi = 6 + ((nt2 * 4 + j) % 2)
                        tsm = p.emit("tensor", lambda e, nt2=nt2, j=j, bi=bi: e.matmul(
                            ps[bi][:, 0:512], ap(M_BTM + nt2 * 128, 128, 64),
                            ap(A_XT + j * 512, 512, 64),
                            start=True, stop=True),
                            deps=[txtd, tbe, psum_guard[bi]], inc=True)
                        tsm_last = tsm
                        sslice = un[:, A_S + nt2 * 2048 + j * 512:
                                    A_S + nt2 * 2048 + j * 512 + 512]
                        tu1 = p.emit("vector", lambda e, sslice=sslice, j=j: e.tensor_tensor(
                            sslice.rearrange("p (h w) -> p h w", h=8),
                            sslice.rearrange("p (h w) -> p h w", h=8),
                            ap(M_DECBC, 32)[:, j * 8:(j + 1) * 8, None]
                            .broadcast_to([128, 8, HDIM]), ALU.mult),
                            deps=[tsm, tdb] + s_dep
                            + (yo_toks if own else []), inc=True)
                        tu2 = p.emit("vector", lambda e, sslice=sslice, bi=bi: e.tensor_add(
                            sslice, sslice, ps[bi][:, 0:512]),
                            deps=[tu1], inc=True)
                        psum_guard[bi] = tu2
                        new_s.append(tu2)
                s_dep = new_s
                cg["xt"] = [tsm_last]
                cg["decbc"] = new_s[-1:]
                if upto == "l1a_sdump" and q == 0 and phase == 0:
                    for j2 in range(4):
                        tdmp = p.emit("gpsimd", lambda e, j2=j2, c=c: e.dma_start(
                            out=hbm["dbg"][:, :].rearrange(
                                "(t p) l -> p t l", p=128)[:, c * 4 + j2, :],
                            in_=un[:, A_S + j2 * 1024:A_S + (j2 + 1) * 1024]),
                            deps=new_s, dma=True)
                        s_dep = s_dep + [tdmp]
                if upto in ("l1a_sdump", "l1a_pre1c1") and q == 0 and phase == 0 \
                        and c == 0:
                    d1 = p.emit("gpsimd", lambda e: e.dma_start(
                        out=hbm["out"][0:64, :], in_=ap(A_XT, 2048, 64)),
                        deps=[txtd, tsm_last], dma=True)
                    d2 = p.emit("gpsimd", lambda e: e.dma_start(
                        out=hbm["out"][64:128, 0:256], in_=ap(M_BTM, 256, 64)),
                        deps=[tbe, tsm_last], dma=True)
                    d3 = p.emit("gpsimd", lambda e: e.dma_start(
                        out=hbm["out"][64:128, 256:288], in_=ap(M_DECT, 32, 64)),
                        deps=[td2, txtd], dma=True)
                    d4 = p.emit("gpsimd", lambda e: e.dma_start(
                        out=hbm["out"][128:256, 0:32], in_=ap(M_DECBC, 32)),
                        deps=[tdb] + new_s, dma=True)
                    d5 = p.emit("gpsimd", lambda e: e.dma_start(
                        out=hbm["out"][256:288, 0:64], in_=ap(A_ACS, 64, 32)),
                        deps=[td2], dma=True)
                    d6 = p.emit("gpsimd", lambda e: e.dma_start(
                        out=hbm["out"][288:352, 0:32], in_=ap(M_AT, 32, 64)),
                        deps=[td2], dma=True)
                    d7 = p.emit("gpsimd", lambda e: e.dma_start(
                        out=hbm["out"][352:384, 0:32], in_=ap(M_ALB, 32, 32)),
                        deps=[tdm], dma=True)
                    p.emit("gpsimd", lambda e: e.nop(),
                           deps=[d1, d2, d3, d4, d5, d6, d7])
                    s_dep = s_dep + [d1, d2, d3, d4, d5, d6, d7]
                if own:
                    zt_toks = []
                    for dt in range(DT):
                        gidx = dt // 4
                        bi = 4 + gidx % 2
                        tt = p.emit("tensor", lambda e, dt=dt, c=c, bi=bi: e.transpose(
                            ps[bi][0:64, (dt % 4) * 128:(dt % 4) * 128 + 128],
                            un[:, A_Z + dt * 256 + c * 64:
                               A_Z + dt * 256 + c * 64 + 64],
                            identt[:, :]),
                            deps=[z_toks[dt]]
                            + ([psum_guard[bi]] if dt % 4 == 0 else []),
                            inc=(dt % 4 == 3))
                        if dt % 4 == 3:
                            ev = p.emit("scalar", lambda e, gidx=gidx, bi=bi: e.activation(
                                out=ap(A_ZT + gidx * 512, 512, 64),
                                in_=ps[bi][0:64, 0:512], func=AF.Silu),
                                deps=[tt] + cg["zt"], inc=True)
                            psum_guard[bi] = ev
                            zt_toks.append(ev)
                    tyz = p.emit("vector", lambda e: e.tensor_mul(
                        ap(A_Y, 2048, 64), ap(A_Y, 2048, 64),
                        ap(A_ZT, 2048, 64)),
                        deps=zt_toks + y_toks_c, inc=True)
                    cg["zt"] = [tyz]
                    tss = p.emit("scalar", lambda e: e.activation(
                        out=ap(A_G, 2048, 64), in_=ap(A_Y, 2048, 64),
                        func=AF.Square, accum_out=ap(M_RSTD, 1, 64)),
                        deps=[tyz], inc=True)
                    cg["g"] = [tss]
                    trs = p.emit("scalar", lambda e: e.activation(
                        out=ap(M_RSTD, 1, 64), in_=ap(M_RSTD, 1, 64),
                        func=AF.Sqrt, scale=1.0 / D, bias=eps_c[0:64, :]),
                        deps=[tss, t_eps], inc=True)
                    trs = p.emit("vector", lambda e: e.reciprocal(
                        ap(M_RSTD, 1, 64), ap(M_RSTD, 1, 64)),
                        deps=[trs], inc=True)
                    tyn = p.emit("vector", lambda e: e.tensor_scalar(
                        ap(A_Y, 2048, 64), ap(A_Y, 2048, 64),
                        ap(M_RSTD, 1, 64), None, ALU.mult),
                        deps=[trs], inc=True)
                    lastev = None
                    for dt in range(DT):
                        bi = dt // 4
                        tt = p.emit("tensor", lambda e, dt=dt, bi=bi: e.transpose(
                            ps[bi][:, (dt % 4) * 64:(dt % 4) * 64 + 64],
                            ap(A_Y + dt * 128, 128, 64), identt[0:64, 0:64]),
                            deps=[tyn]
                            + ([psum_guard[bi]] if dt % 4 == 0 else []),
                            inc=(dt % 4 == 3))
                        if dt % 4 == 3:
                            for d2 in range(dt - 3, dt + 1):
                                lastev = p.emit("vector", lambda e, d2=d2, bi=bi, c=c:
                                    e.tensor_scalar(
                                        un[:, A_Y3 + d2 * 256 + c * 64:
                                           A_Y3 + d2 * 256 + c * 64 + 64],
                                        ps[bi][:, (d2 % 4) * 64:(d2 % 4) * 64 + 64],
                                        gnw_c(d2), None, ALU.mult),
                                    deps=[tt], inc=True)
                                y3_toks[d2] = lastev
                            psum_guard[bi] = lastev
                    cg["y"] = [lastev]

            if not own:
                return s_dep
            o1_toks = [None] * DT
            for m in range(DT):
                bank = 6 + (m % 2)
                t_w2 = p.emit("sync", lambda e, m=m: e.dma_start(
                    out=wslot.rearrange("p (k c) -> p k c", k=DT),
                    in_=hbm["wout"][:, m * 128:(m + 1) * 128].rearrange(
                        "(k p) c -> p k c", p=128)),
                    deps=[prev_mm[0]], dma=True)
                mm = None
                for k in range(DT):
                    mm = p.emit("tensor", lambda e, k=k, bank=bank: e.matmul(
                        ps[bank][:, 0:256], wslot[:, k * 128:(k + 1) * 128],
                        un[:, A_Y3 + k * 256:A_Y3 + k * 256 + 256],
                        start=(k == 0), stop=(k == DT - 1)),
                        deps=[t_w2, psum_guard[bank], y3_toks[k]],
                        inc=(k == DT - 1))
                prev_mm[0] = mm
                o1_toks[m] = p.emit("vector", lambda e, m=m, bank=bank: e.tensor_copy(
                    ap(A_CONV + m * 256, 256), ps[bank][:, 0:256]),
                    deps=[mm], inc=True)
                psum_guard[bank] = o1_toks[m]
            x3_toks = [None] * DT
            for m in range(DT):
                bank = 6 + (m % 2)
                t_w2 = p.emit("sync", lambda e, m=m: e.dma_start(
                    out=wslot.rearrange("p (k c) -> p k c", k=DT),
                    in_=hbm["wo"][:, m * 128:(m + 1) * 128].rearrange(
                        "(k p) c -> p k c", p=128)),
                    deps=[prev_mm[0]], dma=True)
                mm = None
                for k in range(DT):
                    mm = p.emit("tensor", lambda e, k=k, bank=bank: e.matmul(
                        ps[bank][:, 0:256], wslot[:, k * 128:(k + 1) * 128],
                        ap(A_CONV + k * 256, 256),
                        start=(k == 0), stop=(k == DT - 1)),
                        deps=[t_w2, psum_guard[bank], o1_toks[k]],
                        inc=(k == DT - 1))
                prev_mm[0] = mm
                x3_toks[m] = p.emit("vector", lambda e, m=m, bank=bank: e.tensor_add(
                    xsl(m), xsl(m), ps[bank][:, 0:256]), deps=[mm], inc=True)
                psum_guard[bank] = x3_toks[m]
            return ln_fm(256, xsl, x3_toks, None, None, xsl)

        S_X3, S_XG, S_H = 0, 16384, 20480
        S_GT, S_G, S_WD, S_WGU, S_RT = 22016, 22528, 24576, 25600, 29696
        CAP = 256
        c_tm_ap = lambda: un[:, S_RT:S_RT + 128]
        sel_ap = lambda: un[:, S_RT + 128:S_RT + 256]
        postm_ap = lambda: un[:, S_RT + 256:S_RT + 384]
        cg_ap = un[:, S_RT + 384:S_RT + 386]

        def moe_phase():
            prev_rt = [None]
            out_dma = []
            mm = None
            for ts in range(8):
                for k in range(DT):
                    mm = p.emit("tensor", lambda e, ts=ts, k=k: e.matmul(
                        ps[6][0:128, ts * 16:ts * 16 + 16],
                        x_fm[:, k * HALF + ts * 128: k * HALF + ts * 128 + 128],
                        routerp[:, k * 16:(k + 1) * 16],
                        start=(k == 0), stop=(k == DT - 1)),
                        deps=[psum_guard[6]] if ts == 0 and k == 0 else [],
                        inc=(k == DT - 1))
            tsc = p.emit("vector", lambda e: e.tensor_copy(
                un[:, S_RT + 512:S_RT + 640], ps[6][:, 0:128]),
                deps=[mm], inc=True)
            psum_guard[6] = tsc
            for ts in range(8):
                sc = un[:, S_RT + 512 + ts * 16:S_RT + 512 + ts * 16 + 16]
                cb = un[:, S_RT + ts * 16:S_RT + ts * 16 + 16]
                sel = un[:, S_RT + 128 + ts * 16:S_RT + 128 + ts * 16 + 16]
                m1 = un[:, S_RT + 400:S_RT + 401]
                nm1 = un[:, S_RT + 401:S_RT + 402]
                m2 = un[:, S_RT + 402:S_RT + 403]
                den = un[:, S_RT + 403:S_RT + 404]
                msk = un[:, S_RT + 404:S_RT + 420]
                t1 = p.emit("vector", lambda e, sc=sc: e.reduce_max(
                    m1, sc, AX.X), deps=[tsc, prev_rt[0]], inc=True)
                t2 = p.emit("vector", lambda e: e.tensor_scalar_mul(
                    nm1, m1, -1.0), deps=[t1], inc=True)
                t3 = p.emit("scalar", lambda e, sc=sc, cb=cb: e.activation(
                    out=cb, in_=sc, func=AF.Exp, bias=nm1), deps=[t2], inc=True)
                t4 = p.emit("vector", lambda e, sc=sc, sel=sel: e.tensor_scalar(
                    sel, sc, m1, None, ALU.is_equal), deps=[t1], inc=True)
                t5 = p.emit("vector", lambda e, sc=sc, sel=sel: e.scalar_tensor_tensor(
                    msk, sel, -1e30, sc, ALU.mult, ALU.add), deps=[t4], inc=True)
                t6 = p.emit("vector", lambda e: e.reduce_max(
                    m2, msk, AX.X), deps=[t5], inc=True)
                t7 = p.emit("vector", lambda e: e.tensor_scalar(
                    msk, msk, m2, None, ALU.is_equal), deps=[t6], inc=True)
                t8 = p.emit("vector", lambda e, sel=sel: e.tensor_add(
                    sel, sel, msk), deps=[t7, t4], inc=True)
                t9 = p.emit("vector", lambda e, cb=cb, sel=sel: e.tensor_mul(
                    cb, cb, sel), deps=[t8, t3], inc=True)
                ta_ = p.emit("vector", lambda e, cb=cb: e.reduce_sum(
                    den, cb, AX.X), deps=[t9], inc=True)
                tb_ = p.emit("vector", lambda e: e.reciprocal(
                    den, den), deps=[ta_], inc=True)
                tc_ = p.emit("vector", lambda e, cb=cb: e.tensor_scalar(
                    cb, cb, den, None, ALU.mult), deps=[tb_], inc=True)
                prev_rt[0] = tc_
            t_on = p.emit("vector", lambda e: e.memset(
                un[0:16, 2048:3072], 1.0), inc=True)
            tsel = None
            for ts in range(8):
                bk = 6 + ts // 4
                tt = p.emit("tensor", lambda e, ts=ts, bk=bk: e.transpose(
                    ps[bk][0:16, (ts % 4) * 128:(ts % 4) * 128 + 128],
                    un[:, S_RT + 128 + ts * 16:S_RT + 128 + ts * 16 + 16],
                    identt[:, :]),
                    deps=[prev_rt[0]]
                    + ([psum_guard[bk]] if ts % 4 == 0 else []),
                    inc=(ts == 7))
            tsel = p.emit("vector", lambda e: e.tensor_copy(
                un[0:16, 0:512], ps[6][0:16, 0:512]), deps=[tt], inc=True)
            tsel = p.emit("vector", lambda e: e.tensor_copy(
                un[0:16, 512:1024], ps[7][0:16, 0:512]), deps=[tt, tsel], inc=True)
            psum_guard[6] = tsel
            psum_guard[7] = tsel
            tpos = p.emit("vector", lambda e: e.tensor_tensor_scan(
                un[0:16, 1024:2048], un[0:16, 2048:3072], un[0:16, 0:1024],
                0.0, ALU.mult, ALU.add), deps=[tsel, t_on], inc=True)
            tpos = p.emit("vector", lambda e: e.tensor_mul(
                un[0:16, 1024:2048], un[0:16, 1024:2048], un[0:16, 0:1024]),
                deps=[tpos], inc=True)
            tpt = None
            for ts in range(8):
                tt2 = p.emit("tensor", lambda e, ts=ts: e.transpose(
                    ps[6][0:128, ts * 16:ts * 16 + 16],
                    un[0:16, 1024 + ts * 128:1024 + ts * 128 + 128],
                    identt[0:16, 0:16]),
                    deps=[tpos] + ([psum_guard[6]] if ts == 0 else []),
                    inc=(ts == 7))
            tpt = p.emit("vector", lambda e: e.tensor_copy(
                postm_ap(), ps[6][:, 0:128]), deps=[tt2], inc=True)
            psum_guard[6] = tpt
            p.barrier()
            x3_toks = [None] * 8
            for sl in range(8):
                tt = None
                for dt in range(DT):
                    bi = dt // 4
                    tt = p.emit("tensor", lambda e, sl=sl, dt=dt, bi=bi: e.transpose(
                        ps[bi][:, (dt % 4) * 128:(dt % 4) * 128 + 128],
                        x_fm[:, dt * HALF + sl * 128: dt * HALF + sl * 128 + 128],
                        identt[:, :]),
                        deps=([psum_guard[bi]] if dt % 4 == 0 else []),
                        inc=(dt % 4 == 3))
                    if dt % 4 == 3:
                        ev = p.emit("vector", lambda e, sl=sl, bi=bi: e.tensor_copy(
                            un[:, S_X3 + sl * 2048 + bi * 512:
                               S_X3 + sl * 2048 + bi * 512 + 512],
                            ps[bi][:, 0:512]),
                            deps=[tt] + ([tpt, tsel] if sl == 0 and bi == 0 else []),
                            inc=True)
                        psum_guard[bi] = ev
                x3_toks[sl] = ev
            p.barrier()

            gprev = [None]
            xg_guard = [None]
            h_guard = [None]
            ds_guard = [None]
            xfm_guard = {}
            wgu_tok = [None, None]
            wd_tok = [None, None]
            for ex in range(EXPERTS):
                rls = []
                for ts in range(8):
                    rls.append(p.emit("sync", lambda e, ex=ex, ts=ts: e.dma_start(
                        out=bcb[0:1, 128 + ts * 128:128 + ts * 128 + 128],
                        in_=postm_ap()[:, ts * 16 + ex:ts * 16 + ex + 1]),
                        deps=[gprev[0]], dma=True))
                tgm1 = p.emit("tensor", lambda e: e.matmul(
                    ps[6][:, 0:512], onesr_ap, bcb[0:1, 128:640],
                    start=True, stop=True),
                    deps=rls + [t_or, psum_guard[6]], inc=True)
                tgm2 = p.emit("tensor", lambda e: e.matmul(
                    ps[7][:, 0:512], onesr_ap, bcb[0:1, 640:1152],
                    start=True, stop=True), deps=rls + [psum_guard[7]], inc=True)
                gtoks = []
                for gi in range(2):
                    for half2 in range(2):
                        tgx = p.emit("vector", lambda e, gi=gi, half2=half2: e.tensor_scalar(
                            un[:, S_G + gi * 1024 + half2 * 512:
                               S_G + gi * 1024 + half2 * 512 + 512],
                            ps[6 + half2][:, 0:512], iotac[:, gi:gi + 1],
                            None, ALU.is_equal),
                            deps=[tgm1 if half2 == 0 else tgm2, gprev[0]], inc=True)
                        gtoks.append(tgx)
                psum_guard[6] = gtoks[-1]
                psum_guard[7] = gtoks[-1]
                gt_toks = [gprev[0], gprev[0]]
                xg_toks = []
                for wave in range(2):
                    gm = None
                    for sl in range(8):
                        gts = p.emit("vector", lambda e, sl=sl, ex=ex: e.tensor_tensor(
                            un[:, S_GT + (sl % 2) * 256:S_GT + (sl % 2) * 256 + 256],
                            postm_ap()[:, sl * 16 + ex:sl * 16 + ex + 1]
                            .broadcast_to([128, 256]),
                            iotabc[:, :], ALU.is_equal),
                            deps=[tpt, gt_toks[sl % 2]], inc=True)
                        for dtl in range(8):
                            dt = wave * 8 + dtl
                            gm = p.emit("tensor", lambda e, sl=sl, dt=dt, dtl=dtl: e.matmul(
                                ps[dtl][:, 0:256],
                                un[:, S_X3 + sl * 2048 + dt * 128:
                                   S_X3 + sl * 2048 + dt * 128 + 128],
                                un[:, S_GT + (sl % 2) * 256:S_GT + (sl % 2) * 256 + 256],
                                start=(sl == 0), stop=(sl == 7)),
                                deps=[gts, x3_toks[sl]]
                                + ([psum_guard[dtl]] if sl == 0 else []),
                                inc=(dtl == 7))
                        gt_toks[sl % 2] = gm
                    for dtl in range(8):
                        dt = wave * 8 + dtl
                        ev = p.emit("vector", lambda e, dt=dt, dtl=dtl: e.tensor_copy(
                            un[:, S_XG + dt * 256:S_XG + dt * 256 + 256],
                            ps[dtl][:, 0:256]), deps=[gm, xg_guard[0]], inc=True)
                        psum_guard[dtl] = ev
                        xg_toks.append(ev)
                cgm = None
                cg_gt = [gm, gm]
                for sl in range(8):
                    gts2 = p.emit("vector", lambda e, sl=sl, ex=ex: e.tensor_tensor(
                        un[:, S_GT + (sl % 2) * 256:S_GT + (sl % 2) * 256 + 256],
                        postm_ap()[:, sl * 16 + ex:sl * 16 + ex + 1]
                        .broadcast_to([128, 256]),
                        iotabc[:, :], ALU.is_equal),
                        deps=[tpt, cg_gt[sl % 2]], inc=True)
                    for st in range(2):
                        cgm = p.emit("tensor", lambda e, sl=sl, st=st, ex=ex: e.matmul(
                            ps[6 + st][0:128, 0:1],
                            un[:, S_GT + (sl % 2) * 256 + st * 128:
                               S_GT + (sl % 2) * 256 + st * 128 + 128],
                            c_tm_ap()[:, sl * 16 + ex:sl * 16 + ex + 1],
                            start=(sl == 0), stop=(sl == 7)),
                            deps=[gts2, prev_rt[0]]
                            + ([psum_guard[6 + st]] if sl == 0 else []),
                            inc=True)
                    cg_gt[sl % 2] = cgm
                tcg = p.emit("vector", lambda e: e.tensor_copy(
                    cg_ap[:, 0:1], ps[6][:, 0:1]), deps=[cgm], inc=True)
                tcg = p.emit("vector", lambda e: e.tensor_copy(
                    cg_ap[:, 1:2], ps[7][:, 0:1]), deps=[cgm, tcg], inc=True)
                psum_guard[6] = tcg
                psum_guard[7] = tcg
                h_toks = [None] * FT
                for ft in range(FT):
                    slot = ft % 2
                    twg = p.emit("sync", lambda e, ex=ex, ft=ft, slot=slot: e.dma_start(
                        out=un[:, S_WGU + slot * 2048:S_WGU + slot * 2048 + 2048]
                        .rearrange("p (k c) -> p k c", k=DT),
                        in_=hbm["gw"][ex, :, ft * 128:(ft + 1) * 128].rearrange(
                            "(k p) c -> p k c", p=128)),
                        deps=[wgu_tok[slot]], dma=True)
                    mmg = None
                    for k in range(DT):
                        mmg = p.emit("tensor", lambda e, k=k, slot=slot: e.matmul(
                            ps[6][:, 0:256],
                            un[:, S_WGU + slot * 2048 + k * 128:
                               S_WGU + slot * 2048 + k * 128 + 128],
                            un[:, S_XG + k * 256:S_XG + k * 256 + 256],
                            start=(k == 0), stop=(k == DT - 1)),
                            deps=[twg, psum_guard[6]] + xg_toks, inc=(k == DT - 1))
                    twu = p.emit("sync", lambda e, ex=ex, ft=ft, slot=slot: e.dma_start(
                        out=un[:, S_WGU + slot * 2048:S_WGU + slot * 2048 + 2048]
                        .rearrange("p (k c) -> p k c", k=DT),
                        in_=hbm["uw"][ex, :, ft * 128:(ft + 1) * 128].rearrange(
                            "(k p) c -> p k c", p=128)),
                        deps=[mmg], dma=True)
                    mmu = None
                    for k in range(DT):
                        mmu = p.emit("tensor", lambda e, k=k, slot=slot: e.matmul(
                            ps[7][:, 0:256],
                            un[:, S_WGU + slot * 2048 + k * 128:
                               S_WGU + slot * 2048 + k * 128 + 128],
                            un[:, S_XG + k * 256:S_XG + k * 256 + 256],
                            start=(k == 0), stop=(k == DT - 1)),
                            deps=[twu, psum_guard[7]], inc=(k == DT - 1))
                    wgu_tok[slot] = mmu
                    tg_ = p.emit("scalar", lambda e: e.activation(
                        out=un[:, S_GT:S_GT + 256], in_=ps[6][:, 0:256],
                        func=AF.Silu),
                        deps=[mmg, h_toks[ft - 1] if ft else tcg], inc=True)
                    psum_guard[6] = tg_
                    th_ = p.emit("vector", lambda e, ft=ft: e.tensor_mul(
                        un[:, S_H + ft * 256:S_H + ft * 256 + 256],
                        un[:, S_GT:S_GT + 256], ps[7][:, 0:256]),
                        deps=[tg_, mmu, h_guard[0]], inc=True)
                    psum_guard[7] = th_
                    h_toks[ft] = th_
                dmm = [None] * 8
                for dc in range(4):
                    for ft in range(FT):
                        slot = (dc * FT + ft) % 2
                        twd = p.emit("sync", lambda e, ex=ex, ft=ft, dc=dc,
                                     slot=slot: e.dma_start(
                            out=un[:, S_WD + slot * 512:S_WD + slot * 512 + 512],
                            in_=hbm["dw"][ex, ft * 128:(ft + 1) * 128,
                                          dc * 512:(dc + 1) * 512]),
                            deps=[wd_tok[slot]], dma=True)
                        for st in range(2):
                            bi = st * 4 + dc
                            dmm[bi] = p.emit("tensor", lambda e, ft=ft, dc=dc,
                                             st=st, bi=bi, slot=slot: e.matmul(
                                ps[bi][:, 0:512],
                                un[:, S_H + ft * 256 + st * 128:
                                   S_H + ft * 256 + st * 128 + 128],
                                un[:, S_WD + slot * 512:S_WD + slot * 512 + 512],
                                start=(ft == 0), stop=(ft == FT - 1)),
                                deps=[twd, h_toks[ft]]
                                + ([psum_guard[bi]] if ft == 0 else []),
                                inc=True)
                        wd_tok[slot] = dmm[st * 4 + dc]
                ds_toks = []
                for st in range(2):
                    for dc in range(4):
                        bi = st * 4 + dc
                        ev = p.emit("vector", lambda e, st=st, dc=dc, bi=bi: e.tensor_scalar(
                            un[:, S_XG + st * 2048 + dc * 512:
                               S_XG + st * 2048 + dc * 512 + 512],
                            ps[bi][:, 0:512], cg_ap[:, st:st + 1],
                            None, ALU.mult),
                            deps=[dmm[bi], tcg, ds_guard[0],
                                  h_toks[FT - 1]], inc=True)
                        psum_guard[bi] = ev
                        ds_toks.append(ev)
                sc_last = None
                for dt in range(DT):
                    for tc in range(2):
                        bi = 6 + ((dt * 2 + tc) % 2)
                        mm3 = None
                        for st in range(2):
                            mm3 = p.emit("tensor", lambda e, dt=dt, tc=tc, st=st,
                                         bi=bi: e.matmul(
                                ps[bi][:, 0:512],
                                un[:, S_XG + st * 2048 + dt * 128:
                                   S_XG + st * 2048 + dt * 128 + 128],
                                un[:, S_G + st * 1024 + tc * 512:
                                   S_G + st * 1024 + tc * 512 + 512],
                                start=(st == 0), stop=(st == 1)),
                                deps=[ds_toks[st * 4 + dt // 4],
                                      gtoks[-1], psum_guard[bi]],
                                inc=(st == 1))
                        tad = p.emit("vector", lambda e, dt=dt, tc=tc, bi=bi: e.tensor_add(
                            x_fm[:, dt * HALF + tc * 512: dt * HALF + tc * 512 + 512],
                            x_fm[:, dt * HALF + tc * 512: dt * HALF + tc * 512 + 512],
                            ps[bi][:, 0:512]),
                            deps=[mm3, xfm_guard.get((dt, tc))], inc=True)
                        psum_guard[bi] = tad
                        xfm_guard[(dt, tc)] = tad
                        sc_last = tad
                gprev[0] = sc_last
                xg_guard[0] = sc_last
                h_guard[0] = dmm[7]
                ds_guard[0] = sc_last
            p.barrier()

            for gts_ in range(8):
                tts = None
                for dt in range(DT):
                    bank = 4 + (dt // 4) % 2
                    tts = p.emit("tensor", lambda e, dt=dt, gts_=gts_,
                                 bank=bank: e.transpose(
                        ps[bank][:, (dt % 4) * 128:(dt % 4) * 128 + 128],
                        x_fm[:, dt * HALF + gts_ * 128:
                             dt * HALF + gts_ * 128 + 128],
                        identt[:, :]),
                        deps=([psum_guard[bank]] if dt % 4 == 0 else []),
                        inc=(dt % 4 == 3))
                    if dt % 4 == 3:
                        tsum = p.emit("vector", lambda e, dt=dt, bank=bank: e.tensor_copy(
                            un[:, S_XG + (dt - 3) * 128:S_XG + (dt + 1) * 128],
                            ps[bank][:, 0:512]),
                            deps=[tts, prev_rt[0]], inc=True)
                        psum_guard[bank] = tsum
                mcol = un[:, S_RT + 420:S_RT + 421]
                sscol = un[:, S_RT + 421:S_RT + 422]
                sqcol = un[:, S_RT + 422:S_RT + 423]
                t1 = p.emit("vector", lambda e: e.reduce_sum(
                    mcol, un[:, S_XG:S_XG + 2048], AX.X),
                    deps=[tsum, prev_rt[0]], inc=True)
                t2 = p.emit("vector", lambda e: e.tensor_scalar_mul(
                    mcol, mcol, 1.0 / D), deps=[t1], inc=True)
                tsq = p.emit("scalar", lambda e: e.activation(
                    out=un[:, S_H:S_H + 2048], in_=un[:, S_XG:S_XG + 2048],
                    func=AF.Square, accum_out=sscol),
                    deps=[t2], inc=True)
                t4 = p.emit("vector", lambda e: e.tensor_scalar_mul(
                    sscol, sscol, 1.0 / D), deps=[tsq], inc=True)
                t5 = p.emit("vector", lambda e: e.tensor_mul(
                    sqcol, mcol, mcol), deps=[t2, t4], inc=True)
                t6 = p.emit("vector", lambda e: e.tensor_sub(
                    sscol, sscol, sqcol), deps=[t5], inc=True)
                t7 = p.emit("scalar", lambda e: e.activation(
                    out=sscol, in_=sscol, func=AF.Sqrt, bias=eps_c),
                    deps=[t6, t_eps], inc=True)
                t8 = p.emit("vector", lambda e: e.reciprocal(
                    sscol, sscol), deps=[t7], inc=True)
                t9 = p.emit("vector", lambda e: e.tensor_scalar(
                    un[:, S_XG:S_XG + 2048], un[:, S_XG:S_XG + 2048],
                    mcol, sscol, ALU.subtract, ALU.mult),
                    deps=[t8], inc=True)
                td_ = p.emit("gpsimd", lambda e, gts_=gts_: e.dma_start(
                    out=hbm["out"][gts_ * 128:(gts_ + 1) * 128, :],
                    in_=un[:, S_XG:S_XG + 2048]), deps=[t9], dma=True)
                out_dma.append(td_)
                prev_rt[0] = td_
            return out_dma

        for col in range(DT * NST):
            hcar_dep[col] = t_hz

        def load_x(src, c0):
            return p.emit("sync", lambda e, c0=c0, src=src: e.dma_start(
                out=x_fm[:, :].rearrange("p (t c) -> p t c", t=DT)[:, :, c0:c0 + 512],
                in_=hbm[src][:, :].rearrange("(t p) l -> p t l", p=128)[:, :, c0:c0 + 512]),
                dma=True)

        out_toks = []

        def dbg_dump_fm(ap_fn, toks):
            for dt in range(DT):
                out_toks.append(p.emit("gpsimd", lambda e, dt=dt: e.dma_start(
                    out=hbm["dbg"][:, :].rearrange("(t p) l -> p t l", p=128)[:, dt, :],
                    in_=ap_fn(dt)), deps=[toks[dt]], dma=True))

        def dbg_dump_hilo(toks_h):
            for half in range(2):
                c0 = half * 512
                for dt in range(DT):
                    tt = p.emit("vector", lambda e, dt=dt, c0=c0: e.tensor_add(
                        lnscr[:, 0:512], x1hi(dt, c0), x1lo(dt, c0)),
                        deps=[toks_h[half][dt]] + out_toks[-1:], inc=True)
                    out_toks.append(p.emit("gpsimd", lambda e, dt=dt, c0=c0: e.dma_start(
                        out=hbm["dbg"][:, :].rearrange("(t p) l -> p t l", p=128)[:, dt, c0:c0 + 512],
                        in_=lnscr[:, 0:512]), deps=[tt], dma=True))

        stop = False
        for phase, src in enumerate(("xpre", "xown")):
            want = "" if phase == 1 else "_pre"
            w_l0a = load_l0a_weights()
            x_toks_h = []
            for half in range(2):
                c0 = half * 512
                xt = load_x(src, c0)
                y_toks = l0a_half(c0, [xt], w_l0a)
                toks = ln_fm(512,
                             (lambda dt, c0=c0: x_fm[:, dt * HALF + c0: dt * HALF + c0 + 512]),
                             y_toks,
                             dsthi=(lambda dt, c0=c0: x1hi(dt, c0)),
                             dstlo=(lambda dt, c0=c0: x1lo(dt, c0)),
                             dst32=None)
                x_toks_h.append(toks)
                p.barrier()
            if upto == "l0a" + want:
                dbg_dump_hilo(x_toks_h)
                stop = True
                break

            for half in range(2):
                c0 = half * 512
                toks = l0ffn_chunk(c0, x_toks_h[half])
                x_toks_h[half] = ln_fm(
                    512,
                    (lambda dt, c0=c0: x_fm[:, dt * HALF + c0: dt * HALF + c0 + 512]),
                    toks, dsthi=None, dstlo=None,
                    dst32=(lambda dt, c0=c0: x_fm[:, dt * HALF + c0: dt * HALF + c0 + 512]))
                p.barrier()
            if upto == "l0ffn" + want:
                dbg_dump_fm(lambda dt: x_fm[:, dt * HALF: dt * HALF + HALF],
                            [x_toks_h[1][dt] for dt in range(DT)])
                stop = True
                break
            p.barrier()

            if phase == 0:
                t_s0 = p.emit("vector", lambda e: e.memset(
                    un[:, A_S:A_S + 4096], 0.0), inc=True)
                s_dep = [t_s0]
            else:
                t_sr = p.emit("sync", lambda e: e.dma_start(
                    out=un[:, A_S:A_S + 4096], in_=hbm["sspill"][:, :]),
                    dma=True)
                s_dep = [t_sr]
            nq = 1 if ((upto.startswith("l1a_pre1") or upto == "l1a_sdump")
                       and phase == 0) else 4
            nch = 4
            if upto.startswith("l1a_pre1c"):
                nch = int(upto[len("l1a_pre1c"):])
            for q in range(nq):
                res = l1a_quarter(phase, q, x_toks_h[q // 2], s_dep, nch)
                p.barrier()
                if phase == 0:
                    s_dep = res
                else:
                    x_toks_h[q // 2] = res
            if phase == 0:
                t_sp = p.emit("gpsimd", lambda e: e.dma_start(
                    out=hbm["sspill"][:, :], in_=un[:, A_S:A_S + 4096]),
                    deps=s_dep, dma=True)
                p.emit("gpsimd", lambda e: e.nop(), deps=[t_sp])
                p.barrier()
                if upto == "l1a_sdump":
                    stop = True
                    break
                if upto.startswith("l1a_pre"):
                    for j in range(4):
                        out_toks.append(p.emit("gpsimd", lambda e, j=j: e.dma_start(
                            out=hbm["dbg"][:, :].rearrange(
                                "(t p) l -> p t l", p=128)[:, j, :],
                            in_=un[:, A_S + j * 1024:A_S + (j + 1) * 1024]),
                            deps=s_dep, dma=True))
                    stop = True
                    break
            else:
                if upto == "l1a":
                    dbg_dump_fm(lambda dt: x_fm[:, dt * HALF: dt * HALF + HALF],
                                [x_toks_h[1][dt] for dt in range(DT)])
                    stop = True
                    break
                out_toks.extend(moe_phase())
            p.barrier()

        if out_toks:
            p.emit("gpsimd", lambda e: e.nop(), deps=list(out_toks))

        with nc.Block() as block:
            @block.sync
            def _(eng):
                _replay_one(p, "sync", eng, sems)
            @block.tensor
            def _(eng):
                _replay_one(p, "tensor", eng, sems)
            @block.vector
            def _(eng):
                _replay_one(p, "vector", eng, sems)
            @block.scalar
            def _(eng):
                _replay_one(p, "scalar", eng, sems)
            @block.gpsimd
            def _(eng):
                _replay_one(p, "gpsimd", eng, sems)

    return nc



def _u8(wq):
    a = np.ascontiguousarray(np.asarray(wq, np.int32))
    return a.view(np.uint8).reshape(a.shape + (4,))[..., 0]


def _prep_inputs(inp):
    f32 = np.float32
    x = np.ascontiguousarray(np.asarray(inp["L0_attn_x"], f32))
    def tile_col(v):
        return np.ascontiguousarray(v.reshape(DT, 128).T)
    w = {}
    w["wx_t"] = np.ascontiguousarray(np.asarray(inp["L0_attn_W_x"], f32).T)
    w["wdt_t"] = np.ascontiguousarray(np.asarray(inp["L0_attn_W_dt"], f32).T)
    w["bdt"] = tile_col(np.asarray(inp["L0_attn_b_dt"], f32))
    a0 = np.asarray(inp["L0_attn_A_log"], f32)
    w["a0"] = np.ascontiguousarray(a0.reshape(DT, 128, NST).transpose(1, 0, 2).reshape(128, DT * NST))
    w["d0"] = tile_col(np.asarray(inp["L0_attn_D"], f32))
    w["wqg_t"] = np.ascontiguousarray(_u8(inp["L0_ffn_wq_gate"]).T)
    w["wqu_t"] = np.ascontiguousarray(_u8(inp["L0_ffn_wq_up"]).T)
    w["wqd_t"] = np.ascontiguousarray(_u8(inp["L0_ffn_wq_down"]).T)
    w["sg"] = np.ascontiguousarray(np.asarray(inp["L0_ffn_s_gate"], f32).reshape(HT, 128).T)
    w["su"] = np.ascontiguousarray(np.asarray(inp["L0_ffn_s_up"], f32).reshape(HT, 128).T)
    w["sd"] = tile_col(np.asarray(inp["L0_ffn_s_down"], f32))
    w["nw"] = tile_col(np.asarray(inp["L1_attn_norm_w"], f32))
    w["win"] = np.ascontiguousarray(np.asarray(inp["L1_attn_w_in"], f32))
    cw = np.asarray(inp["L1_attn_conv_w"], f32)
    w["convw"] = np.ascontiguousarray(cw.reshape(NCT, 128, KCONV).transpose(1, 0, 2).reshape(128, NCT * KCONV))
    w["convb"] = np.ascontiguousarray(np.asarray(inp["L1_attn_conv_b"], f32).reshape(NCT, 128).T)
    pad = np.zeros((128, 1), f32); pad[:HEADS, 0] = np.asarray(inp["L1_attn_dt_bias"], f32)
    w["dtb"] = pad
    pad = np.zeros((128, 1), f32); pad[:HEADS, 0] = np.asarray(inp["L1_attn_A_log"], f32)
    w["a1"] = pad
    w["dskipbc"] = np.ascontiguousarray(np.broadcast_to(
        np.asarray(inp["L1_attn_D_skip"], f32)[None, :], (128, HEADS)))
    w["gnw"] = tile_col(np.asarray(inp["L1_attn_gnorm_w"], f32))
    w["wout"] = np.ascontiguousarray(np.asarray(inp["L1_attn_w_out"], f32))
    w["wo"] = np.ascontiguousarray(np.asarray(inp["L1_attn_wo_w"], f32))
    w["router"] = np.ascontiguousarray(np.asarray(inp["L1_ffn_router_w"], f32))
    w["gw"] = np.ascontiguousarray(np.asarray(inp["L1_ffn_gate_w"], f32))
    w["uw"] = np.ascontiguousarray(np.asarray(inp["L1_ffn_up_w"], f32))
    w["dw"] = np.ascontiguousarray(np.asarray(inp["L1_ffn_down_w"], f32))
    w["ident"] = np.eye(128, dtype=f32)
    w["onesr"] = np.ones((1, 128), f32)
    w["negr"] = -np.ones((1, 128), f32)
    w["onesc"] = np.ones((128, 1), f32)
    w["maskt"] = np.ascontiguousarray(np.triu(np.ones((64, 64), f32)))
    w["ones32"] = np.ones((32, 128), f32)
    w["i32"] = np.eye(32, dtype=f32)
    w["iotac_h"] = np.ascontiguousarray(
        np.stack([np.arange(1, 129, dtype=f32),
                  np.arange(129, 257, dtype=f32)], 1))
    w["iotabc_h"] = np.ascontiguousarray(np.broadcast_to(
        np.arange(1, 257, dtype=f32)[None, :], (128, 256)))

    zeros_half = np.zeros((D, HALF), f32)
    in_maps = []
    for c in range(8):
        b = c // 2
        odd = c % 2
        m = dict(w)
        if odd:
            m["xpre"] = np.ascontiguousarray(x[b, 0:HALF, :].T)
            m["xown"] = np.ascontiguousarray(x[b, HALF:L, :].T)
        else:
            m["xpre"] = zeros_half
            m["xown"] = np.ascontiguousarray(x[b, 0:HALF, :].T)
        in_maps.append(m)
    return in_maps


def kernel(**inputs):
    in_maps = _prep_inputs(inputs)
    nc = build_program()
    res = run_bass_kernel_spmd(nc, in_maps, core_ids=list(range(8)))
    out = np.empty((B, L, D), np.float32)
    for c in range(8):
        b, odd = c // 2, c % 2
        out[b, odd * HALF:(odd + 1) * HALF, :] = res.results[c]["out"]
    return out


# version 05
# speedup vs baseline: 1.1590x; 1.0102x over previous
import math
import numpy as np
import concourse.bass as bass
import concourse.mybir as mybir
from concourse.bass_utils import run_bass_kernel_spmd

F32 = mybir.dt.float32
BF16 = mybir.dt.bfloat16
U8 = mybir.dt.uint8
AF = mybir.ActivationFunctionType
ALU = mybir.AluOpType
AX = mybir.AxisListType

D = 2048
DT = 16
B = 4
L = 2048
HALF = 1024
NST = 16
RNK = 64
HID = 5632
HT = 44
HEADS = 32
HDIM = 64
STATE = 256
CONVD = 2560
PROJ = 4640
NZT = 16
NCT = 20
KCONV = 4
CHUNK = 64
EXPERTS = 16
FDIM = 768
FT = 6
EPS = 1e-5
UNW = 32000

ENGINES = ("sync", "tensor", "vector", "scalar", "gpsimd")
N_DSEM = {"sync": 12, "gpsimd": 8}
DMA_SEMS = {e: tuple(f"d_{e}{i}" for i in range(n)) for e, n in N_DSEM.items()}
SEMS = ("s_sync", "s_pe", "s_dve", "s_act", "s_gps") \
    + DMA_SEMS["sync"] + DMA_SEMS["gpsimd"]
OWN_SEM = {"sync": "s_sync", "tensor": "s_pe", "vector": "s_dve",
           "scalar": "s_act", "gpsimd": "s_gps"}


class Op:
    __slots__ = ("fn", "deps", "inc")
    def __init__(self, fn, deps, inc):
        self.fn = fn
        self.deps = deps
        self.inc = inc


class Prog:
    def __init__(self):
        self.ops = {e: [] for e in ENGINES}
        self.cnt = {s: 0 for s in SEMS}
        self.drot = {e: 0 for e in N_DSEM}

    def emit(self, eng, fn, deps=(), inc=False, dma=False):
        deps = tuple(d for d in deps if d is not None)
        if dma:
            sems = DMA_SEMS[eng]
            sem = sems[self.drot[eng] % len(sems)]
            self.drot[eng] += 1
            if self.cnt[sem] > 0:
                deps = deps + ((sem, self.cnt[sem]),)
            self.cnt[sem] += 16
            tok = (sem, self.cnt[sem])
            self.ops[eng].append(Op(fn, deps, (sem, 16)))
            return tok
        if inc:
            sem = OWN_SEM[eng]
            self.cnt[sem] += 1
            tok = (sem, self.cnt[sem])
            self.ops[eng].append(Op(fn, deps, (sem, 1)))
            return tok
        self.ops[eng].append(Op(fn, deps, None))
        return None

    def barrier(self, engs=ENGINES):
        toks = {}
        for e in engs:
            toks[e] = self.emit(e, lambda x: x.nop(), inc=True)
        for e in engs:
            self.emit(e, lambda x: x.nop(),
                      deps=[toks[o] for o in engs if o != e])


def _replay_one(prog, ename, eng, sems):
    own = OWN_SEM[ename]
    used = set()
    for e2 in ENGINES:
        for op in prog.ops[e2]:
            for d in op.deps:
                used.add(d)
    waited = {}
    cur = 0
    drained = 0
    for op in prog.ops[ename]:
        for (sn, v) in op.deps:
            if sn == own:
                if ename != "tensor" and v > drained:
                    eng.drain()
                    drained = cur
                continue
            if waited.get(sn, 0) >= v:
                continue
            eng.wait_ge(sems[sn], v)
            waited[sn] = v
        instr = op.fn(eng)
        if op.inc is not None:
            sn, k = op.inc
            instr.then_inc(sems[sn], k)
            if sn == own:
                cur += 1
                if (own, cur) in used:
                    eng.wait_ge(sems[own], cur)



def build_program(upto="all"):
    nc = bass.Bass("TRN2", target_bir_lowering=False, debug=False)

    hbm = {}
    def H(name, shape, dt=F32, kind="ExternalInput"):
        hbm[name] = nc.dram_tensor(name, shape, dt, kind=kind).ap()
        return hbm[name]

    H("xpre", [D, HALF])
    H("xown", [D, HALF])
    H("wx_t", [D, 96])
    H("wdt_t", [RNK, D])
    H("bdt", [128, DT])
    H("a0", [128, DT * NST])
    H("d0", [128, DT])
    H("wqg_t", [D, HID], U8)
    H("wqu_t", [D, HID], U8)
    H("wqd_t", [HID, D], U8)
    H("sg", [128, HT])
    H("su", [128, HT])
    H("sd", [128, DT])
    H("nw", [128, DT])
    H("win", [D, PROJ])
    H("convw", [128, NCT * KCONV])
    H("convb", [128, NCT])
    H("dtb", [128, 1])
    H("a1", [128, 1])
    H("dskipbc", [128, HEADS])
    H("gnw", [128, DT])
    H("ones32", [32, 128])
    H("i32", [32, 32])
    H("iotac_h", [128, 2])
    H("iotabc_h", [128, 256])
    H("wout", [D, D])
    H("wo", [D, D])
    H("router", [D, EXPERTS])
    H("gw", [EXPERTS, D, FDIM])
    H("uw", [EXPERTS, D, FDIM])
    H("dw", [EXPERTS, FDIM, D])
    H("ident", [128, 128])
    H("onesr", [1, 128])
    H("negr", [1, 128])
    H("onesc", [128, 1])
    H("maskt", [64, 64])
    H("sspill", [128, 4096], kind="ExternalOutput")
    H("out", [HALF, D], kind="ExternalOutput")
    H("dbg", [D, HALF], kind="ExternalOutput")

    import contextlib
    stack = contextlib.ExitStack()

    with stack:
        x_fm = stack.enter_context(nc.sbuf_tensor("x_fm", [128, DT * HALF], F32))
        xfb = x_fm[:, :].bitcast(BF16)
        un = stack.enter_context(nc.sbuf_tensor("un", [128, UNW], F32))
        unb = un[:, :].bitcast(BF16)
        unu = un[:, :].bitcast(U8)
        xproj = stack.enter_context(nc.sbuf_tensor("xproj", [128, 512], F32))
        hcar = stack.enter_context(nc.sbuf_tensor("hcar", [128, DT * NST], F32))
        a0p = stack.enter_context(nc.sbuf_tensor("a0p", [128, DT * NST], F32))
        bcb = stack.enter_context(nc.sbuf_tensor("bcb", [128, 1792], F32))
        lnscr = stack.enter_context(nc.sbuf_tensor("lnscr", [128, 512], F32))
        identt = stack.enter_context(nc.sbuf_tensor("identt", [128, 128], F32))
        masktt = stack.enter_context(nc.sbuf_tensor("masktt", [128, 64], F32))
        halo = stack.enter_context(nc.sbuf_tensor("halo", [128, NCT * (KCONV - 1)], F32))
        params = stack.enter_context(nc.sbuf_tensor("params", [128, 192], F32))
        convwp = stack.enter_context(nc.sbuf_tensor("convwp", [128, NCT * KCONV], F32))
        convbp = stack.enter_context(nc.sbuf_tensor("convbp", [128, NCT], F32))
        dskipp = stack.enter_context(nc.sbuf_tensor("dskipp", [128, HEADS], F32))
        ones32t = stack.enter_context(nc.sbuf_tensor("ones32t", [32, 128], F32))
        i32t = stack.enter_context(nc.sbuf_tensor("i32t", [32, 32], F32))
        iotac = stack.enter_context(nc.sbuf_tensor("iotac", [128, 2], F32))
        iotabc = stack.enter_context(nc.sbuf_tensor("iotabc", [128, 256], F32))
        routerp = stack.enter_context(nc.sbuf_tensor("routerp", [128, DT * EXPERTS], F32))

        ps = [stack.enter_context(nc.psum_tensor(f"ps{i}", [128, 512], F32))
              for i in range(8)]

        sems = {}
        for s in SEMS:
            sems[s] = stack.enter_context(nc.semaphore(s))

        p = Prog()
        psum_guard = [None] * 8

        bdt_c = lambda dt: params[:, dt:dt + 1]
        dp1_c = lambda dt: params[:, 16 + dt:17 + dt]
        sg_c = lambda f: params[:, 32 + f:33 + f]
        su_c = lambda f: params[:, 76 + f:77 + f]
        sd_c = lambda dt: params[:, 120 + dt:121 + dt]
        nw_c = lambda dt: params[:, 136 + dt:137 + dt]
        dtb_c = params[:, 152:153]
        a1n_c = params[:, 153:154]
        ones_c = params[:, 154:155]
        eps_c = params[:, 155:156]
        onesr_ap = bcb[0:1, 0:128]
        negr_ap = bcb[0:1, 1664:1792]
        st_row = lambda i, w: bcb[0:1, 128 + i * 512:128 + i * 512 + w]

        def ld(dst, src, eng="sync", deps=()):
            return p.emit(eng, lambda e, d=dst, s=src: e.dma_start(out=d, in_=s),
                          deps=deps, dma=True)

        t_bdt = ld(params[:, 0:16], hbm["bdt"][:, :])
        t_d0 = ld(params[:, 16:32], hbm["d0"][:, :])
        ld(params[:, 32:76], hbm["sg"][:, :])
        ld(params[:, 76:120], hbm["su"][:, :])
        ld(params[:, 120:136], hbm["sd"][:, :])
        ld(params[:, 136:152], hbm["nw"][:, :])
        t_dtb = ld(params[:, 152:153], hbm["dtb"][:, :])
        t_a1 = ld(params[:, 153:154], hbm["a1"][:, :])
        t_oc = ld(params[:, 154:155], hbm["onesc"][:, :])
        t_a0 = ld(a0p[:, :], hbm["a0"][:, :])
        ld(convwp[:, :], hbm["convw"][:, :])
        ld(convbp[:, :], hbm["convb"][:, :])
        ld(dskipp[:, :], hbm["dskipbc"][:, :])
        ld(ones32t[:, :], hbm["ones32"][:, :])
        ld(i32t[:, :], hbm["i32"][:, :])
        ld(iotac[:, :], hbm["iotac_h"][:, :])
        ld(iotabc[:, :], hbm["iotabc_h"][:, :])
        ld(params[:, 160:176], hbm["gnw"][:, :])
        ld(identt[:, :], hbm["ident"][:, :])
        ld(masktt[0:64, :], hbm["maskt"][:, :])
        t_or = ld(bcb[0:1, 0:128], hbm["onesr"][:, :])
        t_nr = ld(bcb[0:1, 1664:1792], hbm["negr"][:, :])
        ld(routerp[:, :].rearrange("p (t e) -> p t e", t=DT),
           hbm["router"][:, :].rearrange("(t p) e -> p t e", p=128))

        t_eps = p.emit("vector", lambda e: e.memset(eps_c, EPS), inc=True)
        t_dp1 = p.emit("vector", lambda e: e.tensor_scalar_add(
            params[:, 16:32], params[:, 16:32], 1.0), deps=[t_d0], inc=True)
        t_bdt = p.emit("vector", lambda e: e.tensor_scalar_mul(
            params[:, 0:16], params[:, 0:16], -1.0), deps=[t_bdt], inc=True)
        t_dtb = p.emit("vector", lambda e: e.tensor_scalar_mul(
            dtb_c, dtb_c, -1.0), deps=[t_dtb], inc=True)
        t_a1n = p.emit("scalar", lambda e: e.activation(
            out=a1n_c, in_=a1n_c, func=AF.Exp), deps=[t_a1], inc=True)
        t_a0e = p.emit("scalar", lambda e: e.activation(
            out=a0p[:, :], in_=a0p[:, :], func=AF.Exp), deps=[t_a0], inc=True)
        t_hz = p.emit("vector", lambda e: e.memset(hcar[:, :], 0.0), inc=True)
        t_hlz = p.emit("vector", lambda e: e.memset(halo[:, :], 0.0), inc=True)

        p.barrier()

        def ln_fm(w, src_ap, src_toks, dsthi, dstlo, dst32, rms=False,
                  wcol=None):
            scr = lnscr[:, 0:w]
            st0 = st_row(0, w)
            st1 = st_row(1, w)
            st2 = st_row(2, w)
            t_sum = None
            t_sqmm = None
            for dt in range(DT):
                t_sq = p.emit("vector", lambda e, dt=dt: e.tensor_mul(
                    scr, src_ap(dt), src_ap(dt)),
                    deps=[src_toks[dt]] + ([t_sqmm] if dt else []), inc=True)
                if not rms:
                    t_sum = p.emit("tensor", lambda e, dt=dt: e.matmul(
                        ps[6][0:1, 0:w], ones_c, src_ap(dt),
                        start=(dt == 0), stop=(dt == DT - 1)),
                        deps=[src_toks[dt], t_oc] + ([psum_guard[6]] if dt == 0 else []),
                        inc=(dt == DT - 1))
                t_sqmm = p.emit("tensor", lambda e, dt=dt: e.matmul(
                    ps[7][0:1, 0:w], ones_c, scr,
                    start=(dt == 0), stop=(dt == DT - 1)),
                    deps=[t_sq] + ([psum_guard[7]] if dt == 0 else []),
                    inc=True)
            if not rms:
                t_mean = p.emit("vector", lambda e: e.tensor_scalar_mul(
                    st0, ps[6][0:1, 0:w], 1.0 / D), deps=[t_sum], inc=True)
                psum_guard[6] = t_mean
                t_msq = p.emit("vector", lambda e: e.tensor_mul(
                    st2, st0, st0), deps=[t_mean], inc=True)
                t_var = p.emit("vector", lambda e: e.scalar_tensor_tensor(
                    st1, ps[7][0:1, 0:w], 1.0 / D, st2,
                    ALU.mult, ALU.subtract), deps=[t_msq, t_sqmm], inc=True)
            else:
                t_var = p.emit("vector", lambda e: e.tensor_scalar_mul(
                    st1, ps[7][0:1, 0:w], 1.0 / D), deps=[t_sqmm], inc=True)
            psum_guard[7] = t_var
            t_rstd = p.emit("scalar", lambda e: e.activation(
                out=st1, in_=st1, func=AF.Sqrt, bias=eps_c[0:1, :]),
                deps=[t_var, t_eps], inc=True)
            t_rstd = p.emit("vector", lambda e: e.reciprocal(st1, st1),
                            deps=[t_rstd], inc=True)
            if not rms:
                t_bm = p.emit("tensor", lambda e: e.matmul(
                    ps[5][:, 0:w], onesr_ap, st0, start=True, stop=True),
                    deps=[t_mean, t_or, psum_guard[5]], inc=True)
            t_br = p.emit("tensor", lambda e: e.matmul(
                ps[4][:, 0:w], onesr_ap, st1, start=True, stop=True),
                deps=[t_rstd, t_or, psum_guard[4]], inc=True)
            toks = [None] * DT
            prev = None
            for dt in range(DT):
                if not rms:
                    tt = p.emit("vector", lambda e, dt=dt: e.tensor_sub(
                        scr, src_ap(dt), ps[5][:, 0:w]),
                        deps=[t_bm, src_toks[dt], prev], inc=True)
                    tu = p.emit("vector", lambda e: e.tensor_mul(
                        scr, scr, ps[4][:, 0:w]), deps=[tt, t_br], inc=True)
                else:
                    tu = p.emit("vector", lambda e, dt=dt: e.scalar_tensor_tensor(
                        scr, src_ap(dt), wcol(dt), ps[4][:, 0:w],
                        ALU.mult, ALU.mult),
                        deps=[src_toks[dt], t_br, prev], inc=True)
                if dst32 is not None:
                    tu = p.emit("vector", lambda e, dt=dt: e.tensor_copy(
                        dst32(dt), scr), deps=[tu], inc=True)
                if dsthi is not None:
                    th = p.emit("vector", lambda e, dt=dt: e.tensor_copy(
                        dsthi(dt), scr), deps=[tu], inc=True)
                    tu = p.emit("vector", lambda e, dt=dt: e.tensor_tensor(
                        dstlo(dt), scr, dsthi(dt), ALU.subtract),
                        deps=[th], inc=True)
                toks[dt] = tu
                prev = tu
            psum_guard[4] = toks[DT - 1]
            if not rms:
                psum_guard[5] = toks[DT - 1]
            return toks

        def wdt_ap(dt):
            return un[0:RNK, 27904 + dt * 128: 27904 + (dt + 1) * 128]

        def wx_ap(k):
            return un[:, 29952 + k * 96: 29952 + (k + 1) * 96]

        def load_l0a_weights(deps=()):
            t1 = p.emit("sync", lambda e: e.dma_start(
                out=un[:, 29952:31488].rearrange("p (t c) -> p t c", t=DT),
                in_=hbm["wx_t"][:, :].rearrange("(t p) c -> p t c", p=128)),
                deps=deps, dma=True)
            t2 = p.emit("sync", lambda e: e.dma_start(
                out=un[0:RNK, 27904:29952], in_=hbm["wdt_t"][:, :]),
                deps=deps, dma=True)
            return [t1, t2]

        hcar_dep = {}

        def l0a_half(c0, x_toks, w_toks):
            def x_sb(dt):
                return x_fm[:, dt * HALF + c0: dt * HALF + c0 + 512]
            t_xp = None
            for k in range(DT):
                t_xp = p.emit("tensor", lambda e, k=k: e.matmul(
                    ps[7][0:96, :], wx_ap(k), x_sb(k),
                    start=(k == 0), stop=(k == DT - 1)),
                    deps=list(x_toks) + list(w_toks) + ([psum_guard[7]] if k == 0 else []),
                    inc=(k == DT - 1))
            t_xpe = p.emit("vector", lambda e: e.tensor_copy(
                xproj[0:96, :], ps[7][0:96, :]), deps=[t_xp], inc=True)
            psum_guard[7] = t_xpe

            t2_toks = [None] * DT
            dtx_toks = [None] * DT
            for dt in range(DT):
                bi = 6 if dt % 2 == 0 else 5
                mm = p.emit("tensor", lambda e, dt=dt, bi=bi: e.matmul(
                    ps[bi][:, :], wdt_ap(dt), xproj[0:RNK, :],
                    start=True, stop=True),
                    deps=[t_xpe, psum_guard[bi]] + list(w_toks), inc=True)
                sp = p.emit("scalar", lambda e, dt=dt, bi=bi: e.activation(
                    out=un[:, dt * 512: dt * 512 + 512], in_=ps[bi][:, :],
                    func=AF.Sigmoid, scale=-1.0, bias=bdt_c(dt)),
                    deps=[mm, t_bdt], inc=True)
                psum_guard[bi] = sp
                sp = p.emit("scalar", lambda e, dt=dt: e.activation(
                    out=un[:, dt * 512: dt * 512 + 512],
                    in_=un[:, dt * 512: dt * 512 + 512], func=AF.Ln),
                    deps=[sp], inc=True)
                t2_toks[dt] = sp
                dtx_toks[dt] = p.emit("vector", lambda e, dt=dt: e.tensor_mul(
                    un[:, 8192 + dt * 512: 8192 + dt * 512 + 512],
                    un[:, dt * 512: dt * 512 + 512], x_sb(dt)),
                    deps=[sp] + list(x_toks), inc=True)

            y_toks = [None] * DT
            for dt in range(DT):
                y_toks[dt] = p.emit("vector", lambda e, dt=dt: e.tensor_scalar(
                    x_sb(dt), x_sb(dt), dp1_c(dt), None, ALU.mult),
                    deps=[t_dp1, dtx_toks[dt]], inc=True)

            scan_tok = [None, None]
            mul_tok = [None, None]
            add_tok = [None, None]
            bcb_guard = [None, None]
            for n in range(NST):
                npar = n % 2
                bbank, cbank = npar, 2 + npar
                boff = 20480 + npar * 1024
                tb = p.emit("sync", lambda e, n=n, bo=boff: e.dma_start(
                    out=un[0:1, bo:bo + 512], in_=xproj[64 + n:65 + n, :]),
                    deps=[t_xpe, bcb_guard[npar]], dma=True)
                tc = p.emit("sync", lambda e, n=n, bo=boff: e.dma_start(
                    out=un[0:1, bo + 512:bo + 1024], in_=xproj[80 + n:81 + n, :]),
                    deps=[t_xpe, bcb_guard[npar]], dma=True)
                bb = p.emit("tensor", lambda e, bo=boff, bbank=bbank: e.matmul(
                    ps[bbank][:, :], negr_ap, un[0:1, bo:bo + 512],
                    start=True, stop=True),
                    deps=[tb, t_nr, psum_guard[bbank]], inc=True)
                cb = p.emit("tensor", lambda e, bo=boff, cbank=cbank: e.matmul(
                    ps[cbank][:, :], onesr_ap, un[0:1, bo + 512:bo + 1024],
                    start=True, stop=True),
                    deps=[tc, t_or, psum_guard[cbank]], inc=True)
                bcb_guard[npar] = cb
                for dt in range(DT):
                    col = dt * NST + n
                    it = n * DT + dt
                    par = it % 2
                    o_da = 16384 + par * 2048
                    o_db = o_da + 512
                    o_h = o_da + 1024
                    o_p = o_da + 1536
                    da = p.emit("scalar", lambda e, dt=dt, col=col, o_da=o_da: e.activation(
                        out=un[:, o_da:o_da + 512],
                        in_=un[:, dt * 512: dt * 512 + 512],
                        func=AF.Exp, scale=a0p[:, col:col + 1]),
                        deps=[t2_toks[dt], t_a0e, scan_tok[par]], inc=True)
                    db = p.emit("vector", lambda e, dt=dt, o_db=o_db, bbank=bbank: e.scalar_tensor_tensor(
                        un[:, o_db:o_db + 512],
                        un[:, 8192 + dt * 512: 8192 + dt * 512 + 512],
                        1.0, ps[bbank][:, :], ALU.mult, ALU.mult),
                        deps=[dtx_toks[dt], bb, scan_tok[par]], inc=True)
                    psum_guard[bbank] = db
                    h = p.emit("vector", lambda e, col=col, o_da=o_da, o_db=o_db, o_h=o_h:
                        e.tensor_tensor_scan(
                            un[:, o_h:o_h + 512], un[:, o_da:o_da + 512],
                            un[:, o_db:o_db + 512], hcar[:, col:col + 1],
                            ALU.mult, ALU.add),
                        deps=[da, db, hcar_dep[col], mul_tok[par]], inc=True)
                    scan_tok[par] = h
                    hc = p.emit("vector", lambda e, col=col, o_h=o_h: e.tensor_copy(
                        hcar[:, col:col + 1], un[:, o_h + 511:o_h + 512]),
                        deps=[h], inc=True)
                    hcar_dep[col] = hc
                    m1 = p.emit("vector", lambda e, o_h=o_h, o_p=o_p, cbank=cbank: e.tensor_mul(
                        un[:, o_p:o_p + 512], un[:, o_h:o_h + 512], ps[cbank][:, :]),
                        deps=[h, cb, add_tok[par]], inc=True)
                    psum_guard[cbank] = m1
                    mul_tok[par] = m1
                    at_ = p.emit("gpsimd", lambda e, dt=dt, o_p=o_p: e.tensor_add(
                        x_sb(dt), x_sb(dt), un[:, o_p:o_p + 512]),
                        deps=[m1, y_toks[dt]], inc=True)
                    y_toks[dt] = at_
                    add_tok[par] = at_
            return y_toks

        def x1hi(dt, c0, w=512):
            return xfb[:, 2 * (dt * HALF + c0): 2 * (dt * HALF + c0) + w]

        def x1lo(dt, c0, w=512):
            return xfb[:, 2 * (dt * HALF + c0) + 512: 2 * (dt * HALF + c0) + 512 + w]

        def l0ffn_chunk(c0, x_toks):
            whb = [unb[:, 45056 + i * 2048: 45056 + (i + 1) * 2048] for i in range(4)]
            wu8 = [unu[:, 106496 + i * 2048: 106496 + (i + 1) * 2048] for i in range(2)]
            slot_tok = [None, None, None, None]
            u8_tok = [None, None]
            gwork = un[:, 27648:28160]
            hwork = un[:, 28160:28672]
            hhi = lambda f: unb[:, f * 512:(f + 1) * 512]
            hlo = lambda f: unb[:, 22528 + f * 512:22528 + (f + 1) * 512]
            h_toks = [None] * HT
            conv_engs = ("scalar", "vector", "gpsimd")

            def conv_op(ceng, dst, srcu8):
                if ceng == "scalar":
                    return lambda e: e.activation(out=dst, in_=srcu8,
                                                  func=AF.Copy, bias=-128.0)
                return lambda e: e.tensor_scalar_add(dst, srcu8, -128.0)

            tg = None
            for f in range(HT):
                for gi, wsrc in enumerate((hbm["wqg_t"], hbm["wqu_t"])):
                    slot = (2 * f + gi) % 4
                    u8s = gi
                    tl = p.emit("sync", lambda e, f=f, wsrc=wsrc, u8s=u8s: e.dma_start(
                        out=wu8[u8s].rearrange("p (k c) -> p k c", k=DT),
                        in_=wsrc[:, f * 128:(f + 1) * 128].rearrange(
                            "(k p) c -> p k c", p=128)),
                        deps=[u8_tok[u8s]], dma=True)
                    ceng = conv_engs[(2 * f + gi) % 3]
                    tcv = p.emit(ceng, conv_op(ceng, whb[slot], wu8[u8s]),
                                 deps=[tl, slot_tok[slot]], inc=True)
                    u8_tok[u8s] = tcv
                    bank = 6 + gi
                    mm = None
                    for k in range(DT):
                        for hl, xap in ((0, x1hi), (1, x1lo)):
                            mm = p.emit("tensor", lambda e, k=k, hl=hl, xap=xap,
                                        slot=slot, bank=bank: e.matmul(
                                ps[bank][:, :], whb[slot][:, k * 128:(k + 1) * 128],
                                xap(k, c0), start=(k == 0 and hl == 0),
                                stop=(k == DT - 1 and hl == 1)),
                                deps=[tcv, psum_guard[bank]]
                                + (list(x_toks) if (k == 0 and hl == 0) else []),
                                inc=(k == DT - 1 and hl == 1))
                    slot_tok[slot] = mm
                    prev_h = h_toks[f - 1] if f else None
                    if gi == 0:
                        tg = p.emit("scalar", lambda e, f=f: e.activation(
                            out=gwork, in_=ps[6][:, :], func=AF.Silu, scale=sg_c(f)),
                            deps=[mm, prev_h], inc=True)
                        psum_guard[6] = tg
                    else:
                        tu = p.emit("vector", lambda e, f=f: e.tensor_scalar(
                            hwork, ps[7][:, :], su_c(f), None, ALU.mult),
                            deps=[mm, prev_h], inc=True)
                        psum_guard[7] = tu
                        tm = p.emit("vector", lambda e: e.tensor_mul(
                            gwork, gwork, hwork), deps=[tu, tg], inc=True)
                        th = p.emit("vector", lambda e, f=f: e.tensor_copy(
                            hhi(f), gwork), deps=[tm], inc=True)
                        h_toks[f] = p.emit("vector", lambda e, f=f: e.tensor_tensor(
                            hlo(f), gwork, hhi(f), ALU.subtract), deps=[th], inc=True)

            dwb = [unb[:, 45056 + i * 5632: 45056 + (i + 1) * 5632] for i in range(2)]
            du8 = [unu[:, 116736 + i * 5632: 116736 + (i + 1) * 5632] for i in range(2)]
            dwork = [un[:, 28160:28672], un[:, 28672:29184]]
            dslot_tok = [slot_tok[0] or slot_tok[2], slot_tok[1] or slot_tok[3]]
            dslot_tok = [slot_tok[2], slot_tok[3]]
            du8_tok = [u8_tok[0], u8_tok[1]]
            dn_toks = [None] * DT
            for dt in range(DT):
                slot = dt % 2
                tl = p.emit("sync", lambda e, dt=dt, slot=slot: e.dma_start(
                    out=du8[slot].rearrange("p (k c) -> p k c", k=HT),
                    in_=hbm["wqd_t"][:, dt * 128:(dt + 1) * 128].rearrange(
                        "(k p) c -> p k c", p=128)),
                    deps=[du8_tok[slot]], dma=True)
                ceng = conv_engs[dt % 3]
                gd = list(slot_tok) + [h_toks[HT - 1]] \
                    if dt < 2 else [dslot_tok[slot]]
                tcv = p.emit(ceng, conv_op(ceng, dwb[slot], du8[slot]),
                             deps=[tl] + gd, inc=True)
                du8_tok[slot] = tcv
                bank = 6 + slot
                mm = None
                for k in range(HT):
                    for hl, hap in ((0, hhi), (1, hlo)):
                        mm = p.emit("tensor", lambda e, k=k, hl=hl, hap=hap,
                                    slot=slot, bank=bank: e.matmul(
                            ps[bank][:, :], dwb[slot][:, k * 128:(k + 1) * 128],
                            hap(k), start=(k == 0 and hl == 0),
                            stop=(k == HT - 1 and hl == 1)),
                            deps=[tcv, psum_guard[bank], h_toks[k]],
                            inc=(k == HT - 1 and hl == 1))
                dslot_tok[slot] = mm
                tv = p.emit("vector", lambda e, dt=dt, bank=bank, slot=slot: e.scalar_tensor_tensor(
                    dwork[slot], ps[bank][:, :], sd_c(dt), x1hi(dt, c0),
                    ALU.mult, ALU.add),
                    deps=[mm] + ([dn_toks[dt - 2]] if dt >= 2 else [h_toks[HT - 1]]),
                    inc=True)
                psum_guard[bank] = tv
                dn_toks[dt] = p.emit("vector", lambda e, dt=dt, slot=slot: e.tensor_add(
                    x_fm[:, dt * HALF + c0: dt * HALF + c0 + 512],
                    dwork[slot], x1lo(dt, c0)), deps=[tv], inc=True)
            return dn_toks

        A_CONV, A_Z, A_DT, A_ACS = 0, 5120, 9216, 9472
        A_XBCT, A_XH, A_G, A_CBT = 9728, 9992, 12040, 14088
        A_XT, A_Y, A_ZT, A_MISC = 14152, 16200, 18248, 20296
        A_Y3, A_S, A_WIN = 20992, 25088, 29184
        M_AT = A_MISC
        M_EXPA = A_MISC + 32
        M_DECT = A_MISC + 64
        M_DECBC = A_MISC + 96
        M_ALB = A_MISC + 128
        M_DTT = A_MISC + 160
        M_BTM = A_MISC + 192
        M_ONES = A_MISC + 448
        M_RSTD = A_MISC + 512
        gnw_c = lambda dt: params[:, 160 + dt:161 + dt]

        def ap(off, w, pn=128):
            return un[0:pn, off:off + w]

        def l1a_quarter(phase, q, xq_toks, s_toks, nchunks=4):
            own = (phase == 1)
            qc0 = q * 256
            xsl = lambda dt: x_fm[:, dt * HALF + qc0: dt * HALF + qc0 + 256]
            xn_off = lambda k: (A_XT + k * 256) if k < 8 else (A_Y + (k - 8) * 256)
            xn_ap = lambda k: un[:, xn_off(k):xn_off(k) + 256]
            xn_toks = ln_fm(256, xsl, xq_toks, None, None, xn_ap,
                            rms=True, wcol=nw_c)
            t_ones = p.emit("vector", lambda e: e.memset(
                ap(M_ONES, 64, 32), 1.0), inc=True)

            conv_toks = [None] * NCT
            z_toks = [None] * NZT
            wslot = un[:, A_WIN:A_WIN + 2048]
            prev_mm = [None]
            t_stage = None
            dt_tok = [None, None]
            mrange = list(range(NZT, 37)) + (list(range(NZT)) if own else [])
            for mi, m in enumerate(mrange):
                bank = 6 + (mi % 2)
                mw = 32 if m == 36 else 128
                t_w = p.emit("sync", lambda e, m=m, mw=mw: e.dma_start(
                    out=wslot.rearrange("p (k c) -> p k c", k=DT)[:, :, 0:mw],
                    in_=hbm["win"][:, m * 128:m * 128 + mw].rearrange(
                        "(k p) c -> p k c", p=128)),
                    deps=[prev_mm[0]], dma=True)
                mm = None
                for k in range(DT):
                    mm = p.emit("tensor", lambda e, k=k, mw=mw, bank=bank: e.matmul(
                        ps[bank][0:mw, 0:256], wslot[:, k * 128:k * 128 + mw],
                        xn_ap(k), start=(k == 0), stop=(k == DT - 1)),
                        deps=[t_w, psum_guard[bank], xn_toks[k]],
                        inc=(k == DT - 1))
                prev_mm[0] = mm
                if m < NZT:
                    z_toks[m] = p.emit("vector", lambda e, m=m, bank=bank: e.tensor_copy(
                        ap(A_Z + m * 256, 256), ps[bank][:, 0:256]),
                        deps=[mm], inc=True)
                    psum_guard[bank] = z_toks[m]
                elif m < 36:
                    ct = m - NZT
                    tse = p.emit("vector", lambda e, bank=bank: e.tensor_copy(
                        un[:, A_XBCT + 3:A_XBCT + 259], ps[bank][:, 0:256]),
                        deps=[mm, t_stage], inc=True)
                    psum_guard[bank] = tse
                    th = p.emit("vector", lambda e, ct=ct: e.tensor_copy(
                        un[:, A_XBCT:A_XBCT + 3], halo[:, ct * 3:ct * 3 + 3]),
                        deps=[tse], inc=True)
                    cv = p.emit("vector", lambda e, ct=ct: e.tensor_scalar(
                        ap(A_CONV + ct * 256, 256), un[:, A_XBCT:A_XBCT + 256],
                        convwp[:, ct * KCONV:ct * KCONV + 1], convbp[:, ct:ct + 1],
                        ALU.mult, ALU.add), deps=[th], inc=True)
                    for kk in range(1, KCONV):
                        cv = p.emit("vector", lambda e, ct=ct, kk=kk: e.scalar_tensor_tensor(
                            ap(A_CONV + ct * 256, 256),
                            un[:, A_XBCT + kk:A_XBCT + kk + 256],
                            convwp[:, ct * KCONV + kk:ct * KCONV + kk + 1],
                            ap(A_CONV + ct * 256, 256), ALU.mult, ALU.add),
                            deps=[cv], inc=True)
                    hn = p.emit("gpsimd", lambda e, ct=ct: e.tensor_copy(
                        halo[:, ct * 3:ct * 3 + 3],
                        un[:, A_XBCT + 256:A_XBCT + 259]), deps=[cv], inc=True)
                    t_stage = hn
                    conv_toks[ct] = p.emit("scalar", lambda e, ct=ct: e.activation(
                        out=ap(A_CONV + ct * 256, 256),
                        in_=ap(A_CONV + ct * 256, 256),
                        func=AF.Silu), deps=[cv], inc=True)
                else:
                    t1 = p.emit("scalar", lambda e, bank=bank: e.activation(
                        out=ap(A_DT, 256, 32), in_=ps[bank][0:32, 0:256],
                        func=AF.Sigmoid, scale=-1.0, bias=dtb_c[0:32, :]),
                        deps=[mm, t_dtb], inc=True)
                    psum_guard[bank] = t1
                    t1 = p.emit("scalar", lambda e: e.activation(
                        out=ap(A_DT, 256, 32), in_=ap(A_DT, 256, 32),
                        func=AF.Ln), deps=[t1], inc=True)
                    t2a = p.emit("vector", lambda e: e.tensor_scalar(
                        ap(A_ACS, 256, 32), ap(A_DT, 256, 32),
                        a1n_c[0:32, :], None, ALU.mult),
                        deps=[t1, t_a1n], inc=True)
                    t2b = p.emit("vector", lambda e: e.tensor_scalar_mul(
                        ap(A_DT, 256, 32), ap(A_DT, 256, 32), -1.0),
                        deps=[t2a], inc=True)
                    cs = None
                    for c in range(4):
                        cs = p.emit("vector", lambda e, c=c: e.tensor_tensor_scan(
                            ap(A_ACS + c * 64, 64, 32), ap(M_ONES, 64, 32),
                            ap(A_ACS + c * 64, 64, 32), 0.0, ALU.mult, ALU.add),
                            deps=[t2a, t_ones, cs], inc=True)
                    dt_tok = [t2b, cs]

            s_dep = list(s_toks)
            cg = {"xt": [prev_mm[0]], "y": [prev_mm[0]], "g": [],
                  "xh": [], "zt": [], "decbc": [], "y3last": None}
            y3_toks = [None] * DT
            for c in range(nchunks):
                acs_c = ap(A_ACS + c * 64, 64, 32)
                ta = p.emit("tensor", lambda e, acs_c=acs_c: e.transpose(
                    ps[6][0:64, 0:32], acs_c, i32t[:, :]),
                    deps=[dt_tok[1], psum_guard[6]], inc=True)
                tb = p.emit("tensor", lambda e, c=c: e.transpose(
                    ps[6][0:64, 64:96], ap(A_DT + c * 64, 64, 32), i32t[:, :]),
                    deps=[dt_tok[0]], inc=True)
                te1 = p.emit("vector", lambda e: e.tensor_copy(
                    ap(M_AT, 32, 64), ps[6][0:64, 0:32]),
                    deps=[ta] + cg["g"], inc=True)
                te2 = p.emit("vector", lambda e: e.tensor_copy(
                    ap(M_DTT, 32, 64), ps[6][0:64, 64:96]), deps=[tb], inc=True)
                psum_guard[6] = te2
                if own:
                    tex = p.emit("scalar", lambda e: e.activation(
                        out=ap(M_EXPA, 32, 64), in_=ap(M_AT, 32, 64),
                        func=AF.Exp), deps=[te1], inc=True)
                tx2 = p.emit("vector", lambda e, acs_c=acs_c: e.tensor_scalar(
                    ap(M_ALB, 32, 32), i32t[:, :], acs_c[:, 63:64],
                    None, ALU.mult), deps=[dt_tok[1], te2], inc=True)
                tdm = p.emit("tensor", lambda e: e.matmul(
                    ps[6][:, 0:32], ones32t[:, :], ap(M_ALB, 32, 32),
                    start=True, stop=True), deps=[tx2, psum_guard[6]], inc=True)
                td1 = p.emit("vector", lambda e: e.tensor_sub(
                    ap(M_DECT, 32, 64), ps[6][0:64, 0:32], ap(M_AT, 32, 64)),
                    deps=[tdm, te1], inc=True)
                td2 = p.emit("scalar", lambda e: e.activation(
                    out=ap(M_DECT, 32, 64), in_=ap(M_DECT, 32, 64),
                    func=AF.Exp), deps=[td1], inc=True)
                tdb = p.emit("scalar", lambda e: e.activation(
                    out=ap(M_DECBC, 32), in_=ps[6][:, 0:32], func=AF.Exp),
                    deps=[tdm] + cg["decbc"], inc=True)
                psum_guard[6] = tdb
                xt_toks = []
                for dt in range(DT):
                    bi = dt // 4
                    tt = p.emit("tensor", lambda e, dt=dt, c=c, bi=bi: e.transpose(
                        ps[bi][0:64, (dt % 4) * 128:(dt % 4) * 128 + 128],
                        un[:, A_CONV + dt * 256 + c * 64:
                           A_CONV + dt * 256 + c * 64 + 64],
                        identt[:, :]),
                        deps=[conv_toks[dt]]
                        + ([psum_guard[bi]] if dt % 4 == 0 else []),
                        inc=(dt % 4 == 3))
                    if dt % 4 == 3:
                        ev = p.emit("vector", lambda e, bi=bi: e.tensor_copy(
                            ap(A_XT + bi * 512, 512, 64), ps[bi][0:64, 0:512]),
                            deps=[tt] + cg["xt"], inc=True)
                        psum_guard[bi] = ev
                        xt_toks.append(ev)
                tbt = None
                for i in range(2):
                    tbt = p.emit("tensor", lambda e, i=i, c=c: e.transpose(
                        ps[4][0:64, i * 128:(i + 1) * 128],
                        un[:, A_CONV + (16 + i) * 256 + c * 64:
                           A_CONV + (16 + i) * 256 + c * 64 + 64],
                        identt[:, :]),
                        deps=[conv_toks[16 + i]]
                        + ([psum_guard[4]] if i == 0 else []),
                        inc=(i == 1))
                tbe = p.emit("vector", lambda e: e.tensor_copy(
                    ap(M_BTM, 256, 64), ps[4][0:64, 0:256]),
                    deps=[tbt] + s_dep, inc=True)
                psum_guard[4] = tbe
                ty0 = None
                if own:
                    ty0 = p.emit("vector", lambda e: e.tensor_tensor(
                        ap(A_Y, 2048, 64).rearrange("p (h w) -> p h w", h=HEADS),
                        ap(A_XT, 2048, 64).rearrange("p (h w) -> p h w", h=HEADS),
                        dskipp[0:64, :, None].broadcast_to([64, HEADS, HDIM]),
                        ALU.mult), deps=xt_toks + cg["y"], inc=True)
                    tcb = None
                    for i in range(2):
                        tcb = p.emit("tensor", lambda e, i=i, c=c: e.matmul(
                            ps[4][0:64, 256:320],
                            un[:, A_CONV + (16 + i) * 256 + c * 64:
                               A_CONV + (16 + i) * 256 + c * 64 + 64],
                            un[:, A_CONV + (18 + i) * 256 + c * 64:
                               A_CONV + (18 + i) * 256 + c * 64 + 64],
                            start=(i == 0), stop=(i == 1)),
                            deps=[conv_toks[16 + i], conv_toks[18 + i], tbe],
                            inc=(i == 1))
                    tcbe = p.emit("vector", lambda e: e.tensor_mul(
                        ap(A_CBT, 64, 64), ps[4][0:64, 256:320],
                        masktt[0:64, 0:64]), deps=[tcb], inc=True)
                    psum_guard[4] = tcbe
                    txh = p.emit("vector", lambda e, acs_c=acs_c: e.tensor_tensor(
                        ap(A_XH, 2048, 32).rearrange("p (h l) -> p h l", h=HEADS),
                        i32t[:, :, None].broadcast_to([32, 32, 64]),
                        acs_c[:, None, :].broadcast_to([32, 32, 64]),
                        ALU.mult), deps=[dt_tok[1]] + cg["xh"], inc=True)
                    g_toks = []
                    for j in range(4):
                        tgm = p.emit("tensor", lambda e, j=j: e.matmul(
                            ps[j][0:64, 0:512], ones32t[:, 0:64],
                            ap(A_XH + j * 512, 512, 32),
                            start=True, stop=True),
                            deps=[txh, psum_guard[j]], inc=True)
                        tgs = p.emit("vector", lambda e, j=j: e.tensor_tensor(
                            ap(A_G + j * 512, 512, 64).rearrange(
                                "p (h l) -> p h l", h=8),
                            ps[j][0:64, 0:512].rearrange("p (h l) -> p h l", h=8),
                            ap(M_AT, 32, 64)[:, j * 8:(j + 1) * 8, None]
                            .broadcast_to([64, 8, 64]),
                            ALU.subtract), deps=[tgm, te1] + cg["g"], inc=True)
                        psum_guard[j] = tgs
                        g_toks.append(tgs)
                    tgc = p.emit("vector", lambda e: e.tensor_scalar_min(
                        ap(A_G, 2048, 64), ap(A_G, 2048, 64), 0.0),
                        deps=g_toks, inc=True)
                    tge = p.emit("scalar", lambda e: e.activation(
                        out=ap(A_G, 2048, 64), in_=ap(A_G, 2048, 64),
                        func=AF.Exp), deps=[tgc], inc=True)
                    tgk = p.emit("vector", lambda e: e.tensor_tensor(
                        ap(A_G, 2048, 64).rearrange("p (h l) -> p h l", h=HEADS),
                        ap(A_G, 2048, 64).rearrange("p (h l) -> p h l", h=HEADS),
                        masktt[0:64, None, 0:64].broadcast_to([64, HEADS, 64]),
                        ALU.mult), deps=[tge], inc=True)
                    tgk = p.emit("vector", lambda e: e.tensor_tensor(
                        ap(A_G, 2048, 64).rearrange("p (h l) -> p h l", h=HEADS),
                        ap(A_G, 2048, 64).rearrange("p (h l) -> p h l", h=HEADS),
                        ap(A_CBT, 64, 64)[:, None, :].broadcast_to([64, HEADS, 64]),
                        ALU.mult), deps=[tgk, tcbe], inc=True)
                txt = p.emit("vector", lambda e: e.tensor_tensor(
                    ap(A_XT, 2048, 64).rearrange("p (h w) -> p h w", h=HEADS),
                    ap(A_XT, 2048, 64).rearrange("p (h w) -> p h w", h=HEADS),
                    ap(M_DTT, 32, 64)[:, :, None].broadcast_to([64, HEADS, HDIM]),
                    ALU.mult),
                    deps=xt_toks + [te2] + ([ty0] if own else []), inc=True)
                yd_toks = [None] * 4
                yo_toks = [None] * 4
                y_toks_c = []
                if own:
                    for h in range(HEADS):
                        bi = h // 8
                        off = (h % 8) * 64
                        td = p.emit("tensor", lambda e, h=h, bi=bi, off=off: e.matmul(
                            ps[bi][0:64, off:off + 64],
                            ap(A_G + h * 64, 64, 64),
                            ap(A_XT + h * 64, 64, 64),
                            start=True, stop=True),
                            deps=[tgk, txt]
                            + ([psum_guard[bi]] if h % 8 == 0 else []),
                            inc=(h % 8 == 7))
                        if h % 8 == 7:
                            yd_toks[bi] = td
                        to = None
                        for i in range(2):
                            to = p.emit("tensor", lambda e, h=h, bi=bi, off=off,
                                        i=i, c=c: e.matmul(
                                ps[4 + bi][0:64, off:off + 64],
                                un[:, A_CONV + (18 + i) * 256 + c * 64:
                                   A_CONV + (18 + i) * 256 + c * 64 + 64],
                                un[:, A_S + i * 2048 + h * 64:
                                   A_S + i * 2048 + h * 64 + 64],
                                start=(i == 0), stop=(i == 1)),
                                deps=[conv_toks[18 + i]] + s_dep
                                + ([psum_guard[4 + bi]] if h % 8 == 0 and i == 0 else []),
                                inc=(h % 8 == 7 and i == 1))
                        if h % 8 == 7:
                            yo_toks[bi] = to
                    prev_t1r = None
                    for bi in range(4):
                        t1 = p.emit("vector", lambda e, bi=bi: e.tensor_tensor(
                            ap(A_XH, 512, 64).rearrange("p (h l) -> p h l", h=8),
                            ps[4 + bi][0:64, 0:512].rearrange("p (h l) -> p h l", h=8),
                            ap(M_EXPA, 32, 64)[:, bi * 8:(bi + 1) * 8, None]
                            .broadcast_to([64, 8, 64]), ALU.mult),
                            deps=[yo_toks[bi], tex, prev_t1r, txh], inc=True)
                        psum_guard[4 + bi] = t1
                        t2_ = p.emit("gpsimd", lambda e, bi=bi: e.tensor_add(
                            ap(A_Y + bi * 512, 512, 64),
                            ap(A_Y + bi * 512, 512, 64),
                            ap(A_XH, 512, 64)), deps=[t1, ty0], inc=True)
                        prev_t1r = t2_
                        t3 = p.emit("vector", lambda e, bi=bi: e.tensor_add(
                            ap(A_Y + bi * 512, 512, 64),
                            ap(A_Y + bi * 512, 512, 64),
                            ps[bi][0:64, 0:512]), deps=[t2_, yd_toks[bi]], inc=True)
                        psum_guard[bi] = t3
                        y_toks_c.append(t3)
                    cg["xh"] = [prev_t1r, y_toks_c[-1]]
                txtd = p.emit("vector", lambda e: e.tensor_tensor(
                    ap(A_XT, 2048, 64).rearrange("p (h w) -> p h w", h=HEADS),
                    ap(A_XT, 2048, 64).rearrange("p (h w) -> p h w", h=HEADS),
                    ap(M_DECT, 32, 64)[:, :, None].broadcast_to([64, HEADS, HDIM]),
                    ALU.mult),
                    deps=[txt, td2] + (yd_toks if own else []), inc=True)
                new_s = []
                tsm_last = None
                for nt2 in range(2):
                    for j in range(4):
                        bi = 6 + ((nt2 * 4 + j) % 2)
                        tsm = p.emit("tensor", lambda e, nt2=nt2, j=j, bi=bi: e.matmul(
                            ps[bi][:, 0:512], ap(M_BTM + nt2 * 128, 128, 64),
                            ap(A_XT + j * 512, 512, 64),
                            start=True, stop=True),
                            deps=[txtd, tbe, psum_guard[bi]], inc=True)
                        tsm_last = tsm
                        sslice = un[:, A_S + nt2 * 2048 + j * 512:
                                    A_S + nt2 * 2048 + j * 512 + 512]
                        tu1 = p.emit("vector", lambda e, sslice=sslice, j=j: e.tensor_tensor(
                            sslice.rearrange("p (h w) -> p h w", h=8),
                            sslice.rearrange("p (h w) -> p h w", h=8),
                            ap(M_DECBC, 32)[:, j * 8:(j + 1) * 8, None]
                            .broadcast_to([128, 8, HDIM]), ALU.mult),
                            deps=[tsm, tdb] + s_dep
                            + (yo_toks if own else []), inc=True)
                        tu2 = p.emit("vector", lambda e, sslice=sslice, bi=bi: e.tensor_add(
                            sslice, sslice, ps[bi][:, 0:512]),
                            deps=[tu1], inc=True)
                        psum_guard[bi] = tu2
                        new_s.append(tu2)
                s_dep = new_s
                cg["xt"] = [tsm_last]
                cg["decbc"] = new_s[-1:]
                if upto == "l1a_sdump" and q == 0 and phase == 0:
                    for j2 in range(4):
                        tdmp = p.emit("gpsimd", lambda e, j2=j2, c=c: e.dma_start(
                            out=hbm["dbg"][:, :].rearrange(
                                "(t p) l -> p t l", p=128)[:, c * 4 + j2, :],
                            in_=un[:, A_S + j2 * 1024:A_S + (j2 + 1) * 1024]),
                            deps=new_s, dma=True)
                        s_dep = s_dep + [tdmp]
                if upto in ("l1a_sdump", "l1a_pre1c1") and q == 0 and phase == 0 \
                        and c == 0:
                    d1 = p.emit("gpsimd", lambda e: e.dma_start(
                        out=hbm["out"][0:64, :], in_=ap(A_XT, 2048, 64)),
                        deps=[txtd, tsm_last], dma=True)
                    d2 = p.emit("gpsimd", lambda e: e.dma_start(
                        out=hbm["out"][64:128, 0:256], in_=ap(M_BTM, 256, 64)),
                        deps=[tbe, tsm_last], dma=True)
                    d3 = p.emit("gpsimd", lambda e: e.dma_start(
                        out=hbm["out"][64:128, 256:288], in_=ap(M_DECT, 32, 64)),
                        deps=[td2, txtd], dma=True)
                    d4 = p.emit("gpsimd", lambda e: e.dma_start(
                        out=hbm["out"][128:256, 0:32], in_=ap(M_DECBC, 32)),
                        deps=[tdb] + new_s, dma=True)
                    d5 = p.emit("gpsimd", lambda e: e.dma_start(
                        out=hbm["out"][256:288, 0:64], in_=ap(A_ACS, 64, 32)),
                        deps=[td2], dma=True)
                    d6 = p.emit("gpsimd", lambda e: e.dma_start(
                        out=hbm["out"][288:352, 0:32], in_=ap(M_AT, 32, 64)),
                        deps=[td2], dma=True)
                    d7 = p.emit("gpsimd", lambda e: e.dma_start(
                        out=hbm["out"][352:384, 0:32], in_=ap(M_ALB, 32, 32)),
                        deps=[tdm], dma=True)
                    p.emit("gpsimd", lambda e: e.nop(),
                           deps=[d1, d2, d3, d4, d5, d6, d7])
                    s_dep = s_dep + [d1, d2, d3, d4, d5, d6, d7]
                if own:
                    zt_toks = []
                    for dt in range(DT):
                        gidx = dt // 4
                        bi = 4 + gidx % 2
                        tt = p.emit("tensor", lambda e, dt=dt, c=c, bi=bi: e.transpose(
                            ps[bi][0:64, (dt % 4) * 128:(dt % 4) * 128 + 128],
                            un[:, A_Z + dt * 256 + c * 64:
                               A_Z + dt * 256 + c * 64 + 64],
                            identt[:, :]),
                            deps=[z_toks[dt]]
                            + ([psum_guard[bi]] if dt % 4 == 0 else []),
                            inc=(dt % 4 == 3))
                        if dt % 4 == 3:
                            ev = p.emit("scalar", lambda e, gidx=gidx, bi=bi: e.activation(
                                out=ap(A_ZT + gidx * 512, 512, 64),
                                in_=ps[bi][0:64, 0:512], func=AF.Silu),
                                deps=[tt] + cg["zt"], inc=True)
                            psum_guard[bi] = ev
                            zt_toks.append(ev)
                    tyz = p.emit("vector", lambda e: e.tensor_mul(
                        ap(A_Y, 2048, 64), ap(A_Y, 2048, 64),
                        ap(A_ZT, 2048, 64)),
                        deps=zt_toks + y_toks_c, inc=True)
                    cg["zt"] = [tyz]
                    tss = p.emit("scalar", lambda e: e.activation(
                        out=ap(A_G, 2048, 64), in_=ap(A_Y, 2048, 64),
                        func=AF.Square, accum_out=ap(M_RSTD, 1, 64)),
                        deps=[tyz], inc=True)
                    cg["g"] = [tss]
                    trs = p.emit("scalar", lambda e: e.activation(
                        out=ap(M_RSTD, 1, 64), in_=ap(M_RSTD, 1, 64),
                        func=AF.Sqrt, scale=1.0 / D, bias=eps_c[0:64, :]),
                        deps=[tss, t_eps], inc=True)
                    trs = p.emit("vector", lambda e: e.reciprocal(
                        ap(M_RSTD, 1, 64), ap(M_RSTD, 1, 64)),
                        deps=[trs], inc=True)
                    tyn = p.emit("vector", lambda e: e.tensor_scalar(
                        ap(A_Y, 2048, 64), ap(A_Y, 2048, 64),
                        ap(M_RSTD, 1, 64), None, ALU.mult),
                        deps=[trs], inc=True)
                    lastev = None
                    for dt in range(DT):
                        bi = dt // 4
                        tt = p.emit("tensor", lambda e, dt=dt, bi=bi: e.transpose(
                            ps[bi][:, (dt % 4) * 64:(dt % 4) * 64 + 64],
                            ap(A_Y + dt * 128, 128, 64), identt[0:64, 0:64]),
                            deps=[tyn]
                            + ([psum_guard[bi]] if dt % 4 == 0 else []),
                            inc=(dt % 4 == 3))
                        if dt % 4 == 3:
                            for d2 in range(dt - 3, dt + 1):
                                lastev = p.emit("vector", lambda e, d2=d2, bi=bi, c=c:
                                    e.tensor_scalar(
                                        un[:, A_Y3 + d2 * 256 + c * 64:
                                           A_Y3 + d2 * 256 + c * 64 + 64],
                                        ps[bi][:, (d2 % 4) * 64:(d2 % 4) * 64 + 64],
                                        gnw_c(d2), None, ALU.mult),
                                    deps=[tt], inc=True)
                                y3_toks[d2] = lastev
                            psum_guard[bi] = lastev
                    cg["y"] = [lastev]

            if not own:
                return s_dep
            o1_toks = [None] * DT
            for m in range(DT):
                bank = 6 + (m % 2)
                t_w2 = p.emit("sync", lambda e, m=m: e.dma_start(
                    out=wslot.rearrange("p (k c) -> p k c", k=DT),
                    in_=hbm["wout"][:, m * 128:(m + 1) * 128].rearrange(
                        "(k p) c -> p k c", p=128)),
                    deps=[prev_mm[0]], dma=True)
                mm = None
                for k in range(DT):
                    mm = p.emit("tensor", lambda e, k=k, bank=bank: e.matmul(
                        ps[bank][:, 0:256], wslot[:, k * 128:(k + 1) * 128],
                        un[:, A_Y3 + k * 256:A_Y3 + k * 256 + 256],
                        start=(k == 0), stop=(k == DT - 1)),
                        deps=[t_w2, psum_guard[bank], y3_toks[k]],
                        inc=(k == DT - 1))
                prev_mm[0] = mm
                o1_toks[m] = p.emit("vector", lambda e, m=m, bank=bank: e.tensor_copy(
                    ap(A_CONV + m * 256, 256), ps[bank][:, 0:256]),
                    deps=[mm], inc=True)
                psum_guard[bank] = o1_toks[m]
            x3_toks = [None] * DT
            for m in range(DT):
                bank = 6 + (m % 2)
                t_w2 = p.emit("sync", lambda e, m=m: e.dma_start(
                    out=wslot.rearrange("p (k c) -> p k c", k=DT),
                    in_=hbm["wo"][:, m * 128:(m + 1) * 128].rearrange(
                        "(k p) c -> p k c", p=128)),
                    deps=[prev_mm[0]], dma=True)
                mm = None
                for k in range(DT):
                    mm = p.emit("tensor", lambda e, k=k, bank=bank: e.matmul(
                        ps[bank][:, 0:256], wslot[:, k * 128:(k + 1) * 128],
                        ap(A_CONV + k * 256, 256),
                        start=(k == 0), stop=(k == DT - 1)),
                        deps=[t_w2, psum_guard[bank], o1_toks[k]],
                        inc=(k == DT - 1))
                prev_mm[0] = mm
                x3_toks[m] = p.emit("vector", lambda e, m=m, bank=bank: e.tensor_add(
                    xsl(m), xsl(m), ps[bank][:, 0:256]), deps=[mm], inc=True)
                psum_guard[bank] = x3_toks[m]
            return ln_fm(256, xsl, x3_toks, None, None, xsl)

        S_X3, S_XG, S_H = 0, 16384, 20480
        S_GT, S_G, S_WD, S_WGU, S_RT = 22016, 22528, 24576, 25600, 29696
        CAP = 256
        c_tm_ap = lambda: un[:, S_RT:S_RT + 128]
        sel_ap = lambda: un[:, S_RT + 128:S_RT + 256]
        postm_ap = lambda: un[:, S_RT + 256:S_RT + 384]
        cg_ap = un[:, S_RT + 384:S_RT + 386]

        def moe_phase():
            prev_rt = [None]
            out_dma = []
            mm = None
            for ts in range(8):
                for k in range(DT):
                    mm = p.emit("tensor", lambda e, ts=ts, k=k: e.matmul(
                        ps[6][0:128, ts * 16:ts * 16 + 16],
                        x_fm[:, k * HALF + ts * 128: k * HALF + ts * 128 + 128],
                        routerp[:, k * 16:(k + 1) * 16],
                        start=(k == 0), stop=(k == DT - 1)),
                        deps=[psum_guard[6]] if ts == 0 and k == 0 else [],
                        inc=(k == DT - 1))
            tsc = p.emit("vector", lambda e: e.tensor_copy(
                un[:, S_RT + 512:S_RT + 640], ps[6][:, 0:128]),
                deps=[mm], inc=True)
            psum_guard[6] = tsc
            for ts in range(8):
                sc = un[:, S_RT + 512 + ts * 16:S_RT + 512 + ts * 16 + 16]
                cb = un[:, S_RT + ts * 16:S_RT + ts * 16 + 16]
                sel = un[:, S_RT + 128 + ts * 16:S_RT + 128 + ts * 16 + 16]
                m1 = un[:, S_RT + 400:S_RT + 401]
                nm1 = un[:, S_RT + 401:S_RT + 402]
                m2 = un[:, S_RT + 402:S_RT + 403]
                den = un[:, S_RT + 403:S_RT + 404]
                msk = un[:, S_RT + 404:S_RT + 420]
                t1 = p.emit("vector", lambda e, sc=sc: e.reduce_max(
                    m1, sc, AX.X), deps=[tsc, prev_rt[0]], inc=True)
                t2 = p.emit("vector", lambda e: e.tensor_scalar_mul(
                    nm1, m1, -1.0), deps=[t1], inc=True)
                t3 = p.emit("scalar", lambda e, sc=sc, cb=cb: e.activation(
                    out=cb, in_=sc, func=AF.Exp, bias=nm1), deps=[t2], inc=True)
                t4 = p.emit("vector", lambda e, sc=sc, sel=sel: e.tensor_scalar(
                    sel, sc, m1, None, ALU.is_equal), deps=[t1], inc=True)
                t5 = p.emit("vector", lambda e, sc=sc, sel=sel: e.scalar_tensor_tensor(
                    msk, sel, -1e30, sc, ALU.mult, ALU.add), deps=[t4], inc=True)
                t6 = p.emit("vector", lambda e: e.reduce_max(
                    m2, msk, AX.X), deps=[t5], inc=True)
                t7 = p.emit("vector", lambda e: e.tensor_scalar(
                    msk, msk, m2, None, ALU.is_equal), deps=[t6], inc=True)
                t8 = p.emit("vector", lambda e, sel=sel: e.tensor_add(
                    sel, sel, msk), deps=[t7, t4], inc=True)
                t9 = p.emit("vector", lambda e, cb=cb, sel=sel: e.tensor_mul(
                    cb, cb, sel), deps=[t8, t3], inc=True)
                ta_ = p.emit("vector", lambda e, cb=cb: e.reduce_sum(
                    den, cb, AX.X), deps=[t9], inc=True)
                tb_ = p.emit("vector", lambda e: e.reciprocal(
                    den, den), deps=[ta_], inc=True)
                tc_ = p.emit("vector", lambda e, cb=cb: e.tensor_scalar(
                    cb, cb, den, None, ALU.mult), deps=[tb_], inc=True)
                prev_rt[0] = tc_
            t_on = p.emit("vector", lambda e: e.memset(
                un[0:16, 2048:3072], 1.0), inc=True)
            tsel = None
            for ts in range(8):
                bk = 6 + ts // 4
                tt = p.emit("tensor", lambda e, ts=ts, bk=bk: e.transpose(
                    ps[bk][0:16, (ts % 4) * 128:(ts % 4) * 128 + 128],
                    un[:, S_RT + 128 + ts * 16:S_RT + 128 + ts * 16 + 16],
                    identt[:, :]),
                    deps=[prev_rt[0]]
                    + ([psum_guard[bk]] if ts % 4 == 0 else []),
                    inc=(ts == 7))
            tsel = p.emit("vector", lambda e: e.tensor_copy(
                un[0:16, 0:512], ps[6][0:16, 0:512]), deps=[tt], inc=True)
            tsel = p.emit("vector", lambda e: e.tensor_copy(
                un[0:16, 512:1024], ps[7][0:16, 0:512]), deps=[tt, tsel], inc=True)
            psum_guard[6] = tsel
            psum_guard[7] = tsel
            tpos = p.emit("vector", lambda e: e.tensor_tensor_scan(
                un[0:16, 1024:2048], un[0:16, 2048:3072], un[0:16, 0:1024],
                0.0, ALU.mult, ALU.add), deps=[tsel, t_on], inc=True)
            tpos = p.emit("vector", lambda e: e.tensor_mul(
                un[0:16, 1024:2048], un[0:16, 1024:2048], un[0:16, 0:1024]),
                deps=[tpos], inc=True)
            tpt = None
            for ts in range(8):
                tt2 = p.emit("tensor", lambda e, ts=ts: e.transpose(
                    ps[6][0:128, ts * 16:ts * 16 + 16],
                    un[0:16, 1024 + ts * 128:1024 + ts * 128 + 128],
                    identt[0:16, 0:16]),
                    deps=[tpos] + ([psum_guard[6]] if ts == 0 else []),
                    inc=(ts == 7))
            tpt = p.emit("vector", lambda e: e.tensor_copy(
                postm_ap(), ps[6][:, 0:128]), deps=[tt2], inc=True)
            psum_guard[6] = tpt
            p.barrier()
            x3_toks = [None] * 8
            for sl in range(8):
                tt = None
                for dt in range(DT):
                    bi = dt // 4
                    tt = p.emit("tensor", lambda e, sl=sl, dt=dt, bi=bi: e.transpose(
                        ps[bi][:, (dt % 4) * 128:(dt % 4) * 128 + 128],
                        x_fm[:, dt * HALF + sl * 128: dt * HALF + sl * 128 + 128],
                        identt[:, :]),
                        deps=([psum_guard[bi]] if dt % 4 == 0 else []),
                        inc=(dt % 4 == 3))
                    if dt % 4 == 3:
                        ev = p.emit("vector", lambda e, sl=sl, bi=bi: e.tensor_copy(
                            un[:, S_X3 + sl * 2048 + bi * 512:
                               S_X3 + sl * 2048 + bi * 512 + 512],
                            ps[bi][:, 0:512]),
                            deps=[tt] + ([tpt, tsel] if sl == 0 and bi == 0 else []),
                            inc=True)
                        psum_guard[bi] = ev
                x3_toks[sl] = ev
            p.barrier()

            gprev = [None]
            xg_guard = [None]
            h_guard = [None]
            ds_guard = [None]
            xfm_guard = {}
            wgu_tok = [None, None]
            wd_tok = [None, None]
            for ex in range(EXPERTS):
                rls = []
                for ts in range(8):
                    rls.append(p.emit("sync", lambda e, ex=ex, ts=ts: e.dma_start(
                        out=bcb[0:1, 128 + ts * 128:128 + ts * 128 + 128],
                        in_=postm_ap()[:, ts * 16 + ex:ts * 16 + ex + 1]),
                        deps=[gprev[0]], dma=True))
                tgm1 = p.emit("tensor", lambda e: e.matmul(
                    ps[6][:, 0:512], onesr_ap, bcb[0:1, 128:640],
                    start=True, stop=True),
                    deps=rls + [t_or, psum_guard[6]], inc=True)
                tgm2 = p.emit("tensor", lambda e: e.matmul(
                    ps[7][:, 0:512], onesr_ap, bcb[0:1, 640:1152],
                    start=True, stop=True), deps=rls + [psum_guard[7]], inc=True)
                gtoks = []
                for gi in range(2):
                    for half2 in range(2):
                        tgx = p.emit("vector", lambda e, gi=gi, half2=half2: e.tensor_scalar(
                            un[:, S_G + gi * 1024 + half2 * 512:
                               S_G + gi * 1024 + half2 * 512 + 512],
                            ps[6 + half2][:, 0:512], iotac[:, gi:gi + 1],
                            None, ALU.is_equal),
                            deps=[tgm1 if half2 == 0 else tgm2, gprev[0]], inc=True)
                        gtoks.append(tgx)
                psum_guard[6] = gtoks[-1]
                psum_guard[7] = gtoks[-1]
                gt_toks = [gprev[0], gprev[0]]
                xg_toks = []
                for wave in range(2):
                    gm = None
                    for sl in range(8):
                        gts = p.emit("vector", lambda e, sl=sl, ex=ex: e.tensor_tensor(
                            un[:, S_GT + (sl % 2) * 256:S_GT + (sl % 2) * 256 + 256],
                            postm_ap()[:, sl * 16 + ex:sl * 16 + ex + 1]
                            .broadcast_to([128, 256]),
                            iotabc[:, :], ALU.is_equal),
                            deps=[tpt, gt_toks[sl % 2]], inc=True)
                        for dtl in range(8):
                            dt = wave * 8 + dtl
                            gm = p.emit("tensor", lambda e, sl=sl, dt=dt, dtl=dtl: e.matmul(
                                ps[dtl][:, 0:256],
                                un[:, S_X3 + sl * 2048 + dt * 128:
                                   S_X3 + sl * 2048 + dt * 128 + 128],
                                un[:, S_GT + (sl % 2) * 256:S_GT + (sl % 2) * 256 + 256],
                                start=(sl == 0), stop=(sl == 7)),
                                deps=[gts, x3_toks[sl]]
                                + ([psum_guard[dtl]] if sl == 0 else []),
                                inc=(dtl == 7))
                        gt_toks[sl % 2] = gm
                    for dtl in range(8):
                        dt = wave * 8 + dtl
                        ev = p.emit("vector", lambda e, dt=dt, dtl=dtl: e.tensor_copy(
                            un[:, S_XG + dt * 256:S_XG + dt * 256 + 256],
                            ps[dtl][:, 0:256]), deps=[gm, xg_guard[0]], inc=True)
                        psum_guard[dtl] = ev
                        xg_toks.append(ev)
                cgm = None
                cg_gt = [gm, gm]
                for sl in range(8):
                    gts2 = p.emit("vector", lambda e, sl=sl, ex=ex: e.tensor_tensor(
                        un[:, S_GT + (sl % 2) * 256:S_GT + (sl % 2) * 256 + 256],
                        postm_ap()[:, sl * 16 + ex:sl * 16 + ex + 1]
                        .broadcast_to([128, 256]),
                        iotabc[:, :], ALU.is_equal),
                        deps=[tpt, cg_gt[sl % 2]], inc=True)
                    for st in range(2):
                        cgm = p.emit("tensor", lambda e, sl=sl, st=st, ex=ex: e.matmul(
                            ps[6 + st][0:128, 0:1],
                            un[:, S_GT + (sl % 2) * 256 + st * 128:
                               S_GT + (sl % 2) * 256 + st * 128 + 128],
                            c_tm_ap()[:, sl * 16 + ex:sl * 16 + ex + 1],
                            start=(sl == 0), stop=(sl == 7)),
                            deps=[gts2, prev_rt[0]]
                            + ([psum_guard[6 + st]] if sl == 0 else []),
                            inc=True)
                    cg_gt[sl % 2] = cgm
                tcg = p.emit("vector", lambda e: e.tensor_copy(
                    cg_ap[:, 0:1], ps[6][:, 0:1]), deps=[cgm], inc=True)
                tcg = p.emit("vector", lambda e: e.tensor_copy(
                    cg_ap[:, 1:2], ps[7][:, 0:1]), deps=[cgm, tcg], inc=True)
                psum_guard[6] = tcg
                psum_guard[7] = tcg
                h_toks = [None] * FT
                for ft in range(FT):
                    slot = ft % 2
                    twg = p.emit("sync", lambda e, ex=ex, ft=ft, slot=slot: e.dma_start(
                        out=un[:, S_WGU + slot * 2048:S_WGU + slot * 2048 + 2048]
                        .rearrange("p (k c) -> p k c", k=DT),
                        in_=hbm["gw"][ex, :, ft * 128:(ft + 1) * 128].rearrange(
                            "(k p) c -> p k c", p=128)),
                        deps=[wgu_tok[slot]], dma=True)
                    mmg = None
                    for k in range(DT):
                        mmg = p.emit("tensor", lambda e, k=k, slot=slot: e.matmul(
                            ps[6][:, 0:256],
                            un[:, S_WGU + slot * 2048 + k * 128:
                               S_WGU + slot * 2048 + k * 128 + 128],
                            un[:, S_XG + k * 256:S_XG + k * 256 + 256],
                            start=(k == 0), stop=(k == DT - 1)),
                            deps=[twg, psum_guard[6]] + xg_toks, inc=(k == DT - 1))
                    twu = p.emit("sync", lambda e, ex=ex, ft=ft, slot=slot: e.dma_start(
                        out=un[:, S_WGU + slot * 2048:S_WGU + slot * 2048 + 2048]
                        .rearrange("p (k c) -> p k c", k=DT),
                        in_=hbm["uw"][ex, :, ft * 128:(ft + 1) * 128].rearrange(
                            "(k p) c -> p k c", p=128)),
                        deps=[mmg], dma=True)
                    mmu = None
                    for k in range(DT):
                        mmu = p.emit("tensor", lambda e, k=k, slot=slot: e.matmul(
                            ps[7][:, 0:256],
                            un[:, S_WGU + slot * 2048 + k * 128:
                               S_WGU + slot * 2048 + k * 128 + 128],
                            un[:, S_XG + k * 256:S_XG + k * 256 + 256],
                            start=(k == 0), stop=(k == DT - 1)),
                            deps=[twu, psum_guard[7]], inc=(k == DT - 1))
                    wgu_tok[slot] = mmu
                    tg_ = p.emit("scalar", lambda e: e.activation(
                        out=un[:, S_GT:S_GT + 256], in_=ps[6][:, 0:256],
                        func=AF.Silu),
                        deps=[mmg, h_toks[ft - 1] if ft else tcg], inc=True)
                    psum_guard[6] = tg_
                    th_ = p.emit("vector", lambda e, ft=ft: e.tensor_mul(
                        un[:, S_H + ft * 256:S_H + ft * 256 + 256],
                        un[:, S_GT:S_GT + 256], ps[7][:, 0:256]),
                        deps=[tg_, mmu, h_guard[0]], inc=True)
                    psum_guard[7] = th_
                    h_toks[ft] = th_
                dmm = [None] * 8
                for dc in range(4):
                    for ft in range(FT):
                        slot = (dc * FT + ft) % 2
                        twd = p.emit("sync", lambda e, ex=ex, ft=ft, dc=dc,
                                     slot=slot: e.dma_start(
                            out=un[:, S_WD + slot * 512:S_WD + slot * 512 + 512],
                            in_=hbm["dw"][ex, ft * 128:(ft + 1) * 128,
                                          dc * 512:(dc + 1) * 512]),
                            deps=[wd_tok[slot]], dma=True)
                        for st in range(2):
                            bi = st * 4 + dc
                            dmm[bi] = p.emit("tensor", lambda e, ft=ft, dc=dc,
                                             st=st, bi=bi, slot=slot: e.matmul(
                                ps[bi][:, 0:512],
                                un[:, S_H + ft * 256 + st * 128:
                                   S_H + ft * 256 + st * 128 + 128],
                                un[:, S_WD + slot * 512:S_WD + slot * 512 + 512],
                                start=(ft == 0), stop=(ft == FT - 1)),
                                deps=[twd, h_toks[ft]]
                                + ([psum_guard[bi]] if ft == 0 else []),
                                inc=True)
                        wd_tok[slot] = dmm[st * 4 + dc]
                ds_toks = []
                for st in range(2):
                    for dc in range(4):
                        bi = st * 4 + dc
                        ev = p.emit("vector", lambda e, st=st, dc=dc, bi=bi: e.tensor_scalar(
                            un[:, S_XG + st * 2048 + dc * 512:
                               S_XG + st * 2048 + dc * 512 + 512],
                            ps[bi][:, 0:512], cg_ap[:, st:st + 1],
                            None, ALU.mult),
                            deps=[dmm[bi], tcg, ds_guard[0],
                                  h_toks[FT - 1]], inc=True)
                        psum_guard[bi] = ev
                        ds_toks.append(ev)
                sc_last = None
                for dt in range(DT):
                    for tc in range(2):
                        bi = 6 + ((dt * 2 + tc) % 2)
                        mm3 = None
                        for st in range(2):
                            mm3 = p.emit("tensor", lambda e, dt=dt, tc=tc, st=st,
                                         bi=bi: e.matmul(
                                ps[bi][:, 0:512],
                                un[:, S_XG + st * 2048 + dt * 128:
                                   S_XG + st * 2048 + dt * 128 + 128],
                                un[:, S_G + st * 1024 + tc * 512:
                                   S_G + st * 1024 + tc * 512 + 512],
                                start=(st == 0), stop=(st == 1)),
                                deps=[ds_toks[st * 4 + dt // 4],
                                      gtoks[-1], psum_guard[bi]],
                                inc=(st == 1))
                        tad = p.emit("vector", lambda e, dt=dt, tc=tc, bi=bi: e.tensor_add(
                            x_fm[:, dt * HALF + tc * 512: dt * HALF + tc * 512 + 512],
                            x_fm[:, dt * HALF + tc * 512: dt * HALF + tc * 512 + 512],
                            ps[bi][:, 0:512]),
                            deps=[mm3, xfm_guard.get((dt, tc))], inc=True)
                        psum_guard[bi] = tad
                        xfm_guard[(dt, tc)] = tad
                        sc_last = tad
                gprev[0] = sc_last
                xg_guard[0] = sc_last
                h_guard[0] = dmm[7]
                ds_guard[0] = sc_last
            p.barrier()

            for gts_ in range(8):
                tts = None
                for dt in range(DT):
                    bank = 4 + (dt // 4) % 2
                    tts = p.emit("tensor", lambda e, dt=dt, gts_=gts_,
                                 bank=bank: e.transpose(
                        ps[bank][:, (dt % 4) * 128:(dt % 4) * 128 + 128],
                        x_fm[:, dt * HALF + gts_ * 128:
                             dt * HALF + gts_ * 128 + 128],
                        identt[:, :]),
                        deps=([psum_guard[bank]] if dt % 4 == 0 else []),
                        inc=(dt % 4 == 3))
                    if dt % 4 == 3:
                        tsum = p.emit("vector", lambda e, dt=dt, bank=bank: e.tensor_copy(
                            un[:, S_XG + (dt - 3) * 128:S_XG + (dt + 1) * 128],
                            ps[bank][:, 0:512]),
                            deps=[tts, prev_rt[0]], inc=True)
                        psum_guard[bank] = tsum
                mcol = un[:, S_RT + 420:S_RT + 421]
                sscol = un[:, S_RT + 421:S_RT + 422]
                sqcol = un[:, S_RT + 422:S_RT + 423]
                t1 = p.emit("vector", lambda e: e.reduce_sum(
                    mcol, un[:, S_XG:S_XG + 2048], AX.X),
                    deps=[tsum, prev_rt[0]], inc=True)
                t2 = p.emit("vector", lambda e: e.tensor_scalar_mul(
                    mcol, mcol, 1.0 / D), deps=[t1], inc=True)
                tsq = p.emit("scalar", lambda e: e.activation(
                    out=un[:, S_H:S_H + 2048], in_=un[:, S_XG:S_XG + 2048],
                    func=AF.Square, accum_out=sscol),
                    deps=[t2], inc=True)
                t4 = p.emit("vector", lambda e: e.tensor_scalar_mul(
                    sscol, sscol, 1.0 / D), deps=[tsq], inc=True)
                t5 = p.emit("vector", lambda e: e.tensor_mul(
                    sqcol, mcol, mcol), deps=[t2, t4], inc=True)
                t6 = p.emit("vector", lambda e: e.tensor_sub(
                    sscol, sscol, sqcol), deps=[t5], inc=True)
                t7 = p.emit("scalar", lambda e: e.activation(
                    out=sscol, in_=sscol, func=AF.Sqrt, bias=eps_c),
                    deps=[t6, t_eps], inc=True)
                t8 = p.emit("vector", lambda e: e.reciprocal(
                    sscol, sscol), deps=[t7], inc=True)
                t9 = p.emit("vector", lambda e: e.tensor_scalar(
                    un[:, S_XG:S_XG + 2048], un[:, S_XG:S_XG + 2048],
                    mcol, sscol, ALU.subtract, ALU.mult),
                    deps=[t8], inc=True)
                td_ = p.emit("gpsimd", lambda e, gts_=gts_: e.dma_start(
                    out=hbm["out"][gts_ * 128:(gts_ + 1) * 128, :],
                    in_=un[:, S_XG:S_XG + 2048]), deps=[t9], dma=True)
                out_dma.append(td_)
                prev_rt[0] = td_
            return out_dma

        for col in range(DT * NST):
            hcar_dep[col] = t_hz

        def load_x(src, c0):
            return p.emit("sync", lambda e, c0=c0, src=src: e.dma_start(
                out=x_fm[:, :].rearrange("p (t c) -> p t c", t=DT)[:, :, c0:c0 + 512],
                in_=hbm[src][:, :].rearrange("(t p) l -> p t l", p=128)[:, :, c0:c0 + 512]),
                dma=True)

        out_toks = []

        def dbg_dump_fm(ap_fn, toks):
            for dt in range(DT):
                out_toks.append(p.emit("gpsimd", lambda e, dt=dt: e.dma_start(
                    out=hbm["dbg"][:, :].rearrange("(t p) l -> p t l", p=128)[:, dt, :],
                    in_=ap_fn(dt)), deps=[toks[dt]], dma=True))

        def dbg_dump_hilo(toks_h):
            for half in range(2):
                c0 = half * 512
                for dt in range(DT):
                    tt = p.emit("vector", lambda e, dt=dt, c0=c0: e.tensor_add(
                        lnscr[:, 0:512], x1hi(dt, c0), x1lo(dt, c0)),
                        deps=[toks_h[half][dt]] + out_toks[-1:], inc=True)
                    out_toks.append(p.emit("gpsimd", lambda e, dt=dt, c0=c0: e.dma_start(
                        out=hbm["dbg"][:, :].rearrange("(t p) l -> p t l", p=128)[:, dt, c0:c0 + 512],
                        in_=lnscr[:, 0:512]), deps=[tt], dma=True))

        stop = False
        for phase, src in enumerate(("xpre", "xown")):
            want = "" if phase == 1 else "_pre"
            w_l0a = load_l0a_weights()
            x_toks_h = []
            for half in range(2):
                c0 = half * 512
                xt = load_x(src, c0)
                y_toks = l0a_half(c0, [xt], w_l0a)
                toks = ln_fm(512,
                             (lambda dt, c0=c0: x_fm[:, dt * HALF + c0: dt * HALF + c0 + 512]),
                             y_toks,
                             dsthi=(lambda dt, c0=c0: x1hi(dt, c0)),
                             dstlo=(lambda dt, c0=c0: x1lo(dt, c0)),
                             dst32=None)
                x_toks_h.append(toks)
                p.barrier()
            if upto == "l0a" + want:
                dbg_dump_hilo(x_toks_h)
                stop = True
                break

            for half in range(2):
                c0 = half * 512
                toks = l0ffn_chunk(c0, x_toks_h[half])
                x_toks_h[half] = ln_fm(
                    512,
                    (lambda dt, c0=c0: x_fm[:, dt * HALF + c0: dt * HALF + c0 + 512]),
                    toks, dsthi=None, dstlo=None,
                    dst32=(lambda dt, c0=c0: x_fm[:, dt * HALF + c0: dt * HALF + c0 + 512]))
                p.barrier()
            if upto == "l0ffn" + want:
                dbg_dump_fm(lambda dt: x_fm[:, dt * HALF: dt * HALF + HALF],
                            [x_toks_h[1][dt] for dt in range(DT)])
                stop = True
                break
            p.barrier()

            if phase == 0:
                t_s0 = p.emit("vector", lambda e: e.memset(
                    un[:, A_S:A_S + 4096], 0.0), inc=True)
                s_dep = [t_s0]
            else:
                t_sr = p.emit("sync", lambda e: e.dma_start(
                    out=un[:, A_S:A_S + 4096], in_=hbm["sspill"][:, :]),
                    dma=True)
                s_dep = [t_sr]
            nq = 1 if ((upto.startswith("l1a_pre1") or upto == "l1a_sdump")
                       and phase == 0) else 4
            nch = 4
            if upto.startswith("l1a_pre1c"):
                nch = int(upto[len("l1a_pre1c"):])
            for q in range(nq):
                res = l1a_quarter(phase, q, x_toks_h[q // 2], s_dep, nch)
                p.barrier()
                if phase == 0:
                    s_dep = res
                else:
                    x_toks_h[q // 2] = res
            if phase == 0:
                t_sp = p.emit("gpsimd", lambda e: e.dma_start(
                    out=hbm["sspill"][:, :], in_=un[:, A_S:A_S + 4096]),
                    deps=s_dep, dma=True)
                p.emit("gpsimd", lambda e: e.nop(), deps=[t_sp])
                p.barrier()
                if upto == "l1a_sdump":
                    stop = True
                    break
                if upto.startswith("l1a_pre"):
                    for j in range(4):
                        out_toks.append(p.emit("gpsimd", lambda e, j=j: e.dma_start(
                            out=hbm["dbg"][:, :].rearrange(
                                "(t p) l -> p t l", p=128)[:, j, :],
                            in_=un[:, A_S + j * 1024:A_S + (j + 1) * 1024]),
                            deps=s_dep, dma=True))
                    stop = True
                    break
            else:
                if upto == "l1a":
                    dbg_dump_fm(lambda dt: x_fm[:, dt * HALF: dt * HALF + HALF],
                                [x_toks_h[1][dt] for dt in range(DT)])
                    stop = True
                    break
                out_toks.extend(moe_phase())
            p.barrier()

        if out_toks:
            p.emit("gpsimd", lambda e: e.nop(), deps=list(out_toks))

        with nc.Block() as block:
            @block.sync
            def _(eng):
                _replay_one(p, "sync", eng, sems)
            @block.tensor
            def _(eng):
                _replay_one(p, "tensor", eng, sems)
            @block.vector
            def _(eng):
                _replay_one(p, "vector", eng, sems)
            @block.scalar
            def _(eng):
                _replay_one(p, "scalar", eng, sems)
            @block.gpsimd
            def _(eng):
                _replay_one(p, "gpsimd", eng, sems)

    return nc



def _u8(wq):
    a = np.ascontiguousarray(np.asarray(wq, np.int32))
    return a.view(np.uint8).reshape(a.shape + (4,))[..., 0]


def _prep_inputs(inp):
    f32 = np.float32
    x = np.ascontiguousarray(np.asarray(inp["L0_attn_x"], f32))
    def tile_col(v):
        return np.ascontiguousarray(v.reshape(DT, 128).T)
    w = {}
    w["wx_t"] = np.ascontiguousarray(np.asarray(inp["L0_attn_W_x"], f32).T)
    w["wdt_t"] = np.ascontiguousarray(np.asarray(inp["L0_attn_W_dt"], f32).T)
    w["bdt"] = tile_col(np.asarray(inp["L0_attn_b_dt"], f32))
    a0 = np.asarray(inp["L0_attn_A_log"], f32)
    w["a0"] = np.ascontiguousarray(a0.reshape(DT, 128, NST).transpose(1, 0, 2).reshape(128, DT * NST))
    w["d0"] = tile_col(np.asarray(inp["L0_attn_D"], f32))
    w["wqg_t"] = np.ascontiguousarray(_u8(inp["L0_ffn_wq_gate"]).T)
    w["wqu_t"] = np.ascontiguousarray(_u8(inp["L0_ffn_wq_up"]).T)
    w["wqd_t"] = np.ascontiguousarray(_u8(inp["L0_ffn_wq_down"]).T)
    w["sg"] = np.ascontiguousarray(np.asarray(inp["L0_ffn_s_gate"], f32).reshape(HT, 128).T)
    w["su"] = np.ascontiguousarray(np.asarray(inp["L0_ffn_s_up"], f32).reshape(HT, 128).T)
    w["sd"] = tile_col(np.asarray(inp["L0_ffn_s_down"], f32))
    w["nw"] = tile_col(np.asarray(inp["L1_attn_norm_w"], f32))
    w["win"] = np.ascontiguousarray(np.asarray(inp["L1_attn_w_in"], f32))
    cw = np.asarray(inp["L1_attn_conv_w"], f32)
    w["convw"] = np.ascontiguousarray(cw.reshape(NCT, 128, KCONV).transpose(1, 0, 2).reshape(128, NCT * KCONV))
    w["convb"] = np.ascontiguousarray(np.asarray(inp["L1_attn_conv_b"], f32).reshape(NCT, 128).T)
    pad = np.zeros((128, 1), f32); pad[:HEADS, 0] = np.asarray(inp["L1_attn_dt_bias"], f32)
    w["dtb"] = pad
    pad = np.zeros((128, 1), f32); pad[:HEADS, 0] = np.asarray(inp["L1_attn_A_log"], f32)
    w["a1"] = pad
    w["dskipbc"] = np.ascontiguousarray(np.broadcast_to(
        np.asarray(inp["L1_attn_D_skip"], f32)[None, :], (128, HEADS)))
    w["gnw"] = tile_col(np.asarray(inp["L1_attn_gnorm_w"], f32))
    w["wout"] = np.ascontiguousarray(np.asarray(inp["L1_attn_w_out"], f32))
    w["wo"] = np.ascontiguousarray(np.asarray(inp["L1_attn_wo_w"], f32))
    w["router"] = np.ascontiguousarray(np.asarray(inp["L1_ffn_router_w"], f32))
    w["gw"] = np.ascontiguousarray(np.asarray(inp["L1_ffn_gate_w"], f32))
    w["uw"] = np.ascontiguousarray(np.asarray(inp["L1_ffn_up_w"], f32))
    w["dw"] = np.ascontiguousarray(np.asarray(inp["L1_ffn_down_w"], f32))
    w["ident"] = np.eye(128, dtype=f32)
    w["onesr"] = np.ones((1, 128), f32)
    w["negr"] = -np.ones((1, 128), f32)
    w["onesc"] = np.ones((128, 1), f32)
    w["maskt"] = np.ascontiguousarray(np.triu(np.ones((64, 64), f32)))
    w["ones32"] = np.ones((32, 128), f32)
    w["i32"] = np.eye(32, dtype=f32)
    w["iotac_h"] = np.ascontiguousarray(
        np.stack([np.arange(1, 129, dtype=f32),
                  np.arange(129, 257, dtype=f32)], 1))
    w["iotabc_h"] = np.ascontiguousarray(np.broadcast_to(
        np.arange(1, 257, dtype=f32)[None, :], (128, 256)))

    zeros_half = np.zeros((D, HALF), f32)
    in_maps = []
    for c in range(8):
        b = c // 2
        odd = c % 2
        m = dict(w)
        if odd:
            m["xpre"] = np.ascontiguousarray(x[b, 0:HALF, :].T)
            m["xown"] = np.ascontiguousarray(x[b, HALF:L, :].T)
        else:
            m["xpre"] = zeros_half
            m["xown"] = np.ascontiguousarray(x[b, 0:HALF, :].T)
        in_maps.append(m)
    return in_maps


def kernel(**inputs):
    in_maps = _prep_inputs(inputs)
    nc = build_program()
    res = run_bass_kernel_spmd(nc, in_maps, core_ids=list(range(8)))
    out = np.empty((B, L, D), np.float32)
    for c in range(8):
        b, odd = c // 2, c % 2
        out[b, odd * HALF:(odd + 1) * HALF, :] = res.results[c]["out"]
    return out


# version 06 (final)
# speedup vs baseline: 1.2796x; 1.1040x over previous
import math
import numpy as np
import concourse.bass as bass
import concourse.mybir as mybir
from concourse.bass_utils import run_bass_kernel_spmd

F32 = mybir.dt.float32
BF16 = mybir.dt.bfloat16
U8 = mybir.dt.uint8
AF = mybir.ActivationFunctionType
ALU = mybir.AluOpType
AX = mybir.AxisListType

D = 2048
DT = 16
B = 4
L = 2048
HALF = 1024
NST = 16
RNK = 64
HID = 5632
HT = 44
HEADS = 32
HDIM = 64
STATE = 256
CONVD = 2560
PROJ = 4640
NZT = 16
NCT = 20
KCONV = 4
CHUNK = 64
EXPERTS = 16
FDIM = 768
FT = 6
EPS = 1e-5
UNW = 32000

ENGINES = ("sync", "tensor", "vector", "scalar", "gpsimd")
N_DSEM = {"sync": 12, "gpsimd": 8}
DMA_SEMS = {e: tuple(f"d_{e}{i}" for i in range(n)) for e, n in N_DSEM.items()}
SEMS = ("s_sync", "s_pe", "s_dve", "s_act", "s_gps") \
    + DMA_SEMS["sync"] + DMA_SEMS["gpsimd"]
OWN_SEM = {"sync": "s_sync", "tensor": "s_pe", "vector": "s_dve",
           "scalar": "s_act", "gpsimd": "s_gps"}


class Op:
    __slots__ = ("fn", "deps", "inc")
    def __init__(self, fn, deps, inc):
        self.fn = fn
        self.deps = deps
        self.inc = inc


class Prog:
    def __init__(self):
        self.ops = {e: [] for e in ENGINES}
        self.cnt = {s: 0 for s in SEMS}
        self.drot = {e: 0 for e in N_DSEM}

    def emit(self, eng, fn, deps=(), inc=False, dma=False):
        deps = tuple(d for d in deps if d is not None)
        if dma:
            sems = DMA_SEMS[eng]
            sem = sems[self.drot[eng] % len(sems)]
            self.drot[eng] += 1
            if self.cnt[sem] > 0:
                deps = deps + ((sem, self.cnt[sem]),)
            self.cnt[sem] += 16
            tok = (sem, self.cnt[sem])
            self.ops[eng].append(Op(fn, deps, (sem, 16)))
            return tok
        if inc:
            sem = OWN_SEM[eng]
            self.cnt[sem] += 1
            tok = (sem, self.cnt[sem])
            self.ops[eng].append(Op(fn, deps, (sem, 1)))
            return tok
        self.ops[eng].append(Op(fn, deps, None))
        return None

    def barrier(self, engs=ENGINES):
        toks = {}
        for e in engs:
            toks[e] = self.emit(e, lambda x: x.nop(), inc=True)
        for e in engs:
            self.emit(e, lambda x: x.nop(),
                      deps=[toks[o] for o in engs if o != e])


def _replay_one(prog, ename, eng, sems):
    own = OWN_SEM[ename]
    used = set()
    for e2 in ENGINES:
        for op in prog.ops[e2]:
            for d in op.deps:
                used.add(d)
    waited = {}
    cur = 0
    drained = 0
    for op in prog.ops[ename]:
        for (sn, v) in op.deps:
            if sn == own:
                if ename != "tensor" and v > drained:
                    eng.drain()
                    drained = cur
                continue
            if waited.get(sn, 0) >= v:
                continue
            eng.wait_ge(sems[sn], v)
            waited[sn] = v
        instr = op.fn(eng)
        if op.inc is not None:
            sn, k = op.inc
            instr.then_inc(sems[sn], k)
            if sn == own:
                cur += 1
                if (own, cur) in used:
                    eng.wait_ge(sems[own], cur)



def build_program(upto="all"):
    nc = bass.Bass("TRN2", target_bir_lowering=False, debug=False)

    hbm = {}
    def H(name, shape, dt=F32, kind="ExternalInput"):
        hbm[name] = nc.dram_tensor(name, shape, dt, kind=kind).ap()
        return hbm[name]

    H("xpre", [D, HALF])
    H("xown", [D, HALF])
    H("wx_t", [D, 96])
    H("wdt_t", [RNK, D])
    H("bdt", [128, DT])
    H("a0", [128, DT * NST])
    H("d0", [128, DT])
    H("wqg_t", [D, HID], U8)
    H("wqu_t", [D, HID], U8)
    H("wqd_t", [HID, D], U8)
    H("sg", [128, HT])
    H("su", [128, HT])
    H("sd", [128, DT])
    H("nw", [128, DT])
    H("win", [D, PROJ])
    H("convw", [128, NCT * KCONV])
    H("convb", [128, NCT])
    H("dtb", [128, 1])
    H("a1", [128, 1])
    H("dskipbc", [128, HEADS])
    H("gnw", [128, DT])
    H("ones32", [32, 128])
    H("i32", [32, 32])
    H("iotac_h", [128, 2])
    H("iotabc_h", [128, 256])
    H("wout", [D, D])
    H("wo", [D, D])
    H("router", [D, EXPERTS])
    H("gw", [EXPERTS, D, FDIM])
    H("uw", [EXPERTS, D, FDIM])
    H("dw", [EXPERTS, FDIM, D])
    H("ident", [128, 128])
    H("onesr", [1, 128])
    H("negr", [1, 128])
    H("onesc", [128, 1])
    H("maskt", [64, 64])
    H("sspill", [128, 4096], kind="ExternalOutput")
    H("out", [HALF, D], kind="ExternalOutput")
    H("dbg", [D, HALF], kind="ExternalOutput")

    import contextlib
    stack = contextlib.ExitStack()

    with stack:
        x_fm = stack.enter_context(nc.sbuf_tensor("x_fm", [128, DT * HALF], F32))
        xfb = x_fm[:, :].bitcast(BF16)
        un = stack.enter_context(nc.sbuf_tensor("un", [128, UNW], F32))
        unb = un[:, :].bitcast(BF16)
        unu = un[:, :].bitcast(U8)
        xproj = stack.enter_context(nc.sbuf_tensor("xproj", [128, 512], F32))
        hcar = stack.enter_context(nc.sbuf_tensor("hcar", [128, DT * NST], F32))
        a0p = stack.enter_context(nc.sbuf_tensor("a0p", [128, DT * NST], F32))
        bcb = stack.enter_context(nc.sbuf_tensor("bcb", [128, 1792], F32))
        lnscr = stack.enter_context(nc.sbuf_tensor("lnscr", [128, 512], F32))
        identt = stack.enter_context(nc.sbuf_tensor("identt", [128, 128], F32))
        masktt = stack.enter_context(nc.sbuf_tensor("masktt", [128, 64], F32))
        halo = stack.enter_context(nc.sbuf_tensor("halo", [128, NCT * (KCONV - 1)], F32))
        params = stack.enter_context(nc.sbuf_tensor("params", [128, 192], F32))
        convwp = stack.enter_context(nc.sbuf_tensor("convwp", [128, NCT * KCONV], F32))
        convbp = stack.enter_context(nc.sbuf_tensor("convbp", [128, NCT], F32))
        dskipp = stack.enter_context(nc.sbuf_tensor("dskipp", [128, HEADS], F32))
        ones32t = stack.enter_context(nc.sbuf_tensor("ones32t", [32, 128], F32))
        i32t = stack.enter_context(nc.sbuf_tensor("i32t", [32, 32], F32))
        iotac = stack.enter_context(nc.sbuf_tensor("iotac", [128, 2], F32))
        iotabc = stack.enter_context(nc.sbuf_tensor("iotabc", [128, 256], F32))
        routerp = stack.enter_context(nc.sbuf_tensor("routerp", [128, DT * EXPERTS], F32))

        ps = [stack.enter_context(nc.psum_tensor(f"ps{i}", [128, 512], F32))
              for i in range(8)]

        sems = {}
        for s in SEMS:
            sems[s] = stack.enter_context(nc.semaphore(s))

        p = Prog()
        psum_guard = [None] * 8

        bdt_c = lambda dt: params[:, dt:dt + 1]
        dp1_c = lambda dt: params[:, 16 + dt:17 + dt]
        sg_c = lambda f: params[:, 32 + f:33 + f]
        su_c = lambda f: params[:, 76 + f:77 + f]
        sd_c = lambda dt: params[:, 120 + dt:121 + dt]
        nw_c = lambda dt: params[:, 136 + dt:137 + dt]
        dtb_c = params[:, 152:153]
        a1n_c = params[:, 153:154]
        ones_c = params[:, 154:155]
        eps_c = params[:, 155:156]
        onesr_ap = bcb[0:1, 0:128]
        negr_ap = bcb[0:1, 1664:1792]
        st_row = lambda i, w: bcb[0:1, 128 + i * 512:128 + i * 512 + w]

        def ld(dst, src, eng="sync", deps=()):
            return p.emit(eng, lambda e, d=dst, s=src: e.dma_start(out=d, in_=s),
                          deps=deps, dma=True)

        t_bdt = ld(params[:, 0:16], hbm["bdt"][:, :])
        t_d0 = ld(params[:, 16:32], hbm["d0"][:, :])
        ld(params[:, 32:76], hbm["sg"][:, :])
        ld(params[:, 76:120], hbm["su"][:, :])
        ld(params[:, 120:136], hbm["sd"][:, :])
        ld(params[:, 136:152], hbm["nw"][:, :])
        t_dtb = ld(params[:, 152:153], hbm["dtb"][:, :])
        t_a1 = ld(params[:, 153:154], hbm["a1"][:, :])
        t_oc = ld(params[:, 154:155], hbm["onesc"][:, :])
        t_a0 = ld(a0p[:, :], hbm["a0"][:, :])
        ld(convwp[:, :], hbm["convw"][:, :])
        ld(convbp[:, :], hbm["convb"][:, :])
        ld(dskipp[:, :], hbm["dskipbc"][:, :])
        ld(ones32t[:, :], hbm["ones32"][:, :])
        ld(i32t[:, :], hbm["i32"][:, :])
        ld(iotac[:, :], hbm["iotac_h"][:, :])
        ld(iotabc[:, :], hbm["iotabc_h"][:, :])
        ld(params[:, 160:176], hbm["gnw"][:, :])
        ld(identt[:, :], hbm["ident"][:, :])
        ld(masktt[0:64, :], hbm["maskt"][:, :])
        t_or = ld(bcb[0:1, 0:128], hbm["onesr"][:, :])
        t_nr = ld(bcb[0:1, 1664:1792], hbm["negr"][:, :])
        ld(routerp[:, :].rearrange("p (t e) -> p t e", t=DT),
           hbm["router"][:, :].rearrange("(t p) e -> p t e", p=128))

        t_eps = p.emit("vector", lambda e: e.memset(eps_c, EPS), inc=True)
        t_dp1 = p.emit("vector", lambda e: e.tensor_scalar_add(
            params[:, 16:32], params[:, 16:32], 1.0), deps=[t_d0], inc=True)
        t_bdt = p.emit("vector", lambda e: e.tensor_scalar_mul(
            params[:, 0:16], params[:, 0:16], -1.0), deps=[t_bdt], inc=True)
        t_dtb = p.emit("vector", lambda e: e.tensor_scalar_mul(
            dtb_c, dtb_c, -1.0), deps=[t_dtb], inc=True)
        t_a1n = p.emit("scalar", lambda e: e.activation(
            out=a1n_c, in_=a1n_c, func=AF.Exp), deps=[t_a1], inc=True)
        t_a0e = p.emit("scalar", lambda e: e.activation(
            out=a0p[:, :], in_=a0p[:, :], func=AF.Exp), deps=[t_a0], inc=True)
        t_hz = p.emit("vector", lambda e: e.memset(hcar[:, :], 0.0), inc=True)
        t_hlz = p.emit("vector", lambda e: e.memset(halo[:, :], 0.0), inc=True)

        p.barrier()

        def ln_fm(w, src_ap, src_toks, dsthi, dstlo, dst32, rms=False,
                  wcol=None):
            scr = lnscr[:, 0:w]
            st0 = st_row(0, w)
            st1 = st_row(1, w)
            st2 = st_row(2, w)
            t_sum = None
            t_sqmm = None
            for dt in range(DT):
                t_sq = p.emit("vector", lambda e, dt=dt: e.tensor_mul(
                    scr, src_ap(dt), src_ap(dt)),
                    deps=[src_toks[dt]] + ([t_sqmm] if dt else []), inc=True)
                if not rms:
                    t_sum = p.emit("tensor", lambda e, dt=dt: e.matmul(
                        ps[6][0:1, 0:w], ones_c, src_ap(dt),
                        start=(dt == 0), stop=(dt == DT - 1)),
                        deps=[src_toks[dt], t_oc] + ([psum_guard[6]] if dt == 0 else []),
                        inc=(dt == DT - 1))
                t_sqmm = p.emit("tensor", lambda e, dt=dt: e.matmul(
                    ps[7][0:1, 0:w], ones_c, scr,
                    start=(dt == 0), stop=(dt == DT - 1)),
                    deps=[t_sq] + ([psum_guard[7]] if dt == 0 else []),
                    inc=True)
            if not rms:
                t_mean = p.emit("vector", lambda e: e.tensor_scalar_mul(
                    st0, ps[6][0:1, 0:w], 1.0 / D), deps=[t_sum], inc=True)
                psum_guard[6] = t_mean
                t_msq = p.emit("vector", lambda e: e.tensor_mul(
                    st2, st0, st0), deps=[t_mean], inc=True)
                t_var = p.emit("vector", lambda e: e.scalar_tensor_tensor(
                    st1, ps[7][0:1, 0:w], 1.0 / D, st2,
                    ALU.mult, ALU.subtract), deps=[t_msq, t_sqmm], inc=True)
            else:
                t_var = p.emit("vector", lambda e: e.tensor_scalar_mul(
                    st1, ps[7][0:1, 0:w], 1.0 / D), deps=[t_sqmm], inc=True)
            psum_guard[7] = t_var
            t_rstd = p.emit("scalar", lambda e: e.activation(
                out=st1, in_=st1, func=AF.Sqrt, bias=eps_c[0:1, :]),
                deps=[t_var, t_eps], inc=True)
            t_rstd = p.emit("vector", lambda e: e.reciprocal(st1, st1),
                            deps=[t_rstd], inc=True)
            if not rms:
                t_bm = p.emit("tensor", lambda e: e.matmul(
                    ps[5][:, 0:w], onesr_ap, st0, start=True, stop=True),
                    deps=[t_mean, t_or, psum_guard[5]], inc=True)
            t_br = p.emit("tensor", lambda e: e.matmul(
                ps[4][:, 0:w], onesr_ap, st1, start=True, stop=True),
                deps=[t_rstd, t_or, psum_guard[4]], inc=True)
            toks = [None] * DT
            prev = None
            for dt in range(DT):
                if not rms:
                    tt = p.emit("vector", lambda e, dt=dt: e.tensor_sub(
                        scr, src_ap(dt), ps[5][:, 0:w]),
                        deps=[t_bm, src_toks[dt], prev], inc=True)
                    tu = p.emit("vector", lambda e: e.tensor_mul(
                        scr, scr, ps[4][:, 0:w]), deps=[tt, t_br], inc=True)
                else:
                    tu = p.emit("vector", lambda e, dt=dt: e.scalar_tensor_tensor(
                        scr, src_ap(dt), wcol(dt), ps[4][:, 0:w],
                        ALU.mult, ALU.mult),
                        deps=[src_toks[dt], t_br, prev], inc=True)
                if dst32 is not None:
                    tu = p.emit("vector", lambda e, dt=dt: e.tensor_copy(
                        dst32(dt), scr), deps=[tu], inc=True)
                if dsthi is not None:
                    th = p.emit("vector", lambda e, dt=dt: e.tensor_copy(
                        dsthi(dt), scr), deps=[tu], inc=True)
                    tu = p.emit("vector", lambda e, dt=dt: e.tensor_tensor(
                        dstlo(dt), scr, dsthi(dt), ALU.subtract),
                        deps=[th], inc=True)
                toks[dt] = tu
                prev = tu
            psum_guard[4] = toks[DT - 1]
            if not rms:
                psum_guard[5] = toks[DT - 1]
            return toks

        def wdt_ap(dt):
            return un[0:RNK, 27904 + dt * 128: 27904 + (dt + 1) * 128]

        def wx_ap(k):
            return un[:, 29952 + k * 96: 29952 + (k + 1) * 96]

        def load_l0a_weights(deps=()):
            t1 = p.emit("sync", lambda e: e.dma_start(
                out=un[:, 29952:31488].rearrange("p (t c) -> p t c", t=DT),
                in_=hbm["wx_t"][:, :].rearrange("(t p) c -> p t c", p=128)),
                deps=deps, dma=True)
            t2 = p.emit("sync", lambda e: e.dma_start(
                out=un[0:RNK, 27904:29952], in_=hbm["wdt_t"][:, :]),
                deps=deps, dma=True)
            return [t1, t2]

        hcar_dep = {}

        def l0a_half(c0, x_toks, w_toks):
            def x_sb(dt):
                return x_fm[:, dt * HALF + c0: dt * HALF + c0 + 512]
            t_xp = None
            for k in range(DT):
                t_xp = p.emit("tensor", lambda e, k=k: e.matmul(
                    ps[7][0:96, :], wx_ap(k), x_sb(k),
                    start=(k == 0), stop=(k == DT - 1)),
                    deps=list(x_toks) + list(w_toks) + ([psum_guard[7]] if k == 0 else []),
                    inc=(k == DT - 1))
            t_xpe = p.emit("vector", lambda e: e.tensor_copy(
                xproj[0:96, :], ps[7][0:96, :]), deps=[t_xp], inc=True)
            psum_guard[7] = t_xpe

            t2_toks = [None] * DT
            dtx_toks = [None] * DT
            for dt in range(DT):
                bi = 6 if dt % 2 == 0 else 5
                mm = p.emit("tensor", lambda e, dt=dt, bi=bi: e.matmul(
                    ps[bi][:, :], wdt_ap(dt), xproj[0:RNK, :],
                    start=True, stop=True),
                    deps=[t_xpe, psum_guard[bi]] + list(w_toks), inc=True)
                sp = p.emit("scalar", lambda e, dt=dt, bi=bi: e.activation(
                    out=un[:, dt * 512: dt * 512 + 512], in_=ps[bi][:, :],
                    func=AF.Sigmoid, scale=-1.0, bias=bdt_c(dt)),
                    deps=[mm, t_bdt], inc=True)
                psum_guard[bi] = sp
                sp = p.emit("scalar", lambda e, dt=dt: e.activation(
                    out=un[:, dt * 512: dt * 512 + 512],
                    in_=un[:, dt * 512: dt * 512 + 512], func=AF.Ln),
                    deps=[sp], inc=True)
                t2_toks[dt] = sp
                dtx_toks[dt] = p.emit("vector", lambda e, dt=dt: e.tensor_mul(
                    un[:, 8192 + dt * 512: 8192 + dt * 512 + 512],
                    un[:, dt * 512: dt * 512 + 512], x_sb(dt)),
                    deps=[sp] + list(x_toks), inc=True)

            y_toks = [None] * DT
            for dt in range(DT):
                y_toks[dt] = p.emit("vector", lambda e, dt=dt: e.tensor_scalar(
                    x_sb(dt), x_sb(dt), dp1_c(dt), None, ALU.mult),
                    deps=[t_dp1, dtx_toks[dt]], inc=True)

            scan_tok = [None, None]
            mul_tok = [None, None]
            add_tok = [None, None]
            bcb_guard = [None, None]
            for n in range(NST):
                npar = n % 2
                bbank, cbank = npar, 2 + npar
                boff = 20480 + npar * 1024
                tb = p.emit("sync", lambda e, n=n, bo=boff: e.dma_start(
                    out=un[0:1, bo:bo + 512], in_=xproj[64 + n:65 + n, :]),
                    deps=[t_xpe, bcb_guard[npar]], dma=True)
                tc = p.emit("sync", lambda e, n=n, bo=boff: e.dma_start(
                    out=un[0:1, bo + 512:bo + 1024], in_=xproj[80 + n:81 + n, :]),
                    deps=[t_xpe, bcb_guard[npar]], dma=True)
                bb = p.emit("tensor", lambda e, bo=boff, bbank=bbank: e.matmul(
                    ps[bbank][:, :], negr_ap, un[0:1, bo:bo + 512],
                    start=True, stop=True),
                    deps=[tb, t_nr, psum_guard[bbank]], inc=True)
                cb = p.emit("tensor", lambda e, bo=boff, cbank=cbank: e.matmul(
                    ps[cbank][:, :], onesr_ap, un[0:1, bo + 512:bo + 1024],
                    start=True, stop=True),
                    deps=[tc, t_or, psum_guard[cbank]], inc=True)
                bcb_guard[npar] = cb
                for dt in range(DT):
                    col = dt * NST + n
                    it = n * DT + dt
                    par = it % 2
                    o_da = 16384 + par * 2048
                    o_db = o_da + 512
                    o_h = o_da + 1024
                    o_p = o_da + 1536
                    da = p.emit("scalar", lambda e, dt=dt, col=col, o_da=o_da: e.activation(
                        out=un[:, o_da:o_da + 512],
                        in_=un[:, dt * 512: dt * 512 + 512],
                        func=AF.Exp, scale=a0p[:, col:col + 1]),
                        deps=[t2_toks[dt], t_a0e, scan_tok[par]], inc=True)
                    db = p.emit("vector", lambda e, dt=dt, o_db=o_db, bbank=bbank: e.scalar_tensor_tensor(
                        un[:, o_db:o_db + 512],
                        un[:, 8192 + dt * 512: 8192 + dt * 512 + 512],
                        1.0, ps[bbank][:, :], ALU.mult, ALU.mult),
                        deps=[dtx_toks[dt], bb, scan_tok[par]], inc=True)
                    psum_guard[bbank] = db
                    h = p.emit("vector", lambda e, col=col, o_da=o_da, o_db=o_db, o_h=o_h:
                        e.tensor_tensor_scan(
                            un[:, o_h:o_h + 512], un[:, o_da:o_da + 512],
                            un[:, o_db:o_db + 512], hcar[:, col:col + 1],
                            ALU.mult, ALU.add),
                        deps=[da, db, hcar_dep[col], mul_tok[par]], inc=True)
                    scan_tok[par] = h
                    hc = p.emit("vector", lambda e, col=col, o_h=o_h: e.tensor_copy(
                        hcar[:, col:col + 1], un[:, o_h + 511:o_h + 512]),
                        deps=[h], inc=True)
                    hcar_dep[col] = hc
                    m1 = p.emit("vector", lambda e, o_h=o_h, o_p=o_p, cbank=cbank: e.tensor_mul(
                        un[:, o_p:o_p + 512], un[:, o_h:o_h + 512], ps[cbank][:, :]),
                        deps=[h, cb, add_tok[par]], inc=True)
                    psum_guard[cbank] = m1
                    mul_tok[par] = m1
                    at_ = p.emit("gpsimd", lambda e, dt=dt, o_p=o_p: e.tensor_add(
                        x_sb(dt), x_sb(dt), un[:, o_p:o_p + 512]),
                        deps=[m1, y_toks[dt]], inc=True)
                    y_toks[dt] = at_
                    add_tok[par] = at_
            return y_toks

        def x1hi(dt, c0, w=512):
            return xfb[:, 2 * (dt * HALF + c0): 2 * (dt * HALF + c0) + w]

        def x1lo(dt, c0, w=512):
            return xfb[:, 2 * (dt * HALF + c0) + 512: 2 * (dt * HALF + c0) + 512 + w]

        def l0ffn_chunk(c0, x_toks):
            whb = [unb[:, 45056 + i * 2048: 45056 + (i + 1) * 2048] for i in range(4)]
            wu8 = [unu[:, 106496 + i * 2048: 106496 + (i + 1) * 2048] for i in range(2)]
            slot_tok = [None, None, None, None]
            u8_tok = [None, None]
            gwork = un[:, 27648:28160]
            hwork = un[:, 28160:28672]
            hhi = lambda f: unb[:, f * 512:(f + 1) * 512]
            hlo = lambda f: unb[:, 22528 + f * 512:22528 + (f + 1) * 512]
            h_toks = [None] * HT
            conv_engs = ("scalar", "vector", "gpsimd")

            def conv_op(ceng, dst, srcu8):
                if ceng == "scalar":
                    return lambda e: e.activation(out=dst, in_=srcu8,
                                                  func=AF.Copy, bias=-128.0)
                return lambda e: e.tensor_scalar_add(dst, srcu8, -128.0)

            tg = None
            for f in range(HT):
                for gi, wsrc in enumerate((hbm["wqg_t"], hbm["wqu_t"])):
                    slot = (2 * f + gi) % 4
                    u8s = gi
                    tl = p.emit("sync", lambda e, f=f, wsrc=wsrc, u8s=u8s: e.dma_start(
                        out=wu8[u8s].rearrange("p (k c) -> p k c", k=DT),
                        in_=wsrc[:, f * 128:(f + 1) * 128].rearrange(
                            "(k p) c -> p k c", p=128)),
                        deps=[u8_tok[u8s]], dma=True)
                    ceng = conv_engs[(2 * f + gi) % 3]
                    tcv = p.emit(ceng, conv_op(ceng, whb[slot], wu8[u8s]),
                                 deps=[tl, slot_tok[slot]], inc=True)
                    u8_tok[u8s] = tcv
                    bank = 6 + gi
                    mm = None
                    for k in range(DT):
                        for hl, xap in ((0, x1hi), (1, x1lo)):
                            mm = p.emit("tensor", lambda e, k=k, hl=hl, xap=xap,
                                        slot=slot, bank=bank: e.matmul(
                                ps[bank][:, :], whb[slot][:, k * 128:(k + 1) * 128],
                                xap(k, c0), start=(k == 0 and hl == 0),
                                stop=(k == DT - 1 and hl == 1)),
                                deps=[tcv, psum_guard[bank]]
                                + (list(x_toks) if (k == 0 and hl == 0) else []),
                                inc=(k == DT - 1 and hl == 1))
                    slot_tok[slot] = mm
                    prev_h = h_toks[f - 1] if f else None
                    if gi == 0:
                        tg = p.emit("scalar", lambda e, f=f: e.activation(
                            out=gwork, in_=ps[6][:, :], func=AF.Silu, scale=sg_c(f)),
                            deps=[mm, prev_h], inc=True)
                        psum_guard[6] = tg
                    else:
                        tu = p.emit("vector", lambda e, f=f: e.tensor_scalar(
                            hwork, ps[7][:, :], su_c(f), None, ALU.mult),
                            deps=[mm, prev_h], inc=True)
                        psum_guard[7] = tu
                        tm = p.emit("vector", lambda e: e.tensor_mul(
                            gwork, gwork, hwork), deps=[tu, tg], inc=True)
                        th = p.emit("vector", lambda e, f=f: e.tensor_copy(
                            hhi(f), gwork), deps=[tm], inc=True)
                        h_toks[f] = p.emit("vector", lambda e, f=f: e.tensor_tensor(
                            hlo(f), gwork, hhi(f), ALU.subtract), deps=[th], inc=True)

            dwb = [unb[:, 45056 + i * 5632: 45056 + (i + 1) * 5632] for i in range(2)]
            du8 = [unu[:, 116736 + i * 5632: 116736 + (i + 1) * 5632] for i in range(2)]
            dwork = [un[:, 28160:28672], un[:, 28672:29184]]
            dslot_tok = [slot_tok[0] or slot_tok[2], slot_tok[1] or slot_tok[3]]
            dslot_tok = [slot_tok[2], slot_tok[3]]
            du8_tok = [u8_tok[0], u8_tok[1]]
            dn_toks = [None] * DT
            for dt in range(DT):
                slot = dt % 2
                tl = p.emit("sync", lambda e, dt=dt, slot=slot: e.dma_start(
                    out=du8[slot].rearrange("p (k c) -> p k c", k=HT),
                    in_=hbm["wqd_t"][:, dt * 128:(dt + 1) * 128].rearrange(
                        "(k p) c -> p k c", p=128)),
                    deps=[du8_tok[slot]], dma=True)
                ceng = conv_engs[dt % 3]
                gd = list(slot_tok) + [h_toks[HT - 1]] \
                    if dt < 2 else [dslot_tok[slot]]
                tcv = p.emit(ceng, conv_op(ceng, dwb[slot], du8[slot]),
                             deps=[tl] + gd, inc=True)
                du8_tok[slot] = tcv
                bank = 6 + slot
                mm = None
                for k in range(HT):
                    for hl, hap in ((0, hhi), (1, hlo)):
                        mm = p.emit("tensor", lambda e, k=k, hl=hl, hap=hap,
                                    slot=slot, bank=bank: e.matmul(
                            ps[bank][:, :], dwb[slot][:, k * 128:(k + 1) * 128],
                            hap(k), start=(k == 0 and hl == 0),
                            stop=(k == HT - 1 and hl == 1)),
                            deps=[tcv, psum_guard[bank], h_toks[k]],
                            inc=(k == HT - 1 and hl == 1))
                dslot_tok[slot] = mm
                tv = p.emit("vector", lambda e, dt=dt, bank=bank, slot=slot: e.scalar_tensor_tensor(
                    dwork[slot], ps[bank][:, :], sd_c(dt), x1hi(dt, c0),
                    ALU.mult, ALU.add),
                    deps=[mm] + ([dn_toks[dt - 2]] if dt >= 2 else [h_toks[HT - 1]]),
                    inc=True)
                psum_guard[bank] = tv
                dn_toks[dt] = p.emit("vector", lambda e, dt=dt, slot=slot: e.tensor_add(
                    x_fm[:, dt * HALF + c0: dt * HALF + c0 + 512],
                    dwork[slot], x1lo(dt, c0)), deps=[tv], inc=True)
            return dn_toks

        A_CONV, A_Z, A_DT, A_ACS = 0, 5120, 9216, 9472
        A_XBCT, A_XH, A_G, A_CBT = 9728, 9992, 12040, 14088
        A_XT, A_Y, A_ZT, A_MISC = 14152, 16200, 18248, 20296
        A_Y3, A_S, A_WIN = 20992, 25088, 29184
        M_AT = A_MISC
        M_EXPA = A_MISC + 32
        M_DECT = A_MISC + 64
        M_DECBC = A_MISC + 96
        M_ALB = A_MISC + 128
        M_DTT = A_MISC + 160
        M_BTM = A_MISC + 192
        M_ONES = A_MISC + 448
        M_RSTD = A_MISC + 512
        gnw_c = lambda dt: params[:, 160 + dt:161 + dt]

        def ap(off, w, pn=128):
            return un[0:pn, off:off + w]

        def l1a_quarter(phase, q, xq_toks, s_toks, nchunks=4):
            own = (phase == 1)
            qc0 = q * 256
            xsl = lambda dt: x_fm[:, dt * HALF + qc0: dt * HALF + qc0 + 256]
            xn_off = lambda k: (A_XT + k * 256) if k < 8 else (A_Y + (k - 8) * 256)
            xn_ap = lambda k: un[:, xn_off(k):xn_off(k) + 256]
            xn_toks = ln_fm(256, xsl, xq_toks, None, None, xn_ap,
                            rms=True, wcol=nw_c)
            t_ones = p.emit("vector", lambda e: e.memset(
                ap(M_ONES, 64, 32), 1.0), inc=True)

            conv_toks = [None] * NCT
            z_toks = [None] * NZT
            wslots = [un[:, A_WIN:A_WIN + 2048], un[:, A_XH:A_XH + 2048]]
            prev_mm = [None]
            prev_mm_sl = [None, None]
            t_stage = None
            dt_tok = [None, None]
            mrange = list(range(NZT, 37)) + (list(range(NZT)) if own else [])
            for mi, m in enumerate(mrange):
                bank = 6 + (mi % 2)
                sl2 = mi % 2
                wslot = wslots[sl2]
                mw = 32 if m == 36 else 128
                t_w = p.emit("sync", lambda e, m=m, mw=mw, wslot=wslot: e.dma_start(
                    out=wslot.rearrange("p (k c) -> p k c", k=DT)[:, :, 0:mw],
                    in_=hbm["win"][:, m * 128:m * 128 + mw].rearrange(
                        "(k p) c -> p k c", p=128)),
                    deps=[prev_mm_sl[sl2]], dma=True)
                mm = None
                for k in range(DT):
                    mm = p.emit("tensor", lambda e, k=k, mw=mw, bank=bank,
                                wslot=wslot: e.matmul(
                        ps[bank][0:mw, 0:256], wslot[:, k * 128:k * 128 + mw],
                        xn_ap(k), start=(k == 0), stop=(k == DT - 1)),
                        deps=[t_w, psum_guard[bank], xn_toks[k]],
                        inc=(k == DT - 1))
                prev_mm[0] = mm
                prev_mm_sl[sl2] = mm
                if m < NZT:
                    z_toks[m] = p.emit("vector", lambda e, m=m, bank=bank: e.tensor_copy(
                        ap(A_Z + m * 256, 256), ps[bank][:, 0:256]),
                        deps=[mm], inc=True)
                    psum_guard[bank] = z_toks[m]
                elif m < 36:
                    ct = m - NZT
                    tse = p.emit("vector", lambda e, bank=bank: e.tensor_copy(
                        un[:, A_XBCT + 3:A_XBCT + 259], ps[bank][:, 0:256]),
                        deps=[mm, t_stage], inc=True)
                    psum_guard[bank] = tse
                    th = p.emit("vector", lambda e, ct=ct: e.tensor_copy(
                        un[:, A_XBCT:A_XBCT + 3], halo[:, ct * 3:ct * 3 + 3]),
                        deps=[tse], inc=True)
                    cv = p.emit("vector", lambda e, ct=ct: e.tensor_scalar(
                        ap(A_CONV + ct * 256, 256), un[:, A_XBCT:A_XBCT + 256],
                        convwp[:, ct * KCONV:ct * KCONV + 1], convbp[:, ct:ct + 1],
                        ALU.mult, ALU.add), deps=[th], inc=True)
                    for kk in range(1, KCONV):
                        cv = p.emit("vector", lambda e, ct=ct, kk=kk: e.scalar_tensor_tensor(
                            ap(A_CONV + ct * 256, 256),
                            un[:, A_XBCT + kk:A_XBCT + kk + 256],
                            convwp[:, ct * KCONV + kk:ct * KCONV + kk + 1],
                            ap(A_CONV + ct * 256, 256), ALU.mult, ALU.add),
                            deps=[cv], inc=True)
                    hn = p.emit("gpsimd", lambda e, ct=ct: e.tensor_copy(
                        halo[:, ct * 3:ct * 3 + 3],
                        un[:, A_XBCT + 256:A_XBCT + 259]), deps=[cv], inc=True)
                    t_stage = hn
                    conv_toks[ct] = p.emit("scalar", lambda e, ct=ct: e.activation(
                        out=ap(A_CONV + ct * 256, 256),
                        in_=ap(A_CONV + ct * 256, 256),
                        func=AF.Silu), deps=[cv], inc=True)
                else:
                    t1 = p.emit("scalar", lambda e, bank=bank: e.activation(
                        out=ap(A_DT, 256, 32), in_=ps[bank][0:32, 0:256],
                        func=AF.Sigmoid, scale=-1.0, bias=dtb_c[0:32, :]),
                        deps=[mm, t_dtb], inc=True)
                    psum_guard[bank] = t1
                    t1 = p.emit("scalar", lambda e: e.activation(
                        out=ap(A_DT, 256, 32), in_=ap(A_DT, 256, 32),
                        func=AF.Ln), deps=[t1], inc=True)
                    t2a = p.emit("vector", lambda e: e.tensor_scalar(
                        ap(A_ACS, 256, 32), ap(A_DT, 256, 32),
                        a1n_c[0:32, :], None, ALU.mult),
                        deps=[t1, t_a1n], inc=True)
                    t2b = p.emit("vector", lambda e: e.tensor_scalar_mul(
                        ap(A_DT, 256, 32), ap(A_DT, 256, 32), -1.0),
                        deps=[t2a], inc=True)
                    cs = None
                    for c in range(4):
                        cs = p.emit("vector", lambda e, c=c: e.tensor_tensor_scan(
                            ap(A_ACS + c * 64, 64, 32), ap(M_ONES, 64, 32),
                            ap(A_ACS + c * 64, 64, 32), 0.0, ALU.mult, ALU.add),
                            deps=[t2a, t_ones, cs], inc=True)
                    dt_tok = [t2b, cs]

            s_dep = list(s_toks)
            cg = {"xt": [prev_mm[0]], "y": [prev_mm[0]], "g": [],
                  "xh": [], "zt": [], "decbc": [], "y3last": None}
            y3_toks = [None] * DT
            for c in range(nchunks):
                if c:
                    p.barrier()
                acs_c = ap(A_ACS + c * 64, 64, 32)
                ta = p.emit("tensor", lambda e, acs_c=acs_c: e.transpose(
                    ps[6][0:64, 0:32], acs_c, i32t[:, :]),
                    deps=[dt_tok[1], psum_guard[6]], inc=True)
                tb = p.emit("tensor", lambda e, c=c: e.transpose(
                    ps[6][0:64, 64:96], ap(A_DT + c * 64, 64, 32), i32t[:, :]),
                    deps=[dt_tok[0]], inc=True)
                te1 = p.emit("vector", lambda e: e.tensor_copy(
                    ap(M_AT, 32, 64), ps[6][0:64, 0:32]),
                    deps=[ta] + cg["g"], inc=True)
                te2 = p.emit("vector", lambda e: e.tensor_copy(
                    ap(M_DTT, 32, 64), ps[6][0:64, 64:96]), deps=[tb], inc=True)
                psum_guard[6] = te2
                if own:
                    tex = p.emit("scalar", lambda e: e.activation(
                        out=ap(M_EXPA, 32, 64), in_=ap(M_AT, 32, 64),
                        func=AF.Exp), deps=[te1], inc=True)
                tx2 = p.emit("vector", lambda e, acs_c=acs_c: e.tensor_scalar(
                    ap(M_ALB, 32, 32), i32t[:, :], acs_c[:, 63:64],
                    None, ALU.mult), deps=[dt_tok[1], te2], inc=True)
                tdm = p.emit("tensor", lambda e: e.matmul(
                    ps[6][:, 0:32], ones32t[:, :], ap(M_ALB, 32, 32),
                    start=True, stop=True), deps=[tx2, psum_guard[6]], inc=True)
                td1 = p.emit("vector", lambda e: e.tensor_sub(
                    ap(M_DECT, 32, 64), ps[6][0:64, 0:32], ap(M_AT, 32, 64)),
                    deps=[tdm, te1], inc=True)
                td2 = p.emit("scalar", lambda e: e.activation(
                    out=ap(M_DECT, 32, 64), in_=ap(M_DECT, 32, 64),
                    func=AF.Exp), deps=[td1], inc=True)
                tdb = p.emit("scalar", lambda e: e.activation(
                    out=ap(M_DECBC, 32), in_=ps[6][:, 0:32], func=AF.Exp),
                    deps=[tdm] + cg["decbc"], inc=True)
                psum_guard[6] = tdb
                xt_toks = []
                for dt in range(DT):
                    bi = dt // 4
                    tt = p.emit("tensor", lambda e, dt=dt, c=c, bi=bi: e.transpose(
                        ps[bi][0:64, (dt % 4) * 128:(dt % 4) * 128 + 128],
                        un[:, A_CONV + dt * 256 + c * 64:
                           A_CONV + dt * 256 + c * 64 + 64],
                        identt[:, :]),
                        deps=[conv_toks[dt]]
                        + ([psum_guard[bi]] if dt % 4 == 0 else []),
                        inc=(dt % 4 == 3))
                    if dt % 4 == 3:
                        ev = p.emit("vector", lambda e, bi=bi: e.tensor_copy(
                            ap(A_XT + bi * 512, 512, 64), ps[bi][0:64, 0:512]),
                            deps=[tt] + cg["xt"], inc=True)
                        psum_guard[bi] = ev
                        xt_toks.append(ev)
                tbt = None
                for i in range(2):
                    tbt = p.emit("tensor", lambda e, i=i, c=c: e.transpose(
                        ps[4][0:64, i * 128:(i + 1) * 128],
                        un[:, A_CONV + (16 + i) * 256 + c * 64:
                           A_CONV + (16 + i) * 256 + c * 64 + 64],
                        identt[:, :]),
                        deps=[conv_toks[16 + i]]
                        + ([psum_guard[4]] if i == 0 else []),
                        inc=(i == 1))
                tbe = p.emit("vector", lambda e: e.tensor_copy(
                    ap(M_BTM, 256, 64), ps[4][0:64, 0:256]),
                    deps=[tbt] + s_dep, inc=True)
                psum_guard[4] = tbe
                ty0 = None
                if own:
                    ty0 = p.emit("vector", lambda e: e.tensor_tensor(
                        ap(A_Y, 2048, 64).rearrange("p (h w) -> p h w", h=HEADS),
                        ap(A_XT, 2048, 64).rearrange("p (h w) -> p h w", h=HEADS),
                        dskipp[0:64, :, None].broadcast_to([64, HEADS, HDIM]),
                        ALU.mult), deps=xt_toks + cg["y"], inc=True)
                    tcb = None
                    for i in range(2):
                        tcb = p.emit("tensor", lambda e, i=i, c=c: e.matmul(
                            ps[4][0:64, 256:320],
                            un[:, A_CONV + (16 + i) * 256 + c * 64:
                               A_CONV + (16 + i) * 256 + c * 64 + 64],
                            un[:, A_CONV + (18 + i) * 256 + c * 64:
                               A_CONV + (18 + i) * 256 + c * 64 + 64],
                            start=(i == 0), stop=(i == 1)),
                            deps=[conv_toks[16 + i], conv_toks[18 + i], tbe],
                            inc=(i == 1))
                    tcbe = p.emit("vector", lambda e: e.tensor_mul(
                        ap(A_CBT, 64, 64), ps[4][0:64, 256:320],
                        masktt[0:64, 0:64]), deps=[tcb], inc=True)
                    psum_guard[4] = tcbe
                    txh = p.emit("vector", lambda e, acs_c=acs_c: e.tensor_tensor(
                        ap(A_XH, 2048, 32).rearrange("p (h l) -> p h l", h=HEADS),
                        i32t[:, :, None].broadcast_to([32, 32, 64]),
                        acs_c[:, None, :].broadcast_to([32, 32, 64]),
                        ALU.mult),
                        deps=[dt_tok[1], prev_mm[0]] + cg["xh"], inc=True)
                    g_toks = []
                    for j in range(4):
                        tgm = p.emit("tensor", lambda e, j=j: e.matmul(
                            ps[j][0:64, 0:512], ones32t[:, 0:64],
                            ap(A_XH + j * 512, 512, 32),
                            start=True, stop=True),
                            deps=[txh, psum_guard[j]], inc=True)
                        tgs = p.emit("vector", lambda e, j=j: e.tensor_tensor(
                            ap(A_G + j * 512, 512, 64).rearrange(
                                "p (h l) -> p h l", h=8),
                            ps[j][0:64, 0:512].rearrange("p (h l) -> p h l", h=8),
                            ap(M_AT, 32, 64)[:, j * 8:(j + 1) * 8, None]
                            .broadcast_to([64, 8, 64]),
                            ALU.subtract), deps=[tgm, te1] + cg["g"], inc=True)
                        psum_guard[j] = tgs
                        g_toks.append(tgs)
                    tgc = p.emit("vector", lambda e: e.tensor_scalar_min(
                        ap(A_G, 2048, 64), ap(A_G, 2048, 64), 0.0),
                        deps=g_toks, inc=True)
                    tge = p.emit("scalar", lambda e: e.activation(
                        out=ap(A_G, 2048, 64), in_=ap(A_G, 2048, 64),
                        func=AF.Exp), deps=[tgc], inc=True)
                    tgk = p.emit("vector", lambda e: e.tensor_tensor(
                        ap(A_G, 2048, 64).rearrange("p (h l) -> p h l", h=HEADS),
                        ap(A_G, 2048, 64).rearrange("p (h l) -> p h l", h=HEADS),
                        masktt[0:64, None, 0:64].broadcast_to([64, HEADS, 64]),
                        ALU.mult), deps=[tge], inc=True)
                    tgk = p.emit("vector", lambda e: e.tensor_tensor(
                        ap(A_G, 2048, 64).rearrange("p (h l) -> p h l", h=HEADS),
                        ap(A_G, 2048, 64).rearrange("p (h l) -> p h l", h=HEADS),
                        ap(A_CBT, 64, 64)[:, None, :].broadcast_to([64, HEADS, 64]),
                        ALU.mult), deps=[tgk, tcbe], inc=True)
                txt = p.emit("vector", lambda e: e.tensor_tensor(
                    ap(A_XT, 2048, 64).rearrange("p (h w) -> p h w", h=HEADS),
                    ap(A_XT, 2048, 64).rearrange("p (h w) -> p h w", h=HEADS),
                    ap(M_DTT, 32, 64)[:, :, None].broadcast_to([64, HEADS, HDIM]),
                    ALU.mult),
                    deps=xt_toks + [te2] + ([ty0] if own else []), inc=True)
                yd_toks = [None] * 4
                yo_toks = [None] * 4
                y_toks_c = []
                if own:
                    for h in range(HEADS):
                        bi = h // 8
                        off = (h % 8) * 64
                        td = p.emit("tensor", lambda e, h=h, bi=bi, off=off: e.matmul(
                            ps[bi][0:64, off:off + 64],
                            ap(A_G + h * 64, 64, 64),
                            ap(A_XT + h * 64, 64, 64),
                            start=True, stop=True),
                            deps=[tgk, txt]
                            + ([psum_guard[bi]] if h % 8 == 0 else []),
                            inc=(h % 8 == 7))
                        if h % 8 == 7:
                            yd_toks[bi] = td
                        to = None
                        for i in range(2):
                            to = p.emit("tensor", lambda e, h=h, bi=bi, off=off,
                                        i=i, c=c: e.matmul(
                                ps[4 + bi][0:64, off:off + 64],
                                un[:, A_CONV + (18 + i) * 256 + c * 64:
                                   A_CONV + (18 + i) * 256 + c * 64 + 64],
                                un[:, A_S + i * 2048 + h * 64:
                                   A_S + i * 2048 + h * 64 + 64],
                                start=(i == 0), stop=(i == 1)),
                                deps=[conv_toks[18 + i]] + s_dep
                                + ([psum_guard[4 + bi]] if h % 8 == 0 and i == 0 else []),
                                inc=(h % 8 == 7 and i == 1))
                        if h % 8 == 7:
                            yo_toks[bi] = to
                    prev_t1r = None
                    for bi in range(4):
                        t1 = p.emit("vector", lambda e, bi=bi: e.tensor_tensor(
                            ap(A_XH, 512, 64).rearrange("p (h l) -> p h l", h=8),
                            ps[4 + bi][0:64, 0:512].rearrange("p (h l) -> p h l", h=8),
                            ap(M_EXPA, 32, 64)[:, bi * 8:(bi + 1) * 8, None]
                            .broadcast_to([64, 8, 64]), ALU.mult),
                            deps=[yo_toks[bi], tex, prev_t1r, txh], inc=True)
                        psum_guard[4 + bi] = t1
                        t2_ = p.emit("gpsimd", lambda e, bi=bi: e.tensor_add(
                            ap(A_Y + bi * 512, 512, 64),
                            ap(A_Y + bi * 512, 512, 64),
                            ap(A_XH, 512, 64)), deps=[t1, ty0], inc=True)
                        prev_t1r = t2_
                        t3 = p.emit("vector", lambda e, bi=bi: e.tensor_add(
                            ap(A_Y + bi * 512, 512, 64),
                            ap(A_Y + bi * 512, 512, 64),
                            ps[bi][0:64, 0:512]), deps=[t2_, yd_toks[bi]], inc=True)
                        psum_guard[bi] = t3
                        y_toks_c.append(t3)
                    cg["xh"] = [prev_t1r, y_toks_c[-1]]
                txtd = p.emit("vector", lambda e: e.tensor_tensor(
                    ap(A_XT, 2048, 64).rearrange("p (h w) -> p h w", h=HEADS),
                    ap(A_XT, 2048, 64).rearrange("p (h w) -> p h w", h=HEADS),
                    ap(M_DECT, 32, 64)[:, :, None].broadcast_to([64, HEADS, HDIM]),
                    ALU.mult),
                    deps=[txt, td2] + (yd_toks if own else []), inc=True)
                new_s = []
                tsm_last = None
                for nt2 in range(2):
                    for j in range(4):
                        bi = 6 + ((nt2 * 4 + j) % 2)
                        tsm = p.emit("tensor", lambda e, nt2=nt2, j=j, bi=bi: e.matmul(
                            ps[bi][:, 0:512], ap(M_BTM + nt2 * 128, 128, 64),
                            ap(A_XT + j * 512, 512, 64),
                            start=True, stop=True),
                            deps=[txtd, tbe, psum_guard[bi]], inc=True)
                        tsm_last = tsm
                        sslice = un[:, A_S + nt2 * 2048 + j * 512:
                                    A_S + nt2 * 2048 + j * 512 + 512]
                        tu1 = p.emit("vector", lambda e, sslice=sslice, j=j: e.tensor_tensor(
                            sslice.rearrange("p (h w) -> p h w", h=8),
                            sslice.rearrange("p (h w) -> p h w", h=8),
                            ap(M_DECBC, 32)[:, j * 8:(j + 1) * 8, None]
                            .broadcast_to([128, 8, HDIM]), ALU.mult),
                            deps=[tsm, tdb] + s_dep
                            + (yo_toks if own else []), inc=True)
                        tu2 = p.emit("vector", lambda e, sslice=sslice, bi=bi: e.tensor_add(
                            sslice, sslice, ps[bi][:, 0:512]),
                            deps=[tu1], inc=True)
                        psum_guard[bi] = tu2
                        new_s.append(tu2)
                s_dep = new_s
                cg["xt"] = [tsm_last]
                cg["decbc"] = new_s[-1:]
                if upto == "l1a_sdump" and q == 0 and phase == 0:
                    for j2 in range(4):
                        tdmp = p.emit("gpsimd", lambda e, j2=j2, c=c: e.dma_start(
                            out=hbm["dbg"][:, :].rearrange(
                                "(t p) l -> p t l", p=128)[:, c * 4 + j2, :],
                            in_=un[:, A_S + j2 * 1024:A_S + (j2 + 1) * 1024]),
                            deps=new_s, dma=True)
                        s_dep = s_dep + [tdmp]
                if upto in ("l1a_sdump", "l1a_pre1c1") and q == 0 and phase == 0 \
                        and c == 0:
                    d1 = p.emit("gpsimd", lambda e: e.dma_start(
                        out=hbm["out"][0:64, :], in_=ap(A_XT, 2048, 64)),
                        deps=[txtd, tsm_last], dma=True)
                    d2 = p.emit("gpsimd", lambda e: e.dma_start(
                        out=hbm["out"][64:128, 0:256], in_=ap(M_BTM, 256, 64)),
                        deps=[tbe, tsm_last], dma=True)
                    d3 = p.emit("gpsimd", lambda e: e.dma_start(
                        out=hbm["out"][64:128, 256:288], in_=ap(M_DECT, 32, 64)),
                        deps=[td2, txtd], dma=True)
                    d4 = p.emit("gpsimd", lambda e: e.dma_start(
                        out=hbm["out"][128:256, 0:32], in_=ap(M_DECBC, 32)),
                        deps=[tdb] + new_s, dma=True)
                    d5 = p.emit("gpsimd", lambda e: e.dma_start(
                        out=hbm["out"][256:288, 0:64], in_=ap(A_ACS, 64, 32)),
                        deps=[td2], dma=True)
                    d6 = p.emit("gpsimd", lambda e: e.dma_start(
                        out=hbm["out"][288:352, 0:32], in_=ap(M_AT, 32, 64)),
                        deps=[td2], dma=True)
                    d7 = p.emit("gpsimd", lambda e: e.dma_start(
                        out=hbm["out"][352:384, 0:32], in_=ap(M_ALB, 32, 32)),
                        deps=[tdm], dma=True)
                    p.emit("gpsimd", lambda e: e.nop(),
                           deps=[d1, d2, d3, d4, d5, d6, d7])
                    s_dep = s_dep + [d1, d2, d3, d4, d5, d6, d7]
                if own:
                    zt_toks = []
                    for dt in range(DT):
                        gidx = dt // 4
                        bi = 4 + gidx % 2
                        tt = p.emit("tensor", lambda e, dt=dt, c=c, bi=bi: e.transpose(
                            ps[bi][0:64, (dt % 4) * 128:(dt % 4) * 128 + 128],
                            un[:, A_Z + dt * 256 + c * 64:
                               A_Z + dt * 256 + c * 64 + 64],
                            identt[:, :]),
                            deps=[z_toks[dt]]
                            + ([psum_guard[bi]] if dt % 4 == 0 else []),
                            inc=(dt % 4 == 3))
                        if dt % 4 == 3:
                            ev = p.emit("scalar", lambda e, gidx=gidx, bi=bi: e.activation(
                                out=ap(A_ZT + gidx * 512, 512, 64),
                                in_=ps[bi][0:64, 0:512], func=AF.Silu),
                                deps=[tt] + cg["zt"], inc=True)
                            psum_guard[bi] = ev
                            zt_toks.append(ev)
                    tyz = p.emit("vector", lambda e: e.tensor_mul(
                        ap(A_Y, 2048, 64), ap(A_Y, 2048, 64),
                        ap(A_ZT, 2048, 64)),
                        deps=zt_toks + y_toks_c, inc=True)
                    cg["zt"] = [tyz]
                    tss = p.emit("scalar", lambda e: e.activation(
                        out=ap(A_G, 2048, 64), in_=ap(A_Y, 2048, 64),
                        func=AF.Square, accum_out=ap(M_RSTD, 1, 64)),
                        deps=[tyz], inc=True)
                    cg["g"] = [tss]
                    trs = p.emit("scalar", lambda e: e.activation(
                        out=ap(M_RSTD, 1, 64), in_=ap(M_RSTD, 1, 64),
                        func=AF.Sqrt, scale=1.0 / D, bias=eps_c[0:64, :]),
                        deps=[tss, t_eps], inc=True)
                    trs = p.emit("vector", lambda e: e.reciprocal(
                        ap(M_RSTD, 1, 64), ap(M_RSTD, 1, 64)),
                        deps=[trs], inc=True)
                    tyn = p.emit("vector", lambda e: e.tensor_scalar(
                        ap(A_Y, 2048, 64), ap(A_Y, 2048, 64),
                        ap(M_RSTD, 1, 64), None, ALU.mult),
                        deps=[trs], inc=True)
                    lastev = None
                    for dt in range(DT):
                        bi = dt // 4
                        tt = p.emit("tensor", lambda e, dt=dt, bi=bi: e.transpose(
                            ps[bi][:, (dt % 4) * 64:(dt % 4) * 64 + 64],
                            ap(A_Y + dt * 128, 128, 64), identt[0:64, 0:64]),
                            deps=[tyn]
                            + ([psum_guard[bi]] if dt % 4 == 0 else []),
                            inc=(dt % 4 == 3))
                        if dt % 4 == 3:
                            for d2 in range(dt - 3, dt + 1):
                                lastev = p.emit("vector", lambda e, d2=d2, bi=bi, c=c:
                                    e.tensor_scalar(
                                        un[:, A_Y3 + d2 * 256 + c * 64:
                                           A_Y3 + d2 * 256 + c * 64 + 64],
                                        ps[bi][:, (d2 % 4) * 64:(d2 % 4) * 64 + 64],
                                        gnw_c(d2), None, ALU.mult),
                                    deps=[tt], inc=True)
                                y3_toks[d2] = lastev
                            psum_guard[bi] = lastev
                    cg["y"] = [lastev]

            if not own:
                return s_dep
            o1_toks = [None] * DT
            prev_mm_sl = [prev_mm[0], y3_toks[-1]]
            for m in range(DT):
                bank = 6 + (m % 2)
                sl2 = m % 2
                wslot = wslots[sl2]
                t_w2 = p.emit("sync", lambda e, m=m, wslot=wslot: e.dma_start(
                    out=wslot.rearrange("p (k c) -> p k c", k=DT),
                    in_=hbm["wout"][:, m * 128:(m + 1) * 128].rearrange(
                        "(k p) c -> p k c", p=128)),
                    deps=[prev_mm_sl[sl2]], dma=True)
                mm = None
                for k in range(DT):
                    mm = p.emit("tensor", lambda e, k=k, bank=bank,
                                wslot=wslot: e.matmul(
                        ps[bank][:, 0:256], wslot[:, k * 128:(k + 1) * 128],
                        un[:, A_Y3 + k * 256:A_Y3 + k * 256 + 256],
                        start=(k == 0), stop=(k == DT - 1)),
                        deps=[t_w2, psum_guard[bank], y3_toks[k]],
                        inc=(k == DT - 1))
                prev_mm[0] = mm
                prev_mm_sl[sl2] = mm
                o1_toks[m] = p.emit("vector", lambda e, m=m, bank=bank: e.tensor_copy(
                    ap(A_CONV + m * 256, 256), ps[bank][:, 0:256]),
                    deps=[mm], inc=True)
                psum_guard[bank] = o1_toks[m]
            x3_toks = [None] * DT
            for m in range(DT):
                bank = 6 + (m % 2)
                sl2 = m % 2
                wslot = wslots[sl2]
                t_w2 = p.emit("sync", lambda e, m=m, wslot=wslot: e.dma_start(
                    out=wslot.rearrange("p (k c) -> p k c", k=DT),
                    in_=hbm["wo"][:, m * 128:(m + 1) * 128].rearrange(
                        "(k p) c -> p k c", p=128)),
                    deps=[prev_mm_sl[sl2]], dma=True)
                mm = None
                for k in range(DT):
                    mm = p.emit("tensor", lambda e, k=k, bank=bank,
                                wslot=wslot: e.matmul(
                        ps[bank][:, 0:256], wslot[:, k * 128:(k + 1) * 128],
                        ap(A_CONV + k * 256, 256),
                        start=(k == 0), stop=(k == DT - 1)),
                        deps=[t_w2, psum_guard[bank], o1_toks[k]],
                        inc=(k == DT - 1))
                prev_mm[0] = mm
                prev_mm_sl[sl2] = mm
                x3_toks[m] = p.emit("vector", lambda e, m=m, bank=bank: e.tensor_add(
                    xsl(m), xsl(m), ps[bank][:, 0:256]), deps=[mm], inc=True)
                psum_guard[bank] = x3_toks[m]
            return ln_fm(256, xsl, x3_toks, None, None, xsl)

        S_X3, S_XG, S_H = 0, 16384, 20480
        S_GT, S_G, S_WD, S_WGU, S_RT = 22016, 22528, 24576, 25600, 29696
        CAP = 256
        c_tm_ap = lambda: un[:, S_RT:S_RT + 128]
        sel_ap = lambda: un[:, S_RT + 128:S_RT + 256]
        postm_ap = lambda: un[:, S_RT + 256:S_RT + 384]
        cg_ap = un[:, S_RT + 384:S_RT + 386]

        def moe_phase():
            prev_rt = [None]
            out_dma = []
            mm = None
            for ts in range(8):
                for k in range(DT):
                    mm = p.emit("tensor", lambda e, ts=ts, k=k: e.matmul(
                        ps[6][0:128, ts * 16:ts * 16 + 16],
                        x_fm[:, k * HALF + ts * 128: k * HALF + ts * 128 + 128],
                        routerp[:, k * 16:(k + 1) * 16],
                        start=(k == 0), stop=(k == DT - 1)),
                        deps=[psum_guard[6]] if ts == 0 and k == 0 else [],
                        inc=(k == DT - 1))
            tsc = p.emit("vector", lambda e: e.tensor_copy(
                un[:, S_RT + 512:S_RT + 640], ps[6][:, 0:128]),
                deps=[mm], inc=True)
            psum_guard[6] = tsc
            for ts in range(8):
                sc = un[:, S_RT + 512 + ts * 16:S_RT + 512 + ts * 16 + 16]
                cb = un[:, S_RT + ts * 16:S_RT + ts * 16 + 16]
                sel = un[:, S_RT + 128 + ts * 16:S_RT + 128 + ts * 16 + 16]
                m1 = un[:, S_RT + 400:S_RT + 401]
                nm1 = un[:, S_RT + 401:S_RT + 402]
                m2 = un[:, S_RT + 402:S_RT + 403]
                den = un[:, S_RT + 403:S_RT + 404]
                msk = un[:, S_RT + 404:S_RT + 420]
                t1 = p.emit("vector", lambda e, sc=sc: e.reduce_max(
                    m1, sc, AX.X), deps=[tsc, prev_rt[0]], inc=True)
                t2 = p.emit("vector", lambda e: e.tensor_scalar_mul(
                    nm1, m1, -1.0), deps=[t1], inc=True)
                t3 = p.emit("scalar", lambda e, sc=sc, cb=cb: e.activation(
                    out=cb, in_=sc, func=AF.Exp, bias=nm1), deps=[t2], inc=True)
                t4 = p.emit("vector", lambda e, sc=sc, sel=sel: e.tensor_scalar(
                    sel, sc, m1, None, ALU.is_equal), deps=[t1], inc=True)
                t5 = p.emit("vector", lambda e, sc=sc, sel=sel: e.scalar_tensor_tensor(
                    msk, sel, -1e30, sc, ALU.mult, ALU.add), deps=[t4], inc=True)
                t6 = p.emit("vector", lambda e: e.reduce_max(
                    m2, msk, AX.X), deps=[t5], inc=True)
                t7 = p.emit("vector", lambda e: e.tensor_scalar(
                    msk, msk, m2, None, ALU.is_equal), deps=[t6], inc=True)
                t8 = p.emit("vector", lambda e, sel=sel: e.tensor_add(
                    sel, sel, msk), deps=[t7, t4], inc=True)
                t9 = p.emit("vector", lambda e, cb=cb, sel=sel: e.tensor_mul(
                    cb, cb, sel), deps=[t8, t3], inc=True)
                ta_ = p.emit("vector", lambda e, cb=cb: e.reduce_sum(
                    den, cb, AX.X), deps=[t9], inc=True)
                tb_ = p.emit("vector", lambda e: e.reciprocal(
                    den, den), deps=[ta_], inc=True)
                tc_ = p.emit("vector", lambda e, cb=cb: e.tensor_scalar(
                    cb, cb, den, None, ALU.mult), deps=[tb_], inc=True)
                prev_rt[0] = tc_
            t_on = p.emit("vector", lambda e: e.memset(
                un[0:16, 2048:3072], 1.0), inc=True)
            tsel = None
            for ts in range(8):
                bk = 6 + ts // 4
                tt = p.emit("tensor", lambda e, ts=ts, bk=bk: e.transpose(
                    ps[bk][0:16, (ts % 4) * 128:(ts % 4) * 128 + 128],
                    un[:, S_RT + 128 + ts * 16:S_RT + 128 + ts * 16 + 16],
                    identt[:, :]),
                    deps=[prev_rt[0]]
                    + ([psum_guard[bk]] if ts % 4 == 0 else []),
                    inc=(ts == 7))
            tsel = p.emit("vector", lambda e: e.tensor_copy(
                un[0:16, 0:512], ps[6][0:16, 0:512]), deps=[tt], inc=True)
            tsel = p.emit("vector", lambda e: e.tensor_copy(
                un[0:16, 512:1024], ps[7][0:16, 0:512]), deps=[tt, tsel], inc=True)
            psum_guard[6] = tsel
            psum_guard[7] = tsel
            tpos = p.emit("vector", lambda e: e.tensor_tensor_scan(
                un[0:16, 1024:2048], un[0:16, 2048:3072], un[0:16, 0:1024],
                0.0, ALU.mult, ALU.add), deps=[tsel, t_on], inc=True)
            tpos = p.emit("vector", lambda e: e.tensor_mul(
                un[0:16, 1024:2048], un[0:16, 1024:2048], un[0:16, 0:1024]),
                deps=[tpos], inc=True)
            tpt = None
            for ts in range(8):
                tt2 = p.emit("tensor", lambda e, ts=ts: e.transpose(
                    ps[6][0:128, ts * 16:ts * 16 + 16],
                    un[0:16, 1024 + ts * 128:1024 + ts * 128 + 128],
                    identt[0:16, 0:16]),
                    deps=[tpos] + ([psum_guard[6]] if ts == 0 else []),
                    inc=(ts == 7))
            tpt = p.emit("vector", lambda e: e.tensor_copy(
                postm_ap(), ps[6][:, 0:128]), deps=[tt2], inc=True)
            psum_guard[6] = tpt
            p.barrier()
            x3_toks = [None] * 8
            for sl in range(8):
                tt = None
                for dt in range(DT):
                    bi = dt // 4
                    tt = p.emit("tensor", lambda e, sl=sl, dt=dt, bi=bi: e.transpose(
                        ps[bi][:, (dt % 4) * 128:(dt % 4) * 128 + 128],
                        x_fm[:, dt * HALF + sl * 128: dt * HALF + sl * 128 + 128],
                        identt[:, :]),
                        deps=([psum_guard[bi]] if dt % 4 == 0 else []),
                        inc=(dt % 4 == 3))
                    if dt % 4 == 3:
                        ev = p.emit("vector", lambda e, sl=sl, bi=bi: e.tensor_copy(
                            un[:, S_X3 + sl * 2048 + bi * 512:
                               S_X3 + sl * 2048 + bi * 512 + 512],
                            ps[bi][:, 0:512]),
                            deps=[tt] + ([tpt, tsel] if sl == 0 and bi == 0 else []),
                            inc=True)
                        psum_guard[bi] = ev
                x3_toks[sl] = ev
            p.barrier()

            gprev = [None]
            xg_guard = [None]
            h_guard = [None]
            ds_guard = [None]
            xfm_guard = {}
            wgu_tok = [None, None]
            wd_tok = [None, None]
            for ex in range(EXPERTS):
                rls = []
                for ts in range(8):
                    rls.append(p.emit("sync", lambda e, ex=ex, ts=ts: e.dma_start(
                        out=bcb[0:1, 128 + ts * 128:128 + ts * 128 + 128],
                        in_=postm_ap()[:, ts * 16 + ex:ts * 16 + ex + 1]),
                        deps=[gprev[0]], dma=True))
                tgm1 = p.emit("tensor", lambda e: e.matmul(
                    ps[6][:, 0:512], onesr_ap, bcb[0:1, 128:640],
                    start=True, stop=True),
                    deps=rls + [t_or, psum_guard[6]], inc=True)
                tgm2 = p.emit("tensor", lambda e: e.matmul(
                    ps[7][:, 0:512], onesr_ap, bcb[0:1, 640:1152],
                    start=True, stop=True), deps=rls + [psum_guard[7]], inc=True)
                gtoks = []
                for gi in range(2):
                    for half2 in range(2):
                        tgx = p.emit("vector", lambda e, gi=gi, half2=half2: e.tensor_scalar(
                            un[:, S_G + gi * 1024 + half2 * 512:
                               S_G + gi * 1024 + half2 * 512 + 512],
                            ps[6 + half2][:, 0:512], iotac[:, gi:gi + 1],
                            None, ALU.is_equal),
                            deps=[tgm1 if half2 == 0 else tgm2, gprev[0]], inc=True)
                        gtoks.append(tgx)
                psum_guard[6] = gtoks[-1]
                psum_guard[7] = gtoks[-1]
                gt_toks = [gprev[0], gprev[0]]
                xg_toks = []
                for wave in range(2):
                    gm = None
                    for sl in range(8):
                        gts = p.emit("vector", lambda e, sl=sl, ex=ex: e.tensor_tensor(
                            un[:, S_GT + (sl % 2) * 256:S_GT + (sl % 2) * 256 + 256],
                            postm_ap()[:, sl * 16 + ex:sl * 16 + ex + 1]
                            .broadcast_to([128, 256]),
                            iotabc[:, :], ALU.is_equal),
                            deps=[tpt, gt_toks[sl % 2]], inc=True)
                        for dtl in range(8):
                            dt = wave * 8 + dtl
                            gm = p.emit("tensor", lambda e, sl=sl, dt=dt, dtl=dtl: e.matmul(
                                ps[dtl][:, 0:256],
                                un[:, S_X3 + sl * 2048 + dt * 128:
                                   S_X3 + sl * 2048 + dt * 128 + 128],
                                un[:, S_GT + (sl % 2) * 256:S_GT + (sl % 2) * 256 + 256],
                                start=(sl == 0), stop=(sl == 7)),
                                deps=[gts, x3_toks[sl]]
                                + ([psum_guard[dtl]] if sl == 0 else []),
                                inc=(dtl == 7))
                        gt_toks[sl % 2] = gm
                    for dtl in range(8):
                        dt = wave * 8 + dtl
                        ev = p.emit("vector", lambda e, dt=dt, dtl=dtl: e.tensor_copy(
                            un[:, S_XG + dt * 256:S_XG + dt * 256 + 256],
                            ps[dtl][:, 0:256]), deps=[gm, xg_guard[0]], inc=True)
                        psum_guard[dtl] = ev
                        xg_toks.append(ev)
                cgm = None
                cg_gt = [gm, gm]
                for sl in range(8):
                    gts2 = p.emit("vector", lambda e, sl=sl, ex=ex: e.tensor_tensor(
                        un[:, S_GT + (sl % 2) * 256:S_GT + (sl % 2) * 256 + 256],
                        postm_ap()[:, sl * 16 + ex:sl * 16 + ex + 1]
                        .broadcast_to([128, 256]),
                        iotabc[:, :], ALU.is_equal),
                        deps=[tpt, cg_gt[sl % 2]], inc=True)
                    for st in range(2):
                        cgm = p.emit("tensor", lambda e, sl=sl, st=st, ex=ex: e.matmul(
                            ps[6 + st][0:128, 0:1],
                            un[:, S_GT + (sl % 2) * 256 + st * 128:
                               S_GT + (sl % 2) * 256 + st * 128 + 128],
                            c_tm_ap()[:, sl * 16 + ex:sl * 16 + ex + 1],
                            start=(sl == 0), stop=(sl == 7)),
                            deps=[gts2, prev_rt[0]]
                            + ([psum_guard[6 + st]] if sl == 0 else []),
                            inc=True)
                    cg_gt[sl % 2] = cgm
                tcg = p.emit("vector", lambda e: e.tensor_copy(
                    cg_ap[:, 0:1], ps[6][:, 0:1]), deps=[cgm], inc=True)
                tcg = p.emit("vector", lambda e: e.tensor_copy(
                    cg_ap[:, 1:2], ps[7][:, 0:1]), deps=[cgm, tcg], inc=True)
                psum_guard[6] = tcg
                psum_guard[7] = tcg
                h_toks = [None] * FT
                for ft in range(FT):
                    slot = ft % 2
                    twg = p.emit("sync", lambda e, ex=ex, ft=ft, slot=slot: e.dma_start(
                        out=un[:, S_WGU + slot * 2048:S_WGU + slot * 2048 + 2048]
                        .rearrange("p (k c) -> p k c", k=DT),
                        in_=hbm["gw"][ex, :, ft * 128:(ft + 1) * 128].rearrange(
                            "(k p) c -> p k c", p=128)),
                        deps=[wgu_tok[slot]], dma=True)
                    mmg = None
                    for k in range(DT):
                        mmg = p.emit("tensor", lambda e, k=k, slot=slot: e.matmul(
                            ps[6][:, 0:256],
                            un[:, S_WGU + slot * 2048 + k * 128:
                               S_WGU + slot * 2048 + k * 128 + 128],
                            un[:, S_XG + k * 256:S_XG + k * 256 + 256],
                            start=(k == 0), stop=(k == DT - 1)),
                            deps=[twg, psum_guard[6]] + xg_toks, inc=(k == DT - 1))
                    twu = p.emit("sync", lambda e, ex=ex, ft=ft, slot=slot: e.dma_start(
                        out=un[:, S_WGU + slot * 2048:S_WGU + slot * 2048 + 2048]
                        .rearrange("p (k c) -> p k c", k=DT),
                        in_=hbm["uw"][ex, :, ft * 128:(ft + 1) * 128].rearrange(
                            "(k p) c -> p k c", p=128)),
                        deps=[mmg], dma=True)
                    mmu = None
                    for k in range(DT):
                        mmu = p.emit("tensor", lambda e, k=k, slot=slot: e.matmul(
                            ps[7][:, 0:256],
                            un[:, S_WGU + slot * 2048 + k * 128:
                               S_WGU + slot * 2048 + k * 128 + 128],
                            un[:, S_XG + k * 256:S_XG + k * 256 + 256],
                            start=(k == 0), stop=(k == DT - 1)),
                            deps=[twu, psum_guard[7]], inc=(k == DT - 1))
                    wgu_tok[slot] = mmu
                    tg_ = p.emit("scalar", lambda e: e.activation(
                        out=un[:, S_GT:S_GT + 256], in_=ps[6][:, 0:256],
                        func=AF.Silu),
                        deps=[mmg, h_toks[ft - 1] if ft else tcg], inc=True)
                    psum_guard[6] = tg_
                    th_ = p.emit("vector", lambda e, ft=ft: e.tensor_mul(
                        un[:, S_H + ft * 256:S_H + ft * 256 + 256],
                        un[:, S_GT:S_GT + 256], ps[7][:, 0:256]),
                        deps=[tg_, mmu, h_guard[0]], inc=True)
                    psum_guard[7] = th_
                    h_toks[ft] = th_
                dmm = [None] * 8
                for dc in range(4):
                    for ft in range(FT):
                        slot = (dc * FT + ft) % 2
                        twd = p.emit("sync", lambda e, ex=ex, ft=ft, dc=dc,
                                     slot=slot: e.dma_start(
                            out=un[:, S_WD + slot * 512:S_WD + slot * 512 + 512],
                            in_=hbm["dw"][ex, ft * 128:(ft + 1) * 128,
                                          dc * 512:(dc + 1) * 512]),
                            deps=[wd_tok[slot]], dma=True)
                        for st in range(2):
                            bi = st * 4 + dc
                            dmm[bi] = p.emit("tensor", lambda e, ft=ft, dc=dc,
                                             st=st, bi=bi, slot=slot: e.matmul(
                                ps[bi][:, 0:512],
                                un[:, S_H + ft * 256 + st * 128:
                                   S_H + ft * 256 + st * 128 + 128],
                                un[:, S_WD + slot * 512:S_WD + slot * 512 + 512],
                                start=(ft == 0), stop=(ft == FT - 1)),
                                deps=[twd, h_toks[ft]]
                                + ([psum_guard[bi]] if ft == 0 else []),
                                inc=True)
                        wd_tok[slot] = dmm[st * 4 + dc]
                ds_toks = []
                for st in range(2):
                    for dc in range(4):
                        bi = st * 4 + dc
                        ev = p.emit("vector", lambda e, st=st, dc=dc, bi=bi: e.tensor_scalar(
                            un[:, S_XG + st * 2048 + dc * 512:
                               S_XG + st * 2048 + dc * 512 + 512],
                            ps[bi][:, 0:512], cg_ap[:, st:st + 1],
                            None, ALU.mult),
                            deps=[dmm[bi], tcg, ds_guard[0],
                                  h_toks[FT - 1]], inc=True)
                        psum_guard[bi] = ev
                        ds_toks.append(ev)
                sc_last = None
                for dt in range(DT):
                    for tc in range(2):
                        bi = 6 + ((dt * 2 + tc) % 2)
                        mm3 = None
                        for st in range(2):
                            mm3 = p.emit("tensor", lambda e, dt=dt, tc=tc, st=st,
                                         bi=bi: e.matmul(
                                ps[bi][:, 0:512],
                                un[:, S_XG + st * 2048 + dt * 128:
                                   S_XG + st * 2048 + dt * 128 + 128],
                                un[:, S_G + st * 1024 + tc * 512:
                                   S_G + st * 1024 + tc * 512 + 512],
                                start=(st == 0), stop=(st == 1)),
                                deps=[ds_toks[st * 4 + dt // 4],
                                      gtoks[-1], psum_guard[bi]],
                                inc=(st == 1))
                        tad = p.emit("vector", lambda e, dt=dt, tc=tc, bi=bi: e.tensor_add(
                            x_fm[:, dt * HALF + tc * 512: dt * HALF + tc * 512 + 512],
                            x_fm[:, dt * HALF + tc * 512: dt * HALF + tc * 512 + 512],
                            ps[bi][:, 0:512]),
                            deps=[mm3, xfm_guard.get((dt, tc))], inc=True)
                        psum_guard[bi] = tad
                        xfm_guard[(dt, tc)] = tad
                        sc_last = tad
                gprev[0] = sc_last
                xg_guard[0] = sc_last
                h_guard[0] = dmm[7]
                ds_guard[0] = sc_last
            p.barrier()

            for gts_ in range(8):
                tts = None
                for dt in range(DT):
                    bank = 4 + (dt // 4) % 2
                    tts = p.emit("tensor", lambda e, dt=dt, gts_=gts_,
                                 bank=bank: e.transpose(
                        ps[bank][:, (dt % 4) * 128:(dt % 4) * 128 + 128],
                        x_fm[:, dt * HALF + gts_ * 128:
                             dt * HALF + gts_ * 128 + 128],
                        identt[:, :]),
                        deps=([psum_guard[bank]] if dt % 4 == 0 else []),
                        inc=(dt % 4 == 3))
                    if dt % 4 == 3:
                        tsum = p.emit("vector", lambda e, dt=dt, bank=bank: e.tensor_copy(
                            un[:, S_XG + (dt - 3) * 128:S_XG + (dt + 1) * 128],
                            ps[bank][:, 0:512]),
                            deps=[tts, prev_rt[0]], inc=True)
                        psum_guard[bank] = tsum
                mcol = un[:, S_RT + 420:S_RT + 421]
                sscol = un[:, S_RT + 421:S_RT + 422]
                sqcol = un[:, S_RT + 422:S_RT + 423]
                t1 = p.emit("vector", lambda e: e.reduce_sum(
                    mcol, un[:, S_XG:S_XG + 2048], AX.X),
                    deps=[tsum, prev_rt[0]], inc=True)
                t2 = p.emit("vector", lambda e: e.tensor_scalar_mul(
                    mcol, mcol, 1.0 / D), deps=[t1], inc=True)
                tsq = p.emit("scalar", lambda e: e.activation(
                    out=un[:, S_H:S_H + 2048], in_=un[:, S_XG:S_XG + 2048],
                    func=AF.Square, accum_out=sscol),
                    deps=[t2], inc=True)
                t4 = p.emit("vector", lambda e: e.tensor_scalar_mul(
                    sscol, sscol, 1.0 / D), deps=[tsq], inc=True)
                t5 = p.emit("vector", lambda e: e.tensor_mul(
                    sqcol, mcol, mcol), deps=[t2, t4], inc=True)
                t6 = p.emit("vector", lambda e: e.tensor_sub(
                    sscol, sscol, sqcol), deps=[t5], inc=True)
                t7 = p.emit("scalar", lambda e: e.activation(
                    out=sscol, in_=sscol, func=AF.Sqrt, bias=eps_c),
                    deps=[t6, t_eps], inc=True)
                t8 = p.emit("vector", lambda e: e.reciprocal(
                    sscol, sscol), deps=[t7], inc=True)
                t9 = p.emit("vector", lambda e: e.tensor_scalar(
                    un[:, S_XG:S_XG + 2048], un[:, S_XG:S_XG + 2048],
                    mcol, sscol, ALU.subtract, ALU.mult),
                    deps=[t8], inc=True)
                td_ = p.emit("gpsimd", lambda e, gts_=gts_: e.dma_start(
                    out=hbm["out"][gts_ * 128:(gts_ + 1) * 128, :],
                    in_=un[:, S_XG:S_XG + 2048]), deps=[t9], dma=True)
                out_dma.append(td_)
                prev_rt[0] = td_
            return out_dma

        for col in range(DT * NST):
            hcar_dep[col] = t_hz

        def load_x(src, c0):
            return p.emit("sync", lambda e, c0=c0, src=src: e.dma_start(
                out=x_fm[:, :].rearrange("p (t c) -> p t c", t=DT)[:, :, c0:c0 + 512],
                in_=hbm[src][:, :].rearrange("(t p) l -> p t l", p=128)[:, :, c0:c0 + 512]),
                dma=True)

        out_toks = []

        def dbg_dump_fm(ap_fn, toks):
            for dt in range(DT):
                out_toks.append(p.emit("gpsimd", lambda e, dt=dt: e.dma_start(
                    out=hbm["dbg"][:, :].rearrange("(t p) l -> p t l", p=128)[:, dt, :],
                    in_=ap_fn(dt)), deps=[toks[dt]], dma=True))

        def dbg_dump_hilo(toks_h):
            for half in range(2):
                c0 = half * 512
                for dt in range(DT):
                    tt = p.emit("vector", lambda e, dt=dt, c0=c0: e.tensor_add(
                        lnscr[:, 0:512], x1hi(dt, c0), x1lo(dt, c0)),
                        deps=[toks_h[half][dt]] + out_toks[-1:], inc=True)
                    out_toks.append(p.emit("gpsimd", lambda e, dt=dt, c0=c0: e.dma_start(
                        out=hbm["dbg"][:, :].rearrange("(t p) l -> p t l", p=128)[:, dt, c0:c0 + 512],
                        in_=lnscr[:, 0:512]), deps=[tt], dma=True))

        stop = False
        for phase, src in enumerate(("xpre", "xown")):
            want = "" if phase == 1 else "_pre"
            w_l0a = load_l0a_weights()
            x_toks_h = []
            for half in range(2):
                c0 = half * 512
                xt = load_x(src, c0)
                y_toks = l0a_half(c0, [xt], w_l0a)
                toks = ln_fm(512,
                             (lambda dt, c0=c0: x_fm[:, dt * HALF + c0: dt * HALF + c0 + 512]),
                             y_toks,
                             dsthi=(lambda dt, c0=c0: x1hi(dt, c0)),
                             dstlo=(lambda dt, c0=c0: x1lo(dt, c0)),
                             dst32=None)
                x_toks_h.append(toks)
                p.barrier()
            if upto == "l0a" + want:
                dbg_dump_hilo(x_toks_h)
                stop = True
                break

            for half in range(2):
                c0 = half * 512
                toks = l0ffn_chunk(c0, x_toks_h[half])
                x_toks_h[half] = ln_fm(
                    512,
                    (lambda dt, c0=c0: x_fm[:, dt * HALF + c0: dt * HALF + c0 + 512]),
                    toks, dsthi=None, dstlo=None,
                    dst32=(lambda dt, c0=c0: x_fm[:, dt * HALF + c0: dt * HALF + c0 + 512]))
                p.barrier()
            if upto == "l0ffn" + want:
                dbg_dump_fm(lambda dt: x_fm[:, dt * HALF: dt * HALF + HALF],
                            [x_toks_h[1][dt] for dt in range(DT)])
                stop = True
                break
            p.barrier()

            if phase == 0:
                t_s0 = p.emit("vector", lambda e: e.memset(
                    un[:, A_S:A_S + 4096], 0.0), inc=True)
                s_dep = [t_s0]
            else:
                t_sr = p.emit("sync", lambda e: e.dma_start(
                    out=un[:, A_S:A_S + 4096], in_=hbm["sspill"][:, :]),
                    dma=True)
                s_dep = [t_sr]
            nq = 1 if ((upto.startswith("l1a_pre1") or upto == "l1a_sdump")
                       and phase == 0) else 4
            nch = 4
            if upto.startswith("l1a_pre1c"):
                nch = int(upto[len("l1a_pre1c"):])
            for q in range(nq):
                res = l1a_quarter(phase, q, x_toks_h[q // 2], s_dep, nch)
                p.barrier()
                if phase == 0:
                    s_dep = res
                else:
                    x_toks_h[q // 2] = res
            if phase == 0:
                t_sp = p.emit("gpsimd", lambda e: e.dma_start(
                    out=hbm["sspill"][:, :], in_=un[:, A_S:A_S + 4096]),
                    deps=s_dep, dma=True)
                p.emit("gpsimd", lambda e: e.nop(), deps=[t_sp])
                p.barrier()
                if upto == "l1a_sdump":
                    stop = True
                    break
                if upto.startswith("l1a_pre"):
                    for j in range(4):
                        out_toks.append(p.emit("gpsimd", lambda e, j=j: e.dma_start(
                            out=hbm["dbg"][:, :].rearrange(
                                "(t p) l -> p t l", p=128)[:, j, :],
                            in_=un[:, A_S + j * 1024:A_S + (j + 1) * 1024]),
                            deps=s_dep, dma=True))
                    stop = True
                    break
            else:
                if upto == "l1a":
                    dbg_dump_fm(lambda dt: x_fm[:, dt * HALF: dt * HALF + HALF],
                                [x_toks_h[1][dt] for dt in range(DT)])
                    stop = True
                    break
                out_toks.extend(moe_phase())
            p.barrier()

        if out_toks:
            p.emit("gpsimd", lambda e: e.nop(), deps=list(out_toks))

        with nc.Block() as block:
            @block.sync
            def _(eng):
                _replay_one(p, "sync", eng, sems)
            @block.tensor
            def _(eng):
                _replay_one(p, "tensor", eng, sems)
            @block.vector
            def _(eng):
                _replay_one(p, "vector", eng, sems)
            @block.scalar
            def _(eng):
                _replay_one(p, "scalar", eng, sems)
            @block.gpsimd
            def _(eng):
                _replay_one(p, "gpsimd", eng, sems)

    return nc



def _u8(wq):
    a = np.ascontiguousarray(np.asarray(wq, np.int32))
    return a.view(np.uint8).reshape(a.shape + (4,))[..., 0]


def _prep_inputs(inp):
    f32 = np.float32
    x = np.ascontiguousarray(np.asarray(inp["L0_attn_x"], f32))
    def tile_col(v):
        return np.ascontiguousarray(v.reshape(DT, 128).T)
    w = {}
    w["wx_t"] = np.ascontiguousarray(np.asarray(inp["L0_attn_W_x"], f32).T)
    w["wdt_t"] = np.ascontiguousarray(np.asarray(inp["L0_attn_W_dt"], f32).T)
    w["bdt"] = tile_col(np.asarray(inp["L0_attn_b_dt"], f32))
    a0 = np.asarray(inp["L0_attn_A_log"], f32)
    w["a0"] = np.ascontiguousarray(a0.reshape(DT, 128, NST).transpose(1, 0, 2).reshape(128, DT * NST))
    w["d0"] = tile_col(np.asarray(inp["L0_attn_D"], f32))
    w["wqg_t"] = np.ascontiguousarray(_u8(inp["L0_ffn_wq_gate"]).T)
    w["wqu_t"] = np.ascontiguousarray(_u8(inp["L0_ffn_wq_up"]).T)
    w["wqd_t"] = np.ascontiguousarray(_u8(inp["L0_ffn_wq_down"]).T)
    w["sg"] = np.ascontiguousarray(np.asarray(inp["L0_ffn_s_gate"], f32).reshape(HT, 128).T)
    w["su"] = np.ascontiguousarray(np.asarray(inp["L0_ffn_s_up"], f32).reshape(HT, 128).T)
    w["sd"] = tile_col(np.asarray(inp["L0_ffn_s_down"], f32))
    w["nw"] = tile_col(np.asarray(inp["L1_attn_norm_w"], f32))
    w["win"] = np.ascontiguousarray(np.asarray(inp["L1_attn_w_in"], f32))
    cw = np.asarray(inp["L1_attn_conv_w"], f32)
    w["convw"] = np.ascontiguousarray(cw.reshape(NCT, 128, KCONV).transpose(1, 0, 2).reshape(128, NCT * KCONV))
    w["convb"] = np.ascontiguousarray(np.asarray(inp["L1_attn_conv_b"], f32).reshape(NCT, 128).T)
    pad = np.zeros((128, 1), f32); pad[:HEADS, 0] = np.asarray(inp["L1_attn_dt_bias"], f32)
    w["dtb"] = pad
    pad = np.zeros((128, 1), f32); pad[:HEADS, 0] = np.asarray(inp["L1_attn_A_log"], f32)
    w["a1"] = pad
    w["dskipbc"] = np.ascontiguousarray(np.broadcast_to(
        np.asarray(inp["L1_attn_D_skip"], f32)[None, :], (128, HEADS)))
    w["gnw"] = tile_col(np.asarray(inp["L1_attn_gnorm_w"], f32))
    w["wout"] = np.ascontiguousarray(np.asarray(inp["L1_attn_w_out"], f32))
    w["wo"] = np.ascontiguousarray(np.asarray(inp["L1_attn_wo_w"], f32))
    w["router"] = np.ascontiguousarray(np.asarray(inp["L1_ffn_router_w"], f32))
    w["gw"] = np.ascontiguousarray(np.asarray(inp["L1_ffn_gate_w"], f32))
    w["uw"] = np.ascontiguousarray(np.asarray(inp["L1_ffn_up_w"], f32))
    w["dw"] = np.ascontiguousarray(np.asarray(inp["L1_ffn_down_w"], f32))
    w["ident"] = np.eye(128, dtype=f32)
    w["onesr"] = np.ones((1, 128), f32)
    w["negr"] = -np.ones((1, 128), f32)
    w["onesc"] = np.ones((128, 1), f32)
    w["maskt"] = np.ascontiguousarray(np.triu(np.ones((64, 64), f32)))
    w["ones32"] = np.ones((32, 128), f32)
    w["i32"] = np.eye(32, dtype=f32)
    w["iotac_h"] = np.ascontiguousarray(
        np.stack([np.arange(1, 129, dtype=f32),
                  np.arange(129, 257, dtype=f32)], 1))
    w["iotabc_h"] = np.ascontiguousarray(np.broadcast_to(
        np.arange(1, 257, dtype=f32)[None, :], (128, 256)))

    zeros_half = np.zeros((D, HALF), f32)
    in_maps = []
    for c in range(8):
        b = c // 2
        odd = c % 2
        m = dict(w)
        if odd:
            m["xpre"] = np.ascontiguousarray(x[b, 0:HALF, :].T)
            m["xown"] = np.ascontiguousarray(x[b, HALF:L, :].T)
        else:
            m["xpre"] = zeros_half
            m["xown"] = np.ascontiguousarray(x[b, 0:HALF, :].T)
        in_maps.append(m)
    return in_maps


def kernel(**inputs):
    in_maps = _prep_inputs(inputs)
    nc = build_program()
    res = run_bass_kernel_spmd(nc, in_maps, core_ids=list(range(8)))
    out = np.empty((B, L, D), np.float32)
    for c in range(8):
        b, odd = c // 2, c % 2
        out[b, odd * HALF:(odd + 1) * HALF, :] = res.results[c]["out"]
    return out
